# version 09
# baseline (speedup 1.0000x reference)
import numpy as np
import concourse.bass as bass
import concourse.mybir as mybir
from concourse.bass_utils import run_bass_kernel_spmd
from concourse.alu_op_type import AluOpType

F32 = mybir.dt.float32
BF16 = mybir.dt.bfloat16
AF = mybir.ActivationFunctionType
AX = mybir.AxisListType.X

B, L, D, S, C = 2, 2048, 2048, 16, 32
KT, NH = 16, 2
LH = L // NH
NKH = LH // C
TT = 512
NEG = -1.0e30
EPS = 1e-5


def bitrev_perm(n):
    idx = np.arange(n)
    c, pos = idx // 32, idx % 32
    r = np.zeros_like(pos)
    for b in range(5):
        r |= ((pos >> b) & 1) << (4 - b)
    return c * 32 + r


class Sched:
    ENGS = ("tensor", "vector", "scalar", "sync", "gpsimd")

    def __init__(self):
        self.events = {e: [] for e in self.ENGS}
        self.cum = {e: 0 for e in self.ENGS}

    def ev(self, eng, fn, deps=(), ndma=1):
        n = len(self.events[eng])
        dd = [d for d in deps
              if d is not None and (d[0] != eng or eng in ("sync", "gpsimd"))]
        if eng in ("sync", "gpsimd"):
            self.cum[eng] += 16 * ndma
        else:
            self.cum[eng] += 1
        self.events[eng].append((fn, dd, self.cum[eng], ndma))
        return (eng, n)

    def emit(self, block, sems):
        after = {e: [ev[2] for ev in self.events[e]] for e in self.ENGS}
        for eng in self.ENGS:
            evs = self.events[eng]
            is_dma = eng in ("sync", "gpsimd")

            def make(eng=eng, evs=evs, is_dma=is_dma):
                def body(h):
                    sem = sems[eng]
                    cnt = [0]
                    if is_dma:
                        orig = h.dma_start

                        def counted(*a, **k):
                            cnt[0] += 1
                            return orig(*a, **k)
                        h.dma_start = counted
                    for i, (fn, dd, _v, ndma) in enumerate(evs):
                        best = {}
                        for (deng, didx) in dd:
                            v = after[deng][didx]
                            if best.get(deng, -1) < v:
                                best[deng] = v
                        for deng, v in best.items():
                            h.wait_ge(sems[deng], v)
                        cnt[0] = 0
                        r = fn(h)
                        if is_dma:
                            assert cnt[0] == ndma, (
                                f"{eng} event {i}: declared ndma={ndma} actual={cnt[0]}")
                        else:
                            h.drain().then_inc(sem, 1)
                return body
            getattr(block, eng)(make())


def build_nc(dbg=False, b_run=B):
    nc = bass.Bass("TRN2", target_bir_lowering=False)

    def din(name, shp, dt=F32):
        return nc.dram_tensor(name, shp, dt, kind="ExternalInput").ap()

    def dint(name, shp, dt=F32):
        return nc.dram_tensor(name, shp, dt, kind="Internal").ap()

    dkind = "ExternalOutput" if dbg else "Internal"
    xin = din("xin", [B, KT, 128, L])
    win = din("win", [KT, 128, D])
    wffn = din("wffn", [KT, 128, D])
    wout = din("wout", [KT, 128, D])
    amoe = din("amoe", [KT, 128, 256])
    bmoe = din("bmoe", [2, 128, D])
    rw1 = din("rw1", [KT, 128, 64])
    rw2 = din("rw2", [64, 32])
    bwd = din("bw", [KT, 128, 16])
    cwd = din("cw", [KT, 128, 16])
    aA = din("aA", [KT, 128, 16])
    dpar = din("dpar", [128, KT])
    binp = din("binp", [128, KT])
    bcmb = din("bcmb", [2, 128, KT])
    rb1v = din("rb1v", [64, 1])
    rb2rep = din("rb2rep", [128, 32])
    gaterep = din("gaterep", [128, 16])
    identb = din("identb", [128, 128], BF16)
    maskt = din("maskt", [4, 128, 32], BF16)
    repmoe = din("repmoe", [16, 256], BF16)
    repd = din("repd", [16, 128], BF16)

    winb = nc.dram_tensor("winb", [KT, 128, D], BF16, kind=dkind).ap()
    wffnb = nc.dram_tensor("wffnb", [KT, 128, D], BF16, kind=dkind).ap()
    woutdb = nc.dram_tensor("woutdb", [KT, 128, D], BF16, kind=dkind).ap()
    wcwescr = dint("wcwescr", [128, D], BF16)
    pwfscr = dint("pwfscr", [128, KT * S * 32], BF16)
    pwiscr = dint("pwiscr", [128, KT * S * 32], BF16)
    dscrB = nc.dram_tensor("dscrB", [B, 16, L], BF16, kind=dkind).ap()
    fscr = nc.dram_tensor("fscr", [B, 64, 512], BF16, kind=dkind).ap()
    part = nc.dram_tensor("part", [B, KT, 128, L], F32, kind=dkind).ap()
    yout = nc.dram_tensor("yout", [B, KT, 128, L], F32, kind="ExternalOutput").ap()

    sch = Sched()
    E = sch.ev

    import contextlib
    st_ = contextlib.ExitStack()
    with st_:
        XT = st_.enter_context(nc.sbuf_tensor([128, KT * LH], BF16))
        XPB = st_.enter_context(nc.sbuf_tensor([128, KT * LH], BF16))
        PWY = st_.enter_context(nc.sbuf_tensor([128, KT * S * 32], BF16))
        POOL = st_.enter_context(nc.sbuf_tensor([128, 16384], BF16))
        PWF = POOL[:, 0:8192]
        PWI = POOL[:, 8192:16384]
        WQ1 = POOL[:, 0:8192]
        WQ2 = POOL[:, 8192:12288]
        AMB = st_.enter_context(nc.sbuf_tensor([128, KT * 256], BF16))
        BMB = st_.enter_context(nc.sbuf_tensor([128, 2 * D], BF16))
        RW1B = st_.enter_context(nc.sbuf_tensor([128, KT * 64], BF16))
        RW2T = st_.enter_context(nc.sbuf_tensor([128, 32], BF16))
        RW2B = RW2T[64:128, 0:32]
        BWB = st_.enter_context(nc.sbuf_tensor([128, KT * 16], BF16))
        CWB = st_.enter_context(nc.sbuf_tensor([128, KT * 16], BF16))
        XFT = st_.enter_context(nc.sbuf_tensor([128, 2048], F32))
        XB2 = st_.enter_context(nc.sbuf_tensor([128, 2048], BF16))
        XXB = st_.enter_context(nc.sbuf_tensor([128, 2048], BF16))
        ISR = st_.enter_context(nc.sbuf_tensor([128, 128], F32))
        MISR = st_.enter_context(nc.sbuf_tensor([128, 128], F32))
        ZS = st_.enter_context(nc.sbuf_tensor([128, 4 * L], BF16))
        ZACC = st_.enter_context(nc.sbuf_tensor([128, L], BF16))
        ZACCB = st_.enter_context(nc.sbuf_tensor([128, L], BF16))
        DEXP = st_.enter_context(nc.sbuf_tensor([128, L], BF16))
        DREP = st_.enter_context(nc.sbuf_tensor([128, LH], BF16))
        ROW1 = st_.enter_context(nc.sbuf_tensor([128, L], BF16))
        DSS = ROW1[0:16, :]
        DSB = POOL[0:16, 12288:12288 + L]
        H1B = ROW1[64:128, :]
        ROW2 = st_.enter_context(nc.sbuf_tensor([128, 1536], F32))
        DTMP = ROW2[0:16, 0:512]
        H1F = ROW2[64:128, 0:1024]
        XAB = st_.enter_context(nc.sbuf_tensor([128, 2 * TT], BF16))
        CTT = st_.enter_context(nc.sbuf_tensor([128, 64], BF16))
        GEMB = st_.enter_context(nc.sbuf_tensor([128, 64], F32))
        CFF = ROW1[32:48, :]
        CREP = st_.enter_context(nc.sbuf_tensor([128, 2 * TT], BF16))
        TOP5 = st_.enter_context(nc.sbuf_tensor([128, 264], F32))
        CONSTS = st_.enter_context(nc.sbuf_tensor([128, 4], F32))
        DBGT = st_.enter_context(nc.sbuf_tensor([128, 48], F32))
        MROW = ROW2[0:1, 512:1024]
        SROW = ROW2[0:1, 1024:1280]
        MSEQ = XB2[:, 0:512]
        MS2 = XB2[:, 512:1024]
        ACZ = XB2[:, 1024:1536]
        CARRY = st_.enter_context(nc.sbuf_tensor([128, KT * 16], BF16))
        FST = st_.enter_context(nc.sbuf_tensor([64, 512], BF16))
        GFS = POOL[0:16, 4096:4096 + 63 * 32]
        SIGA = XFT[:, 0:256]
        PST = XFT[:, 256:512]
        PST2 = XFT[:, 512:768]
        IDB = st_.enter_context(nc.sbuf_tensor([128, 128], BF16))
        MSK = st_.enter_context(nc.sbuf_tensor([128, 4 * 32], BF16))
        SMALL = st_.enter_context(nc.sbuf_tensor([128, 256], BF16))
        RPM = SMALL[32:48, 0:256]
        RPD = SMALL[0:16, 0:128]
        BINS = st_.enter_context(nc.sbuf_tensor([128, KT], F32))
        BCS = st_.enter_context(nc.sbuf_tensor([128, KT], F32))
        BC2 = st_.enter_context(nc.sbuf_tensor([128, KT], F32))
        DPS = st_.enter_context(nc.sbuf_tensor([128, KT], F32))
        RB1S = st_.enter_context(nc.sbuf_tensor([64, 1], F32))
        B2G = st_.enter_context(nc.sbuf_tensor([128, 32], F32))
        GATS = st_.enter_context(nc.sbuf_tensor([128, 16], F32))
        NEGI = st_.enter_context(nc.sbuf_tensor([128, 16], F32))
        ZERI = st_.enter_context(nc.sbuf_tensor([128, 16], F32))
        ONER = st_.enter_context(nc.sbuf_tensor([1, 128], F32))
        ONEC = st_.enter_context(nc.sbuf_tensor([128, 1], BF16))
        WCWE = POOL[:, 6144:6144 + D]
        WCWS = POOL[0:16, 2048:2048 + D]
        WTMP = POOL[:, 0:D]
        TCO = st_.enter_context(nc.sbuf_tensor([128, 2 * TT], F32))
        s_te = st_.enter_context(nc.semaphore("s_te"))
        s_ve = st_.enter_context(nc.semaphore("s_ve"))
        s_se = st_.enter_context(nc.semaphore("s_se"))
        s_sy = st_.enter_context(nc.semaphore("s_sy"))
        s_gp = st_.enter_context(nc.semaphore("s_gp"))
        s_sh = st_.enter_context(nc.semaphore("s_sh"))
        PB0 = st_.enter_context(nc.psum_tensor([128, 512], F32))
        PB1 = st_.enter_context(nc.psum_tensor([128, 512], F32))
        PB2 = st_.enter_context(nc.psum_tensor([128, 512], F32))
        PB3 = st_.enter_context(nc.psum_tensor([128, 512], F32))
        PB4 = st_.enter_context(nc.psum_tensor([128, 512], F32))
        PB5 = st_.enter_context(nc.psum_tensor([128, 512], F32))
        PB6 = st_.enter_context(nc.psum_tensor([128, 512], F32))
        PB7 = st_.enter_context(nc.psum_tensor([128, 1024], BF16))
        block = st_.enter_context(nc.Block())
        PB = [PB0, PB1, PB2, PB3, PB4, PB5, PB6, PB7]
        sems = dict(tensor=s_te, vector=s_ve, scalar=s_se, sync=s_sy, gpsimd=s_gp)

        def xtv(t=LH):
            return XT[:].rearrange("p (k t) -> p k t", t=t)

        def xpv(t=LH):
            return XPB[:].rearrange("p (k t) -> p k t", t=t)

        def f_lds(h):
            h.dma_start(out=SIGA.rearrange("p (k s) -> p k s", s=16),
                        in_=aA.transpose([1, 0, 2])).then_inc(s_sy, 16)
            h.dma_start(out=IDB[:], in_=identb[:]).then_inc(s_sy, 16)
            h.dma_start(out=MSK[:].rearrange("p (t j) -> p t j", j=32),
                        in_=maskt.transpose([1, 0, 2])).then_inc(s_sy, 16)
            h.dma_start(out=BINS[:], in_=binp[:]).then_inc(s_sy, 16)
            h.dma_start(out=BCS[:], in_=bcmb[0]).then_inc(s_sy, 16)
            h.dma_start(out=BC2[:], in_=bcmb[1]).then_inc(s_sy, 16)
            h.dma_start(out=DPS[:], in_=dpar[:]).then_inc(s_sy, 16)
            h.dma_start(out=RB1S[:], in_=rb1v[:]).then_inc(s_sy, 16)
            h.dma_start(out=B2G[:], in_=rb2rep[:]).then_inc(s_sy, 16)
            h.dma_start(out=GATS[:], in_=gaterep[:]).then_inc(s_sy, 16)
            return None
        e_lds = E("sync", f_lds, ndma=10)

        def f_ldc(h):
            h.dma_start(out=AMB[:].rearrange("p (k e) -> p k e", e=256), in_=amoe.transpose([1, 0, 2])).then_inc(s_gp, 16)
            h.dma_start(out=BMB[:].rearrange("p (q o) -> p q o", o=D), in_=bmoe.transpose([1, 0, 2])).then_inc(s_gp, 16)
            h.dma_start(out=RW1B[:].rearrange("p (k e) -> p k e", e=64), in_=rw1.transpose([1, 0, 2])).then_inc(s_gp, 16)
            h.dma_start(out=RW2B, in_=rw2[:]).then_inc(s_gp, 16)
            h.dma_start(out=BWB[:].rearrange("p (k s) -> p k s", s=16), in_=bwd.transpose([1, 0, 2])).then_inc(s_gp, 16)
            h.dma_start(out=CWB[:].rearrange("p (k s) -> p k s", s=16), in_=cwd.transpose([1, 0, 2])).then_inc(s_gp, 16)
            h.dma_start(out=RPM, in_=repmoe[:]).then_inc(s_gp, 16)
            h.dma_start(out=RPD, in_=repd[:]).then_inc(s_gp, 16)
            return None
        e_ldc = E("gpsimd", f_ldc, ndma=8)

        def f_const(h):
            h.memset(NEGI[:], NEG)
            h.memset(CONSTS[:, 0:1], EPS)
            h.memset(CONSTS[:, 1:2], 1.0)
            h.memset(CONSTS[:, 2:3], NEG)
            h.memset(ZERI[:], 0.0)
            h.memset(ONER[:], 1.0)
            h.memset(ONEC[:], 1.0)
            h.tensor_tensor(out=BCS[:], in0=BCS[:], in1=BC2[:], op=AluOpType.add)
            return h.tensor_tensor(out=B2G[:, 0:16], in0=B2G[:, 0:16], in1=GATS[:],
                                   op=AluOpType.add)
        e_const = E("vector", f_const, deps=[e_lds, e_ldc])

        def f_siga(h):
            return h.activation(out=SIGA, in_=SIGA, func=AF.Sigmoid)
        e_siga = E("scalar", f_siga, deps=[e_lds])

        brv5 = bitrev_perm(32)[:32]

        def f_pow(h):
            h.memset(PST, 1.0)
            h.drain()
            last = None
            for p in range(64):
                if p < 32:
                    off = (p // 8) * 128 + (p % 8) * 16
                    ov = PWY[:].rearrange("q (k c s) -> q k c s", c=32, s=16)[:, :, off // 16]
                else:
                    ov = PWF.rearrange("q (k s r) -> q k s r", s=16, r=32)[:, :, :, p - 32]
                last = h.tensor_copy(out=ov, in_=PST.rearrange("q (k s) -> q k s", s=16))
                if p < 63:
                    last = h.tensor_tensor(out=PST, in0=PST, in1=SIGA, op=AluOpType.mult)
                    h.drain()
            h.reciprocal(out=PST2, in_=SIGA)
            h.memset(PST, 1.0)
            h.drain()
            for i in range(32):
                slot = int(np.where(brv5 == i)[0][0])
                ov = PWI.rearrange("q (k s r) -> q k s r", s=16, r=32)[:, :, :, slot]
                last = h.tensor_copy(out=ov, in_=PST.rearrange("q (k s) -> q k s", s=16))
                if i < 31:
                    last = h.tensor_tensor(out=PST, in0=PST, in1=PST2, op=AluOpType.mult)
                    h.drain()
            return last
        e_pow = E("vector", f_pow, deps=[e_siga])

        def f_powst(h):
            h.dma_start(out=pwfscr[:], in_=PWF).then_inc(s_sy, 16)
            h.dma_start(out=pwiscr[:], in_=PWI).then_inc(s_sy, 16)
            return None
        e_powst = E("sync", f_powst, deps=[e_pow], ndma=2)

        ew_prev = None
        e_wcw_mm = None
        for kt in range(KT):
            def f_wl(h, kt=kt):
                h.dma_start(out=WTMP, in_=wout[kt]).then_inc(s_gp, 16)
                return None
            ew_l = E("gpsimd", f_wl, deps=[ew_prev, e_powst])

            def f_wcw(h, kt=kt):
                last = None
                cw_sl = CWB[:].rearrange("p (k s) -> p k s", s=16)[:, kt]
                for nt in range(4):
                    last = h.matmul(PB[nt][0:16, :], cw_sl, WTMP[:, 512 * nt:512 * nt + 512],
                                    start=(kt == 0), stop=(kt == KT - 1))
                return last
            e_wcw_mm = E("tensor", f_wcw, deps=[ew_l, e_ldc])

            def f_wd(h, kt=kt):
                return h.activation(out=WTMP, in_=WTMP, func=AF.Identity,
                                    scale=DPS[:, kt:kt + 1])
            e_wd = E("scalar", f_wd, deps=[e_wcw_mm, e_lds])

            def f_wds(h, kt=kt):
                h.dma_start(out=woutdb[kt], in_=WTMP).then_inc(s_gp, 16)
                return None
            ew_prev = E("gpsimd", f_wds, deps=[e_wd])

        for kt in range(KT):
            def f_wl2(h, kt=kt):
                h.dma_start(out=WTMP, in_=wffn[kt]).then_inc(s_gp, 16)
                return None
            ew_l2 = E("gpsimd", f_wl2, deps=[ew_prev])

            def f_wadd(h, kt=kt):
                return h.tensor_tensor(out=WTMP[:, 128 * kt:128 * kt + 128],
                                       in0=WTMP[:, 128 * kt:128 * kt + 128],
                                       in1=IDB[:], op=AluOpType.add)
            e_wa = E("vector", f_wadd, deps=[ew_l2, e_lds])

            def f_ws2(h, kt=kt):
                h.dma_start(out=wffnb[kt], in_=WTMP).then_inc(s_gp, 16)
                return None
            ew_prev = E("gpsimd", f_ws2, deps=[e_wa])

            def f_wl3(h, kt=kt):
                h.dma_start(out=WTMP, in_=win[kt]).then_inc(s_gp, 16)
                return None
            ew_l3 = E("gpsimd", f_wl3, deps=[ew_prev])

            def f_ws3(h, kt=kt):
                h.dma_start(out=winb[kt], in_=WTMP).then_inc(s_gp, 16)
                return None
            ew_prev = E("gpsimd", f_ws3, deps=[ew_l3])
        e_wprep_done = ew_prev

        def f_wcwev(h):
            last = None
            for nt in range(4):
                last = h.activation(out=WCWS[:, 512 * nt:512 * nt + 512],
                                    in_=PB[nt][0:16, :], func=AF.Copy)
            return last
        e_wcwev = E("scalar", f_wcwev, deps=[e_wcw_mm])

        def f_wcwx(h):
            last = None
            for nt in range(4):
                last = h.matmul(PB7.bitcast(F32)[:, 0:512] if False else PB[4][:, :],
                                RPD[:, :], WCWS[:, 512 * nt:512 * nt + 512],
                                start=True, stop=True)
                last = h.matmul(PB[4][:, :], RPD[:, :], WCWS[:, 512 * nt:512 * nt + 512],
                                start=True, stop=True) if False else last
            return last
        e_wcwx_list = []
        for nt in range(4):
            def f_x1(h, nt=nt):
                return h.matmul(PB[4][:, :], RPD[:, :], WCWS[:, 512 * nt:512 * nt + 512],
                                start=True, stop=True)
            e_x1 = E("tensor", f_x1, deps=[e_wcwev, e_ldc] + ([e_wcwx_list[-1]] if e_wcwx_list else []))

            def f_x2(h, nt=nt):
                return h.activation(out=WCWE[:, 512 * nt:512 * nt + 512], in_=PB[4][:, :],
                                    func=AF.Copy)
            e_x2 = E("scalar", f_x2, deps=[e_x1])
            e_wcwx_list.append(e_x2)
        def f_wcwst(h):
            h.dma_start(out=wcwescr[:], in_=WCWE).then_inc(s_sy, 16)
            return None
        e_wcwst = E("sync", f_wcwst, deps=[e_wcwx_list[-1]])
        e_wcwe_done = e_wcwst

        bank_free = {i: e_wcwe_done for i in range(8)}
        tco_free = {i: None for i in range(2)}

        part_evt = {}
        DBG_TI = [0]
        SHCUM = [0]
        pool_guard = []
        e_shift_done = {}
        e_zbrv_done = {}
        e_cmm_last = {}
        e_zs_written = {}
        e_dst_last = {}
        e_fstore = {}
        last_scan_evt = None
        prev_xn_evt = None
        prev_st = None
        prev_xx = None
        omm_last = None
        cst_last = None
        h1_reader_last = None

        for b in range(b_run):
            e_zs_written[b] = []
            for g in range(NH):
                t0g = g * LH
                for tt in range(LH // TT):
                    t0 = t0g + tt * TT
                    e_xn_list = []
                    prev_repev = None
                    for st in range(4):
                        t0s = t0 + st * 128
                        slot = 0

                        def f_xl(h, b=b, t0s=t0s, slot=slot):
                            h.dma_start(out=XFT[:, 2048 * slot:2048 * slot + 2048]
                                        .rearrange("p (k t) -> p k t", t=128),
                                        in_=xin[b, :, :, t0s:t0s + 128].transpose([1, 0, 2])).then_inc(s_sy, 16)
                            return None
                        e_xl = E("sync", f_xl, deps=[prev_xn_evt, last_scan_evt, e_powst, prev_xx])

                        def f_xbl(h, b=b, t0s=t0s):
                            h.dma_start(out=XB2[:].rearrange("p (k t) -> p k t", t=128),
                                        in_=xin[b, :, :, t0s:t0s + 128].transpose([1, 0, 2])).then_inc(s_gp, 16)
                            return None
                        e_xbl = E("gpsimd", f_xbl, deps=[prev_st, last_scan_evt] + pool_guard)

                        def f_xx(h, slot=slot):
                            return h.activation(out=XXB[:], in_=XFT[:, 2048 * slot:2048 * slot + 2048],
                                                func=AF.Square)
                        e_xx = E("scalar", f_xx, deps=[e_xl, prev_st])
                        prev_xx = e_xx

                        def f_st(h, st=st):
                            last = None
                            for kt in range(KT):
                                last = h.matmul(PB[6][0:1, 0:128], ONEC[:, :],
                                                XB2[:, 128 * kt:128 * kt + 128],
                                                start=(kt == 0), stop=(kt == KT - 1))
                            for kt in range(KT):
                                last = h.matmul(PB[6][32:33, 0:128], ONEC[:, :],
                                                XXB[:, 128 * kt:128 * kt + 128],
                                                start=(kt == 0), stop=(kt == KT - 1))
                            return last
                        e_st = E("tensor", f_st, deps=[e_xbl, e_xx, e_const, bank_free[6]])
                        prev_st = e_st

                        def f_ms(h):
                            h.tensor_scalar(out=MROW[0:1, 0:128], in0=PB[6][0:1, 0:128],
                                            scalar1=1.0 / D, scalar2=None, op0=AluOpType.mult)
                            h.tensor_scalar(out=MROW[0:1, 128:256], in0=PB[6][32:33, 0:128],
                                            scalar1=1.0 / D, scalar2=None, op0=AluOpType.mult)
                            h.drain()
                            h.tensor_tensor(out=MROW[0:1, 256:384], in0=MROW[0:1, 0:128],
                                            in1=MROW[0:1, 0:128], op=AluOpType.mult)
                            h.drain()
                            return h.tensor_tensor(out=MROW[0:1, 128:256], in0=MROW[0:1, 128:256],
                                                   in1=MROW[0:1, 256:384], op=AluOpType.subtract)
                        e_ms = E("vector", f_ms, deps=[e_st])
                        bank_free[6] = e_ms

                        def f_is(h):
                            h.activation(out=MROW[0:1, 384:512], in_=MROW[0:1, 128:256],
                                         func=AF.Ln, bias=CONSTS[0:1, 0:1])
                            h.drain()
                            return h.activation(out=SROW[0:1, 0:128], in_=MROW[0:1, 384:512],
                                                func=AF.Exp, scale=-0.5)
                        e_is = E("scalar", f_is, deps=[e_ms, e_const])

                        def f_mis(h):
                            return h.tensor_tensor(out=SROW[0:1, 128:256], in0=MROW[0:1, 0:128],
                                                   in1=SROW[0:1, 0:128], op=AluOpType.mult)
                        e_mis = E("vector", f_mis, deps=[e_is])

                        def f_rep(h):
                            h.matmul(PB[5][:, 0:128], ONER[:, :], SROW[0:1, 0:128],
                                     start=True, stop=True)
                            return h.matmul(PB[5][:, 128:256], ONER[:, :], SROW[0:1, 128:256],
                                            start=True, stop=True)
                        e_rep = E("tensor", f_rep, deps=[e_mis, bank_free[5]])

                        def f_repev(h):
                            h.activation(out=ISR[:, 0:128], in_=PB[5][:, 0:128], func=AF.Copy)
                            return h.activation(out=MISR[:, 0:128], in_=PB[5][:, 128:256],
                                                func=AF.Copy)
                        e_repev = E("scalar", f_repev, deps=[e_rep, prev_repev])
                        bank_free[5] = e_repev
                        prev_repev = e_repev

                        def f_xn(h, tt=tt, st=st, slot=slot):
                            xf = XFT[:, 2048 * slot:2048 * slot + 2048]
                            xf3 = xf.rearrange("p (k t) -> p k t", t=128)
                            isv = ISR[:, 0:128].unsqueeze(1).broadcast_to([128, KT, 128])
                            misv = MISR[:, 0:128].unsqueeze(1).broadcast_to([128, KT, 128])
                            h.tensor_tensor(out=xf3, in0=xf3, in1=isv, op=AluOpType.mult)
                            h.drain()
                            ov = xtv()[:, :, tt * TT + st * 128: tt * TT + st * 128 + 128]
                            return h.tensor_tensor(out=ov, in0=xf3, in1=misv,
                                                   op=AluOpType.subtract)
                        e_xn = E("vector", f_xn, deps=[e_repev])
                        prev_xn_evt = e_xn
                        e_xn_list.append(e_xn)
                    e_xn_done = e_xn_list[-1]

                    e_xp_done = None
                    wq_dep = omm_last
                    for eq in range(4):
                        def f_wql(h, eq=eq):
                            h.dma_start(out=WQ1.rearrange("p (k e) -> p k e", e=512),
                                        in_=winb[:, :, 512 * eq:512 * eq + 512].transpose([1, 0, 2])).then_inc(s_sy, 16)
                            return None
                        e_wql = E("sync", f_wql, deps=[wq_dep, e_wprep_done, e_wcwe_done] + pool_guard)
                        for et4 in range(4):
                            et = eq * 4 + et4
                            pbk = et4 % 2

                            def f_pmm(h, et4=et4, pbk=pbk, tt=tt):
                                last = None
                                for kt in range(KT):
                                    wsl = WQ1.rearrange("p (k e) -> p k e", e=512)[:, kt, 128 * et4:128 * et4 + 128]
                                    xsl = xtv()[:, kt, tt * TT: tt * TT + TT]
                                    last = h.matmul(PB[pbk][:, :], wsl, xsl,
                                                    start=(kt == 0), stop=(kt == KT - 1))
                                return last
                            e_pmm = E("tensor", f_pmm, deps=[e_wql, e_xn_done, bank_free[pbk]])

                            def f_pev(h, et=et, pbk=pbk, tt=tt):
                                ov = xpv()[:, et, tt * TT:tt * TT + TT]
                                return h.activation(out=ov, in_=PB[pbk][:, :], func=AF.Identity,
                                                    bias=BINS[:, et:et + 1])
                            e_pev = E("scalar", f_pev, deps=[e_pmm])
                            bank_free[pbk] = e_pev
                            e_xp_done = e_pev
                            wq_dep = e_pmm

                    def f_ymm(h, tt=tt):
                        last = None
                        for kt in range(KT):
                            xsl = xpv()[:, kt, tt * TT:tt * TT + TT]
                            for pt in range(4):
                                wv = PWY[:, 512 * kt + 128 * pt:512 * kt + 128 * pt + 128]
                                last = h.matmul(PB[pt][:, :], wv, xsl,
                                                start=(kt == 0), stop=(kt == KT - 1))
                            bsl = BWB[:].rearrange("p (k s) -> p k s", s=16)[:, kt]
                            last = h.matmul(PB[4][0:16, :], bsl, xsl,
                                            start=(kt == 0), stop=(kt == KT - 1))
                        return last
                    e_ymm = E("tensor", f_ymm, deps=[e_xp_done, e_pow,
                                                     bank_free[0], bank_free[1], bank_free[2],
                                                     bank_free[3], bank_free[4]])

                    def f_zev(h, b=b, t0=t0):
                        last = None
                        for pt in range(4):
                            for hi in range(8):
                                j4, j3, j2 = hi >> 2, (hi >> 1) & 1, hi & 1
                                inv = PB[pt][:, :].rearrange("p (c jh j1 j0) -> p c jh j1 j0",
                                                             jh=8, j1=2, j0=2)[:, :, hi]
                                mv = MSK[:, 32 * pt:32 * pt + 32].rearrange(
                                    "p (jh j1 j0) -> p jh j1 j0", jh=8, j1=2, j0=2)[:, hi]
                                mv = mv.unsqueeze(1).broadcast_to([128, 16, 2, 2])
                                zt = ZS[:, L * pt + t0: L * pt + t0 + TT]
                                ov = zt.rearrange("p (c j0 j1 jh) -> p c j0 j1 jh",
                                                  j0=2, j1=2, jh=8)
                                ov = ov[:, :, :, :, 4 * j2 + 2 * j3 + j4].transpose([0, 1, 3, 2])
                                last = h.tensor_tensor(out=ov, in0=inv, in1=mv, op=AluOpType.mult)
                        return last
                    e_zev = E("vector", f_zev, deps=[e_ymm, e_lds, e_shift_done.get(b - 1)])
                    for pt in range(4):
                        bank_free[pt] = e_zev
                    e_zs_written[b].append(e_zev)

                    def f_dev(h, t0=t0):
                        h.activation(out=DTMP[:, :], in_=PB[4][0:16, :], func=AF.Exp)
                        return h.activation(out=DSB[:, t0:t0 + TT], in_=DTMP[:, :],
                                            func=AF.Ln, bias=1.0)
                    e_dev = E("scalar", f_dev, deps=[e_ymm, e_dst_last.get(b - 1), e_const])

                    def f_devS(h, t0=t0):
                        last = None
                        for hi in range(8):
                            j4, j3, j2 = hi >> 2, (hi >> 1) & 1, hi & 1
                            inv = DSB[:, t0:t0 + TT].rearrange("p (c jh j1 j0) -> p c jh j1 j0",
                                                               jh=8, j1=2, j0=2)[:, :, hi]
                            ov = DSS[:, t0:t0 + TT].rearrange("p (c j0 j1 jh) -> p c j0 j1 jh",
                                                              j0=2, j1=2, jh=8)
                            ov = ov[:, :, :, :, 4 * j2 + 2 * j3 + j4].transpose([0, 1, 3, 2])
                            last = h.activation(out=ov, in_=inv, func=AF.Copy)
                        return last
                    e_devS = E("scalar", f_devS, deps=[e_dev])
                    bank_free[4] = e_devS

                    def f_dst(h, b=b, t0=t0):
                        h.dma_start(out=dscrB[b, :, t0:t0 + TT], in_=DSB[:, t0:t0 + TT]).then_inc(s_sy, 16)
                        return None
                    e_dst = E("sync", f_dst, deps=[e_devS])
                    e_dst_last[b] = e_dst

                    def f_h1(h, tt=tt):
                        last = None
                        for kt in range(KT):
                            wsl = RW1B[:].rearrange("p (k e) -> p k e", e=64)[:, kt]
                            xsl = xtv()[:, kt, tt * TT:tt * TT + TT]
                            last = h.matmul(PB[4][0:64, :], wsl, xsl,
                                            start=(kt == 0), stop=(kt == KT - 1))
                        return last
                    e_h1 = E("tensor", f_h1, deps=[e_xn_done, bank_free[4], e_ldc])

                    def f_h1b(h):
                        return h.activation(out=H1F[:, 0:TT], in_=PB[4][0:64, :],
                                            func=AF.Identity, bias=RB1S[:, 0:1])
                    e_h1b = E("scalar", f_h1b, deps=[e_h1])
                    bank_free[4] = e_h1b

                    def f_h1e(h):
                        return h.activation(out=H1F[:, TT:2 * TT], in_=H1F[:, 0:TT],
                                            func=AF.Exp, scale=-1.0)
                    e_h1e = E("scalar", f_h1e, deps=[e_h1b])

                    def f_h1v(h, t0=t0):
                        ex = H1F[:, TT:2 * TT]
                        h.tensor_scalar(out=ex, in0=ex, scalar1=1.0, scalar2=None,
                                        op0=AluOpType.add)
                        h.drain()
                        h.reciprocal(out=ex, in_=ex)
                        h.drain()
                        return h.tensor_tensor(out=H1B[:, t0:t0 + TT], in0=H1F[:, 0:TT],
                                               in1=ex, op=AluOpType.mult)
                    e_h1v = E("vector", f_h1v, deps=[e_h1e, h1_reader_last])

                    LGS = TOP5[:, 0:64]
                    SCR = TOP5[:, 64:128]
                    LSB = TOP5[:, 128:192]
                    ETP = TOP5[:, 192:256]
                    MC = TOP5[:, 256:264]

                    def f_lgmm(h, t0=t0):
                        last = None
                        for st in range(4):
                            last = h.matmul(PB[4][:, 128 * st:128 * st + 32],
                                            H1B[:, t0 + 128 * st:t0 + 128 * st + 128],
                                            RW2B, start=True, stop=True)
                        return last
                    e_lgmm = E("tensor", f_lgmm, deps=[e_h1v, bank_free[4], e_ldc])
                    h1_reader_last = e_lgmm

                    def f_top1(h):
                        for st in range(4):
                            lg = LGS[:, 16 * st:16 * st + 16]
                            h.tensor_tensor(out=lg, in0=PB[4][:, 128 * st:128 * st + 16],
                                            in1=B2G[:, 0:16], op=AluOpType.add)
                            h.drain()
                            m1 = MC[:, 2 * st:2 * st + 1]
                            m2 = MC[:, 2 * st + 1:2 * st + 2]
                            h.reduce_max(out=m1, in_=lg, axis=AX)
                            h.drain()
                            eq = SCR[:, 16 * st:16 * st + 16]
                            h.tensor_scalar(out=eq, in0=lg, scalar1=m1, scalar2=None,
                                            op0=AluOpType.is_ge)
                            h.drain()
                            h.scalar_tensor_tensor(out=eq, in0=eq, scalar=CONSTS[:, 2:3],
                                                   in1=lg, op0=AluOpType.mult,
                                                   op1=AluOpType.add)
                            h.drain()
                            h.reduce_max(out=m2, in_=eq, axis=AX)
                            h.drain()
                            h.tensor_scalar(out=LSB[:, 16 * st:16 * st + 16], in0=lg,
                                            scalar1=m1, scalar2=None, op0=AluOpType.subtract)
                            h.tensor_scalar(out=GEMB[:, 16 * st:16 * st + 16], in0=lg,
                                            scalar1=m2, scalar2=None, op0=AluOpType.is_ge)
                        return None
                    e_top1 = E("vector", f_top1, deps=[e_lgmm, e_const])
                    bank_free[4] = e_top1

                    def f_texp(h):
                        return h.activation(out=ETP, in_=LSB, func=AF.Exp)
                    e_texp = E("scalar", f_texp, deps=[e_top1])

                    def f_top2(h):
                        h.tensor_tensor(out=ETP, in0=ETP, in1=GEMB[:], op=AluOpType.mult)
                        h.drain()
                        for st in range(4):
                            den = MC[:, 2 * st + 1:2 * st + 2]
                            h.reduce_sum(out=den, in_=ETP[:, 16 * st:16 * st + 16], axis=AX)
                            h.drain()
                            h.reciprocal(out=den, in_=den)
                            h.drain()
                            h.tensor_scalar(out=CTT[:, 16 * st:16 * st + 16],
                                            in0=ETP[:, 16 * st:16 * st + 16],
                                            scalar1=den, scalar2=None, op0=AluOpType.mult)
                        return None
                    e_top2 = E("vector", f_top2, deps=[e_texp])

                    e_cev_prev = cst_last
                    for st in range(4):
                        t0s = t0 + st * 128

                        def f_ctr(h, st=st):
                            return h.transpose(PB7[0:16, 0:128], CTT[:, 16 * st:16 * st + 16],
                                               IDB[:])
                        e_ctr = E("tensor", f_ctr, deps=[e_top2, e_cev_prev, e_lds])

                        def f_cev(h, t0s=t0s):
                            return h.activation(out=CFF[:, t0s:t0s + 128],
                                                in_=PB7[0:16, 0:128], func=AF.Copy)
                        e_cev = E("scalar", f_cev, deps=[e_ctr])
                        e_cev_prev = e_cev
                    e_cf_last = e_cev_prev

                    def f_crx(h, t0=t0):
                        h.matmul(PB[5][:, :], RPM[:, 0:128], CFF[:, t0:t0 + TT],
                                 start=True, stop=True)
                        return h.matmul(PB[6][:, :], RPM[:, 128:256], CFF[:, t0:t0 + TT],
                                        start=True, stop=True)
                    e_crx = E("tensor", f_crx, deps=[e_cf_last, bank_free[5], bank_free[6],
                                                     e_ldc])

                    def f_crev(h):
                        h.activation(out=CREP[:, 0:TT], in_=PB[5][:, :], func=AF.Copy)
                        return h.activation(out=CREP[:, TT:2 * TT], in_=PB[6][:, :],
                                            func=AF.Copy)
                    e_crev = E("scalar", f_crev, deps=[e_crx, omm_last])
                    bank_free[5] = e_crev
                    bank_free[6] = e_crev
                    cst_last = e_crev

                    def f_xam(h, tt=tt):
                        last = None
                        for kt in range(KT):
                            xsl = xtv()[:, kt, tt * TT:tt * TT + TT]
                            for q in range(2):
                                wsl = AMB[:].rearrange("p (k e) -> p k e", e=256)[:, kt, 128 * q:128 * q + 128]
                                last = h.matmul(PB[q][:, :], wsl, xsl,
                                                start=(kt == 0), stop=(kt == KT - 1))
                        return last
                    e_xam = E("tensor", f_xam, deps=[e_xn_done, bank_free[0], bank_free[1],
                                                     e_ldc])

                    def f_xaev(h):
                        h.activation(out=XAB[:, 0:TT], in_=PB[0][:, :], func=AF.Copy)
                        return h.activation(out=XAB[:, TT:2 * TT], in_=PB[1][:, :], func=AF.Copy)
                    e_xaev = E("scalar", f_xaev, deps=[e_xam, omm_last])
                    bank_free[0] = e_xaev
                    bank_free[1] = e_xaev

                    def f_wxa(h):
                        return h.tensor_tensor(out=XAB[:, :], in0=XAB[:, :], in1=CREP[:, :],
                                               op=AluOpType.mult)
                    e_wxa = E("vector", f_wxa, deps=[e_xaev, e_crev])

                    oq_dep = omm_last
                    for oq in range(8):
                        def f_oql(h, oq=oq):
                            h.dma_start(out=WQ2.rearrange("p (k e) -> p k e", e=256),
                                        in_=wffnb[:, :, 256 * oq:256 * oq + 256].transpose([1, 0, 2])).then_inc(s_sy, 16)
                            h.dma_start(out=WQ1[:, 0:KT * 256].rearrange("p (k e) -> p k e", e=256),
                                        in_=woutdb[:, :, 256 * oq:256 * oq + 256].transpose([1, 0, 2])).then_inc(s_sy, 16)
                            return None
                        e_oql = E("sync", f_oql, deps=[oq_dep, e_xp_done if oq == 0 else None] + pool_guard, ndma=2)
                        for ot2 in range(2):
                            ot = oq * 2 + ot2
                            pbk = 2 + ot2

                            def f_omm(h, ot=ot, ot2=ot2, pbk=pbk, tt=tt):
                                last = None
                                for kt in range(KT):
                                    wsl = WQ2.rearrange("p (k e) -> p k e", e=256)[:, kt, 128 * ot2:128 * ot2 + 128]
                                    xsl = xtv()[:, kt, tt * TT:tt * TT + TT]
                                    last = h.matmul(PB[pbk][:, :], wsl, xsl,
                                                    start=(kt == 0), stop=False)
                                for kt in range(KT):
                                    wsl = WQ1[:, 0:KT * 256].rearrange("p (k e) -> p k e", e=256)[:, kt, 128 * ot2:128 * ot2 + 128]
                                    xsl = xpv()[:, kt, tt * TT:tt * TT + TT]
                                    last = h.matmul(PB[pbk][:, :], wsl, xsl,
                                                    start=False, stop=False)
                                for q in range(2):
                                    wsl = BMB[:].rearrange("p (q o) -> p q o", o=D)[:, q, 128 * ot:128 * ot + 128]
                                    last = h.matmul(PB[pbk][:, :], wsl, XAB[:, TT * q:TT * q + TT],
                                                    start=False, stop=(q == 1))
                                return last
                            e_omm = E("tensor", f_omm, deps=[e_oql, e_wxa, e_xp_done,
                                                             bank_free[pbk]])
                            omm_last = e_omm
                            oq_dep = e_omm
                            tc = ot % 2

                            def f_oev(h, ot=ot, pbk=pbk, tc=tc):
                                return h.activation(out=TCO[:, TT * tc:TT * tc + TT],
                                                    in_=PB[pbk][:, :], func=AF.Identity,
                                                    bias=BCS[:, ot:ot + 1])
                            e_oev = E("scalar", f_oev, deps=[e_omm, tco_free[tc]])
                            bank_free[pbk] = e_oev

                            def f_oacc(h, b=b, ot=ot, t0=t0, tc=tc):
                                h.dma_start(out=TCO[:, TT * tc:TT * tc + TT],
                                            in_=xin[b, ot, :, t0:t0 + TT],
                                            accum_op=AluOpType.add).then_inc(s_gp, 16)
                                return None
                            e_oacc = E("gpsimd", f_oacc, deps=[e_oev])

                            def f_ost(h, b=b, ot=ot, t0=t0, tc=tc):
                                h.dma_start(out=part[b, ot, :, t0:t0 + TT],
                                            in_=TCO[:, TT * tc:TT * tc + TT]).then_inc(s_sy, 16)
                                return None
                            e_ost = E("sync", f_ost, deps=[e_oacc])
                            tco_free[tc] = e_ost
                            part_evt[(b, ot, t0)] = e_ost
                a_done = omm_last

                def f_pwl(h):
                    h.dma_start(out=PWF, in_=pwfscr[:]).then_inc(s_sy, 16)
                    h.dma_start(out=PWI, in_=pwiscr[:]).then_inc(s_sy, 16)
                    return None
                e_pwl = E("sync", f_pwl, deps=[a_done, e_powst], ndma=2)
                drep_dep = [None, None]
                e_sc_prev = None
                e_fmm_prev = None
                for s in range(S):
                    sl = 0

                    def f_dr(h, b=b, g=g, s=s, sl=sl):
                        src = dscrB[b, s:s + 1, g * LH:(g + 1) * LH].broadcast_to([128, LH])
                        h.dma_start(out=DREP[:, 0:LH], in_=src).then_inc(s_gp, 16)
                        return None
                    e_dr = E("gpsimd", f_dr, deps=[e_dst_last[b], drep_dep[sl]])

                    def f_sc(h, s=s, sl=sl):
                        drv = DREP[:, 0:LH].unsqueeze(1).broadcast_to([128, KT, LH])
                        h.tensor_tensor(out=xtv(), in0=xpv(), in1=drv, op=AluOpType.mult)
                        h.drain()
                        piv = PWI.rearrange("p (k s r) -> p k s r", s=16, r=32)[:, :, s]
                        piv = piv.unsqueeze(2).broadcast_to([128, KT, NKH, 32])
                        xcv = XT[:].rearrange("p (k c r) -> p k c r", c=NKH, r=32)
                        h.tensor_tensor(out=xcv, in0=xcv, in1=piv, op=AluOpType.mult)
                        h.drain()
                        w = 16
                        for lev in range(5):
                            iv0 = xcv[:, :, :, 0:w]
                            iv1 = xcv[:, :, :, w:2 * w]
                            h.tensor_tensor(out=iv0, in0=iv0, in1=iv1, op=AluOpType.add)
                            h.drain()
                            w //= 2
                        sv = xcv[:, :, :, 0]
                        return h.tensor_copy(out=MSEQ.rearrange("p (k c) -> p k c", c=NKH),
                                             in_=sv)
                    e_sc = E("vector", f_sc, deps=[e_dr, a_done, e_sc_prev, e_pwl, e_fmm_prev])
                    drep_dep[sl] = e_sc

                    def f_sc2(h, g=g, s=s):
                        a32 = PWF.rearrange("p (k s r) -> p k s r", s=16, r=32)[:, :, s, 0]
                        av = a32.unsqueeze(2).broadcast_to([128, KT, NKH])
                        h.tensor_copy(out=ACZ.rearrange("p (k c) -> p k c", c=NKH), in_=av)
                        h.memset(ACZ.rearrange("p (k c) -> p k c", c=NKH)[:, :, 0:1], 0.0)
                        if g == 1:
                            cv = CARRY[:].rearrange("p (k s) -> p k s", s=16)[:, :, s]
                            h.tensor_tensor(out=TOP5[:, 0:KT], in0=a32, in1=cv,
                                            op=AluOpType.mult)
                            mv0 = MSEQ.rearrange("p (k c) -> p k c", c=NKH)[:, :, 0]
                            h.tensor_tensor(out=mv0, in0=mv0, in1=TOP5[:, 0:KT],
                                            op=AluOpType.add)
                        h.tensor_tensor_scan(out=MS2, data0=ACZ, data1=MSEQ,
                                             initial=0.0, op0=AluOpType.mult, op1=AluOpType.add)
                        if g == 0:
                            return h.tensor_copy(
                                out=CARRY[:].rearrange("p (k s) -> p k s", s=16)[:, :, s],
                                in_=MS2.rearrange("p (k c) -> p k c", c=NKH)[:, :, NKH - 1])
                        return None
                    e_sc2 = E("vector", f_sc2, deps=[e_sc, e_pwl])
                    e_sc_prev = e_sc2

                    def f_fmm(h, s=s):
                        last = None
                        for kt in range(KT):
                            msl = MS2.rearrange("p (k c) -> p k c", c=NKH)[:, kt]
                            pfv = PWF.rearrange("p (k s r) -> p k s r", s=16, r=32)[:, kt, s]
                            last = h.matmul(PB[4][0:32, 0:32], msl, pfv,
                                            start=(kt == 0), stop=(kt == KT - 1))
                        return last
                    e_fmm = E("tensor", f_fmm, deps=[e_sc2, bank_free[4], e_pwl])
                    e_fmm_prev = e_fmm

                    def f_fev(h, s=s, g=g):
                        return h.tensor_copy(out=FST[32 * g:32 * g + 32, 32 * s:32 * s + 32],
                                             in_=PB[4][0:32, 0:32])
                    e_fev = E("vector", f_fev, deps=[e_fmm, e_fstore.get(b - 1)])
                    bank_free[4] = e_fev
                last_scan_evt = e_sc_prev
                pool_guard = [e_sc_prev, e_fmm_prev]
                e_fev_done = e_fev

            e_zs0 = None
            if dbg and b == 0:
                dzs0_ap = nc.dram_tensor("dzs0", [128, 2 * L], BF16, kind="ExternalOutput").ap()

                def f_zs0(h):
                    h.dma_start(out=dzs0_ap[:, 0:L], in_=ZS[:, 0:L]).then_inc(s_sy, 16)
                    h.dma_start(out=dzs0_ap[:, L:2 * L], in_=ZS[:, L:2 * L]).then_inc(s_sy, 16)
                    return None
                e_zs0 = E("sync", f_zs0, deps=list(e_zs_written[b]), ndma=2)

            def f_fstore(h, b=b):
                h.dma_start(out=fscr[b], in_=FST[:, :]).then_inc(s_sy, 16)
                return None
            e_fst = E("sync", f_fstore, deps=[e_fev_done])
            e_fstore[b] = e_fst

            def f_fgath(h, b=b):
                src = fscr[b].rearrange("k (s j) -> s k j", s=16)[:, 0:63, :]
                h.dma_start(out=GFS.rearrange("s (k j) -> s k j", j=32), in_=src).then_inc(s_gp, 16)
                return None
            e_fgath = E("gpsimd", f_fgath, deps=[e_fst] + pool_guard)

            e_dx_list = []
            for nt in range(4):
                def f_dx(h, nt=nt):
                    return h.matmul(PB[5][:, :], RPD[:, :], DSS[:, 512 * nt:512 * nt + 512],
                                    start=True, stop=True)
                e_dx = E("tensor", f_dx, deps=[bank_free[5], e_zs_written[b][-1],
                                               (e_dx_list[-1] if e_dx_list else None)])

                def f_dxe(h, nt=nt):
                    return h.activation(out=DEXP[:, 512 * nt:512 * nt + 512], in_=PB[5][:, :],
                                        func=AF.Copy)
                e_dxe = E("scalar", f_dxe, deps=[e_dx])
                bank_free[5] = e_dxe
                e_dx_list.append(e_dxe)
            e_dexp_done = e_dx_list[-1]

            def f_zmul(h):
                last = None
                for pt in range(4):
                    last = h.tensor_tensor(out=ZS[:, L * pt:L * pt + L],
                                           in0=ZS[:, L * pt:L * pt + L],
                                           in1=DEXP[:, :], op=AluOpType.mult)
                return last
            e_zmul = E("vector", f_zmul, deps=[e_dexp_done, e_zs0] + e_zs_written[b])

            def f_zacc0(h):
                return h.memset(ZACC[:, :], 0.0)
            e_zacc0 = E("vector", f_zacc0, deps=[e_zmul, e_zbrv_done.get(b - 1)])

            shift_base = sch.cum["gpsimd"]

            def f_shift(h, base=shift_base):
                cum = base
                for pt in range(4):
                    for ph in range(8):
                        p = 8 * pt + ph
                        rows = slice(16 * ph, 16 * ph + 16)
                        if p == 0:
                            src = ZS[rows, 0:L]
                            dst = ZACC[rows, 0:L]
                        else:
                            src = ZS[rows, L * pt:L * pt + L - p]
                            dst = ZACC[rows, p:L]
                        h.dma_start(out=dst, in_=src,
                                    accum_op=AluOpType.add).then_inc(s_gp, 16)
                        cum += 16
                    h.wait_ge(s_gp, cum)
                return None
            e_shift = E("gpsimd", f_shift, deps=[e_zacc0], ndma=32)
            e_shift_done[b] = e_shift

            def f_gfadd(h):
                return h.tensor_tensor(out=ZACC[0:16, 32:32 + 63 * 32],
                                       in0=ZACC[0:16, 32:32 + 63 * 32],
                                       in1=GFS[:, :], op=AluOpType.add)
            e_gfadd = E("vector", f_gfadd, deps=[e_shift, e_fgath])
            pool_guard = pool_guard + [e_gfadd]

            def f_zbrv(h):
                last = None
                for hi in range(8):
                    j4, j3, j2 = hi >> 2, (hi >> 1) & 1, hi & 1
                    inv = ZACC[:].rearrange("p (c jh j1 j0) -> p c jh j1 j0",
                                            jh=8, j1=2, j0=2)[:, :, hi]
                    ov = ZACCB[:].rearrange("p (c j0 j1 jh) -> p c j0 j1 jh",
                                            j0=2, j1=2, jh=8)
                    ov = ov[:, :, :, :, 4 * j2 + 2 * j3 + j4].transpose([0, 1, 3, 2])
                    last = h.activation(out=ov, in_=inv, func=AF.Copy)
                return last
            e_zbrv = E("scalar", f_zbrv, deps=[e_gfadd, e_cmm_last.get(b - 1)])
            e_zbrv_done[b] = e_zbrv

            def f_wcwl(h):
                h.dma_start(out=WCWE, in_=wcwescr[:]).then_inc(s_sy, 16)
                return None
            e_wcwl = E("sync", f_wcwl, deps=[e_zbrv] + pool_guard)
            pool_guard = pool_guard + [e_wcwl]
            for tt in range(4):
                t0 = tt * TT
                for ot in range(KT):
                    pbk = 2 + (ot % 2)
                    tc = ot % 2

                    def f_cmm(h, ot=ot, t0=t0, pbk=pbk):
                        return h.matmul(PB[pbk][:, :], WCWE[:, 128 * ot:128 * ot + 128],
                                        ZACCB[:, t0:t0 + TT], start=True, stop=True)
                    e_cmm = E("tensor", f_cmm, deps=[e_zbrv, e_wcwl, bank_free[pbk]])
                    e_cmm_last[b] = e_cmm

                    def f_cev(h, pbk=pbk, tc=tc):
                        return h.activation(out=TCO[:, TT * tc:TT * tc + TT],
                                            in_=PB[pbk][:, :], func=AF.Copy)
                    e_cev = E("scalar", f_cev, deps=[e_cmm, tco_free[tc]])
                    bank_free[pbk] = e_cev

                    def f_cacc(h, b=b, ot=ot, t0=t0, tc=tc):
                        h.dma_start(out=TCO[:, TT * tc:TT * tc + TT],
                                    in_=part[b, ot, :, t0:t0 + TT],
                                    accum_op=AluOpType.add).then_inc(s_gp, 16)
                        return None
                    e_cacc = E("gpsimd", f_cacc, deps=[e_cev, part_evt[(b, ot, t0)]])

                    def f_cst(h, b=b, ot=ot, t0=t0, tc=tc):
                        h.dma_start(out=yout[b, ot, :, t0:t0 + TT],
                                    in_=TCO[:, TT * tc:TT * tc + TT]).then_inc(s_sy, 16)
                        return None
                    e_cst = E("sync", f_cst, deps=[e_cacc])
                    tco_free[tc] = e_cst
                    final_store = e_cst
            pool_guard = pool_guard + [e_cmm_last[b]]

        if dbg:
            ddump = {}
            for nm, buf in [("dxpb", XPB), ("dxt", XT), ("dzs", ZS), ("dza", ZACC),
                            ("dzb", ZACCB), ("ddexp", DEXP), ("dcar", CARRY),
                            ("dms2", MS2), ("dfst", FST), ("dgfs", GFS),
                            ("drow1", ROW1), ("dpwy", PWY), ("ddbgt", DBGT)]:
                shp = list(buf.shape) if hasattr(buf, "shape") else None
                dt_ = F32 if nm == "ddbgt" else BF16
                ddump[nm] = nc.dram_tensor(nm, shp, dt_, kind="ExternalOutput").ap()

            def f_dump(h):
                for nm, buf in [("dxpb", XPB), ("dxt", XT), ("dzs", ZS), ("dza", ZACC),
                                ("dzb", ZACCB), ("ddexp", DEXP), ("dcar", CARRY),
                                ("dms2", MS2), ("dfst", FST), ("dgfs", GFS),
                                ("drow1", ROW1), ("dpwy", PWY), ("ddbgt", DBGT)]:
                    h.dma_start(out=ddump[nm][:], in_=(buf[:] if hasattr(buf, 'ap_type') or str(type(buf)).find('Handle') >= 0 else buf)).then_inc(s_sy, 16)
                return None
            E("sync", f_dump, deps=[final_store, e_cmm_last.get(b_run - 1)], ndma=13)

        def f_fin(h):
            return None
        E("gpsimd", f_fin, deps=[final_store], ndma=0)

        sch.emit(block, sems)

    return nc


def kernel(**inputs):
    L0 = {k[3:]: np.asarray(v) for k, v in inputs.items() if k.startswith("L0_")}
    x = np.ascontiguousarray(L0["attn_x"])
    perm = bitrev_perm(L)
    import ml_dtypes

    def lhsT(w):
        return np.ascontiguousarray(w.T.reshape(KT, 128, -1))

    brv5 = bitrev_perm(32)[:32]
    mask_h = np.zeros((4, 128, 32), dtype=np.float32)
    for t in range(4):
        for ph in range(8):
            p = 8 * t + ph
            mask_h[t, 16 * ph:16 * ph + 16, :] = ((brv5 + p) < 32)[None, :]
    repmoe_h = np.zeros((16, 256), dtype=np.float32)
    for e in range(16):
        repmoe_h[e, 16 * e:16 * e + 16] = 1.0
    repd_h = np.zeros((16, 128), dtype=np.float32)
    for ph in range(8):
        repd_h[:, 16 * ph:16 * ph + 16] = np.eye(16)

    shared = dict(
        win=lhsT(L0["attn_proj_in_w"]),
        wffn=lhsT(L0["ffn_W"]),
        wout=lhsT(L0["attn_proj_out_w"]),
        amoe=np.ascontiguousarray(L0["ffn_A"].reshape(256, D).T.reshape(KT, 128, 256)),
        bmoe=np.ascontiguousarray(L0["ffn_B"].transpose(0, 2, 1).reshape(256, D)
                                  .reshape(2, 128, D)),
        rw1=lhsT(L0["ffn_rw1"]),
        rw2=np.ascontiguousarray(L0["ffn_rw2"].T),
        bw=lhsT(L0["attn_B_w"]),
        cw=np.ascontiguousarray(L0["attn_C_w"].reshape(KT, 128, 16)),
        aA=np.ascontiguousarray(L0["attn_A"].reshape(KT, 128, 16)),
        dpar=np.ascontiguousarray(L0["attn_D_param"].reshape(KT, 128).T),
        binp=np.ascontiguousarray(L0["attn_proj_in_b"].reshape(KT, 128).T),
        bcmb=np.ascontiguousarray(np.stack([L0["ffn_bias"].reshape(KT, 128).T,
                                            L0["attn_proj_out_b"].reshape(KT, 128).T], 0)),
        rb1v=np.ascontiguousarray(L0["ffn_rb1"].reshape(64, 1)),
        rb2rep=np.ascontiguousarray(np.broadcast_to(L0["ffn_rb2"][None, :], (128, 32))),
        gaterep=np.ascontiguousarray(np.broadcast_to(L0["ffn_gates"][None, :], (128, 16))),
        identb=np.eye(128, dtype=ml_dtypes.bfloat16),
        maskt=mask_h.astype(ml_dtypes.bfloat16),
        repmoe=repmoe_h.astype(ml_dtypes.bfloat16),
        repd=repd_h.astype(ml_dtypes.bfloat16),
    )

    in_maps = []
    for c in range(8):
        xs = x[2 * c:2 * c + 2][:, perm, :]
        xs = np.ascontiguousarray(xs.transpose(0, 2, 1).reshape(B, KT, 128, L))
        m = dict(shared)
        m["xin"] = xs
        in_maps.append(m)

    nc = build_nc()
    res = run_bass_kernel_spmd(nc, in_maps, core_ids=list(range(8)))
    outs = []
    for c in range(8):
        y = np.asarray(res.results[c]["yout"]).reshape(B, D, L).transpose(0, 2, 1)
        outs.append(y[:, perm, :])
    return np.concatenate(outs, 0).astype(np.float32)


# version 10
# speedup vs baseline: 1.0092x; 1.0092x over previous
import numpy as np
import concourse.bass as bass
import concourse.mybir as mybir
from concourse.bass_utils import run_bass_kernel_spmd
from concourse.alu_op_type import AluOpType

F32 = mybir.dt.float32
BF16 = mybir.dt.bfloat16
AF = mybir.ActivationFunctionType
AX = mybir.AxisListType.X

B, L, D, S, C = 2, 2048, 2048, 16, 32
KT, NH = 16, 2
LH = L // NH
NKH = LH // C
TT = 512
NEG = -1.0e30
EPS = 1e-5


def bitrev_perm(n):
    idx = np.arange(n)
    c, pos = idx // 32, idx % 32
    r = np.zeros_like(pos)
    for b in range(5):
        r |= ((pos >> b) & 1) << (4 - b)
    return c * 32 + r


class Sched:
    ENGS = ("tensor", "vector", "scalar", "sync", "gpsimd")

    def __init__(self):
        self.events = {e: [] for e in self.ENGS}
        self.cum = {e: 0 for e in self.ENGS}

    def ev(self, eng, fn, deps=(), ndma=1):
        n = len(self.events[eng])
        dd = [d for d in deps
              if d is not None and (d[0] != eng or eng in ("sync", "gpsimd"))]
        if eng in ("sync", "gpsimd"):
            self.cum[eng] += 16 * ndma
        else:
            self.cum[eng] += 1
        self.events[eng].append((fn, dd, self.cum[eng], ndma))
        return (eng, n)

    def emit(self, block, sems):
        after = {e: [ev[2] for ev in self.events[e]] for e in self.ENGS}
        for eng in self.ENGS:
            evs = self.events[eng]
            is_dma = eng in ("sync", "gpsimd")

            def make(eng=eng, evs=evs, is_dma=is_dma):
                def body(h):
                    sem = sems[eng]
                    cnt = [0]
                    if is_dma:
                        orig = h.dma_start

                        def counted(*a, **k):
                            cnt[0] += 1
                            return orig(*a, **k)
                        h.dma_start = counted
                    for i, (fn, dd, _v, ndma) in enumerate(evs):
                        best = {}
                        for (deng, didx) in dd:
                            v = after[deng][didx]
                            if best.get(deng, -1) < v:
                                best[deng] = v
                        for deng, v in best.items():
                            h.wait_ge(sems[deng], v)
                        cnt[0] = 0
                        r = fn(h)
                        if is_dma:
                            assert cnt[0] == ndma, (
                                f"{eng} event {i}: declared ndma={ndma} actual={cnt[0]}")
                        else:
                            h.drain().then_inc(sem, 1)
                return body
            getattr(block, eng)(make())


def build_nc(dbg=False, b_run=B):
    nc = bass.Bass("TRN2", target_bir_lowering=False)

    def din(name, shp, dt=F32):
        return nc.dram_tensor(name, shp, dt, kind="ExternalInput").ap()

    def dint(name, shp, dt=F32):
        return nc.dram_tensor(name, shp, dt, kind="Internal").ap()

    dkind = "ExternalOutput" if dbg else "Internal"
    xin = din("xin", [B, KT, 128, L])
    win = din("win", [KT, 128, D])
    wffn = din("wffn", [KT, 128, D])
    wout = din("wout", [KT, 128, D])
    amoe = din("amoe", [KT, 128, 256])
    bmoe = din("bmoe", [2, 128, D])
    rw1 = din("rw1", [KT, 128, 64])
    rw2 = din("rw2", [64, 32])
    bwd = din("bw", [KT, 128, 16])
    cwd = din("cw", [KT, 128, 16])
    aA = din("aA", [KT, 128, 16])
    dpar = din("dpar", [128, KT])
    binp = din("binp", [128, KT])
    bcmb = din("bcmb", [2, 128, KT])
    rb1v = din("rb1v", [64, 1])
    rb2rep = din("rb2rep", [128, 32])
    gaterep = din("gaterep", [128, 16])
    identb = din("identb", [128, 128], BF16)
    maskt = din("maskt", [4, 128, 32], BF16)
    repmoe = din("repmoe", [16, 256], BF16)
    repd = din("repd", [16, 128], BF16)

    winb = nc.dram_tensor("winb", [KT, 128, D], BF16, kind=dkind).ap()
    wffnb = nc.dram_tensor("wffnb", [KT, 128, D], BF16, kind=dkind).ap()
    woutdb = nc.dram_tensor("woutdb", [KT, 128, D], BF16, kind=dkind).ap()
    wcwescr = dint("wcwescr", [128, D], BF16)
    pwfscr = dint("pwfscr", [128, KT * S * 32], BF16)
    pwiscr = dint("pwiscr", [128, KT * S * 32], BF16)
    dscrB = nc.dram_tensor("dscrB", [B, 16, L], BF16, kind=dkind).ap()
    fscr = nc.dram_tensor("fscr", [B, 64, 512], BF16, kind=dkind).ap()
    part = nc.dram_tensor("part", [B, KT, 128, L], F32, kind=dkind).ap()
    yout = nc.dram_tensor("yout", [B, KT, 128, L], F32, kind="ExternalOutput").ap()

    sch = Sched()
    E = sch.ev

    import contextlib
    st_ = contextlib.ExitStack()
    with st_:
        XT = st_.enter_context(nc.sbuf_tensor([128, KT * LH], BF16))
        XPB = st_.enter_context(nc.sbuf_tensor([128, KT * LH], BF16))
        PWY = st_.enter_context(nc.sbuf_tensor([128, KT * S * 32], BF16))
        POOL = st_.enter_context(nc.sbuf_tensor([128, 16384], BF16))
        PWF = POOL[:, 0:8192]
        PWI = POOL[:, 8192:16384]
        WQ1 = POOL[:, 0:8192]
        WQ2 = POOL[:, 8192:12288]
        AMB = st_.enter_context(nc.sbuf_tensor([128, KT * 256], BF16))
        BMB = st_.enter_context(nc.sbuf_tensor([128, 2 * D], BF16))
        RW1B = st_.enter_context(nc.sbuf_tensor([128, KT * 64], BF16))
        RW2T = st_.enter_context(nc.sbuf_tensor([128, 32], BF16))
        RW2B = RW2T[64:128, 0:32]
        BWB = st_.enter_context(nc.sbuf_tensor([128, KT * 16], BF16))
        CWB = st_.enter_context(nc.sbuf_tensor([128, KT * 16], BF16))
        XFT = st_.enter_context(nc.sbuf_tensor([128, 2048], F32))
        XB2 = st_.enter_context(nc.sbuf_tensor([128, 2048], BF16))
        XXB = st_.enter_context(nc.sbuf_tensor([128, 2048], BF16))
        ISR = st_.enter_context(nc.sbuf_tensor([128, 128], F32))
        MISR = st_.enter_context(nc.sbuf_tensor([128, 128], F32))
        ZS = st_.enter_context(nc.sbuf_tensor([128, 4 * L], BF16))
        ZACC = st_.enter_context(nc.sbuf_tensor([128, L], BF16))
        ZACCB = st_.enter_context(nc.sbuf_tensor([128, L], BF16))
        DEXP = st_.enter_context(nc.sbuf_tensor([128, L], BF16))
        DREP = st_.enter_context(nc.sbuf_tensor([128, LH], BF16))
        ROW1 = st_.enter_context(nc.sbuf_tensor([128, L], BF16))
        DSS = ROW1[0:16, :]
        DSB = POOL[0:16, 12288:12288 + L]
        H1B = ROW1[64:128, :]
        ROW2 = st_.enter_context(nc.sbuf_tensor([128, 1536], F32))
        DTMP = ROW2[0:16, 0:512]
        H1F = ROW2[64:128, 0:1024]
        XAB = st_.enter_context(nc.sbuf_tensor([128, 2 * TT], BF16))
        CTT = st_.enter_context(nc.sbuf_tensor([128, 64], BF16))
        GEMB = st_.enter_context(nc.sbuf_tensor([128, 64], F32))
        CFF = ROW1[32:48, :]
        CREP = st_.enter_context(nc.sbuf_tensor([128, 2 * TT], BF16))
        TOP5 = st_.enter_context(nc.sbuf_tensor([128, 264], F32))
        CONSTS = st_.enter_context(nc.sbuf_tensor([128, 4], F32))
        DBGT = st_.enter_context(nc.sbuf_tensor([128, 48], F32))
        MROW = ROW2[0:1, 512:1024]
        SROW = ROW2[0:1, 1024:1280]
        MSEQ = XB2[:, 0:512]
        MS2 = XB2[:, 512:1024]
        ACZ = XB2[:, 1024:1536]
        CARRY = st_.enter_context(nc.sbuf_tensor([128, KT * 16], BF16))
        FST = st_.enter_context(nc.sbuf_tensor([64, 512], BF16))
        GFS = POOL[0:16, 4096:4096 + 63 * 32]
        SIGA = XFT[:, 0:256]
        PST = XFT[:, 256:512]
        PST2 = XFT[:, 512:768]
        IDB = st_.enter_context(nc.sbuf_tensor([128, 128], BF16))
        MSK = st_.enter_context(nc.sbuf_tensor([128, 4 * 32], BF16))
        SMALL = st_.enter_context(nc.sbuf_tensor([128, 256], BF16))
        RPM = SMALL[32:48, 0:256]
        RPD = SMALL[0:16, 0:128]
        BINS = st_.enter_context(nc.sbuf_tensor([128, KT], F32))
        BCS = st_.enter_context(nc.sbuf_tensor([128, KT], F32))
        BC2 = st_.enter_context(nc.sbuf_tensor([128, KT], F32))
        DPS = st_.enter_context(nc.sbuf_tensor([128, KT], F32))
        RB1S = st_.enter_context(nc.sbuf_tensor([64, 1], F32))
        B2G = st_.enter_context(nc.sbuf_tensor([128, 32], F32))
        GATS = st_.enter_context(nc.sbuf_tensor([128, 16], F32))
        NEGI = st_.enter_context(nc.sbuf_tensor([128, 16], F32))
        ZERI = st_.enter_context(nc.sbuf_tensor([128, 16], F32))
        ONER = st_.enter_context(nc.sbuf_tensor([1, 128], F32))
        ONEC = st_.enter_context(nc.sbuf_tensor([128, 1], BF16))
        WCWE = POOL[:, 6144:6144 + D]
        WCWS = POOL[0:16, 2048:2048 + D]
        WTMP = POOL[:, 0:D]
        TCO = st_.enter_context(nc.sbuf_tensor([128, 2 * TT], F32))
        s_te = st_.enter_context(nc.semaphore("s_te"))
        s_ve = st_.enter_context(nc.semaphore("s_ve"))
        s_se = st_.enter_context(nc.semaphore("s_se"))
        s_sy = st_.enter_context(nc.semaphore("s_sy"))
        s_gp = st_.enter_context(nc.semaphore("s_gp"))
        s_sh = st_.enter_context(nc.semaphore("s_sh"))
        PB0 = st_.enter_context(nc.psum_tensor([128, 512], F32))
        PB1 = st_.enter_context(nc.psum_tensor([128, 512], F32))
        PB2 = st_.enter_context(nc.psum_tensor([128, 512], F32))
        PB3 = st_.enter_context(nc.psum_tensor([128, 512], F32))
        PB4 = st_.enter_context(nc.psum_tensor([128, 512], F32))
        PB5 = st_.enter_context(nc.psum_tensor([128, 512], F32))
        PB6 = st_.enter_context(nc.psum_tensor([128, 512], F32))
        PB7 = st_.enter_context(nc.psum_tensor([128, 1024], BF16))
        block = st_.enter_context(nc.Block())
        PB = [PB0, PB1, PB2, PB3, PB4, PB5, PB6, PB7]
        sems = dict(tensor=s_te, vector=s_ve, scalar=s_se, sync=s_sy, gpsimd=s_gp)

        def xtv(t=LH):
            return XT[:].rearrange("p (k t) -> p k t", t=t)

        def xpv(t=LH):
            return XPB[:].rearrange("p (k t) -> p k t", t=t)

        def f_lds(h):
            h.dma_start(out=SIGA.rearrange("p (k s) -> p k s", s=16),
                        in_=aA.transpose([1, 0, 2])).then_inc(s_sy, 16)
            h.dma_start(out=IDB[:], in_=identb[:]).then_inc(s_sy, 16)
            h.dma_start(out=MSK[:].rearrange("p (t j) -> p t j", j=32),
                        in_=maskt.transpose([1, 0, 2])).then_inc(s_sy, 16)
            h.dma_start(out=BINS[:], in_=binp[:]).then_inc(s_sy, 16)
            h.dma_start(out=BCS[:], in_=bcmb[0]).then_inc(s_sy, 16)
            h.dma_start(out=BC2[:], in_=bcmb[1]).then_inc(s_sy, 16)
            h.dma_start(out=DPS[:], in_=dpar[:]).then_inc(s_sy, 16)
            h.dma_start(out=RB1S[:], in_=rb1v[:]).then_inc(s_sy, 16)
            h.dma_start(out=B2G[:], in_=rb2rep[:]).then_inc(s_sy, 16)
            h.dma_start(out=GATS[:], in_=gaterep[:]).then_inc(s_sy, 16)
            return None
        e_lds = E("sync", f_lds, ndma=10)

        def f_ldc(h):
            h.dma_start(out=AMB[:].rearrange("p (k e) -> p k e", e=256), in_=amoe.transpose([1, 0, 2])).then_inc(s_gp, 16)
            h.dma_start(out=BMB[:].rearrange("p (q o) -> p q o", o=D), in_=bmoe.transpose([1, 0, 2])).then_inc(s_gp, 16)
            h.dma_start(out=RW1B[:].rearrange("p (k e) -> p k e", e=64), in_=rw1.transpose([1, 0, 2])).then_inc(s_gp, 16)
            h.dma_start(out=RW2B, in_=rw2[:]).then_inc(s_gp, 16)
            h.dma_start(out=BWB[:].rearrange("p (k s) -> p k s", s=16), in_=bwd.transpose([1, 0, 2])).then_inc(s_gp, 16)
            h.dma_start(out=CWB[:].rearrange("p (k s) -> p k s", s=16), in_=cwd.transpose([1, 0, 2])).then_inc(s_gp, 16)
            h.dma_start(out=RPM, in_=repmoe[:]).then_inc(s_gp, 16)
            h.dma_start(out=RPD, in_=repd[:]).then_inc(s_gp, 16)
            return None
        e_ldc = E("gpsimd", f_ldc, ndma=8)

        def f_const(h):
            h.memset(NEGI[:], NEG)
            h.memset(CONSTS[:, 0:1], EPS)
            h.memset(CONSTS[:, 1:2], 1.0)
            h.memset(CONSTS[:, 2:3], NEG)
            h.memset(ZERI[:], 0.0)
            h.memset(ONER[:], 1.0)
            h.memset(ONEC[:], 1.0)
            h.tensor_tensor(out=BCS[:], in0=BCS[:], in1=BC2[:], op=AluOpType.add)
            return h.tensor_tensor(out=B2G[:, 0:16], in0=B2G[:, 0:16], in1=GATS[:],
                                   op=AluOpType.add)
        e_const = E("vector", f_const, deps=[e_lds, e_ldc])

        def f_siga(h):
            return h.activation(out=SIGA, in_=SIGA, func=AF.Sigmoid)
        e_siga = E("scalar", f_siga, deps=[e_lds])

        brv5 = bitrev_perm(32)[:32]

        def f_pow(h):
            h.memset(PST, 1.0)
            h.drain()
            last = None
            for p in range(64):
                if p < 32:
                    off = (p // 8) * 128 + (p % 8) * 16
                    ov = PWY[:].rearrange("q (k c s) -> q k c s", c=32, s=16)[:, :, off // 16]
                else:
                    ov = PWF.rearrange("q (k s r) -> q k s r", s=16, r=32)[:, :, :, p - 32]
                last = h.tensor_copy(out=ov, in_=PST.rearrange("q (k s) -> q k s", s=16))
                if p < 63:
                    last = h.tensor_tensor(out=PST, in0=PST, in1=SIGA, op=AluOpType.mult)
            h.reciprocal(out=PST2, in_=SIGA)
            h.memset(PST, 1.0)
            h.drain()
            for i in range(32):
                slot = int(np.where(brv5 == i)[0][0])
                ov = PWI.rearrange("q (k s r) -> q k s r", s=16, r=32)[:, :, :, slot]
                last = h.tensor_copy(out=ov, in_=PST.rearrange("q (k s) -> q k s", s=16))
                if i < 31:
                    last = h.tensor_tensor(out=PST, in0=PST, in1=PST2, op=AluOpType.mult)
            return last
        e_pow = E("vector", f_pow, deps=[e_siga])

        def f_powst(h):
            h.dma_start(out=pwfscr[:], in_=PWF).then_inc(s_sy, 16)
            h.dma_start(out=pwiscr[:], in_=PWI).then_inc(s_sy, 16)
            return None
        e_powst = E("sync", f_powst, deps=[e_pow], ndma=2)

        ew_prev = None
        e_wcw_mm = None
        for kt in range(KT):
            def f_wl(h, kt=kt):
                h.dma_start(out=WTMP, in_=wout[kt]).then_inc(s_gp, 16)
                return None
            ew_l = E("gpsimd", f_wl, deps=[ew_prev, e_powst])

            def f_wcw(h, kt=kt):
                last = None
                cw_sl = CWB[:].rearrange("p (k s) -> p k s", s=16)[:, kt]
                for nt in range(4):
                    last = h.matmul(PB[nt][0:16, :], cw_sl, WTMP[:, 512 * nt:512 * nt + 512],
                                    start=(kt == 0), stop=(kt == KT - 1))
                return last
            e_wcw_mm = E("tensor", f_wcw, deps=[ew_l, e_ldc])

            def f_wd(h, kt=kt):
                return h.activation(out=WTMP, in_=WTMP, func=AF.Identity,
                                    scale=DPS[:, kt:kt + 1])
            e_wd = E("scalar", f_wd, deps=[e_wcw_mm, e_lds])

            def f_wds(h, kt=kt):
                h.dma_start(out=woutdb[kt], in_=WTMP).then_inc(s_gp, 16)
                return None
            ew_prev = E("gpsimd", f_wds, deps=[e_wd])

        for kt in range(KT):
            def f_wl2(h, kt=kt):
                h.dma_start(out=WTMP, in_=wffn[kt]).then_inc(s_gp, 16)
                return None
            ew_l2 = E("gpsimd", f_wl2, deps=[ew_prev])

            def f_wadd(h, kt=kt):
                return h.tensor_tensor(out=WTMP[:, 128 * kt:128 * kt + 128],
                                       in0=WTMP[:, 128 * kt:128 * kt + 128],
                                       in1=IDB[:], op=AluOpType.add)
            e_wa = E("vector", f_wadd, deps=[ew_l2, e_lds])

            def f_ws2(h, kt=kt):
                h.dma_start(out=wffnb[kt], in_=WTMP).then_inc(s_gp, 16)
                return None
            ew_prev = E("gpsimd", f_ws2, deps=[e_wa])

            def f_wl3(h, kt=kt):
                h.dma_start(out=WTMP, in_=win[kt]).then_inc(s_gp, 16)
                return None
            ew_l3 = E("gpsimd", f_wl3, deps=[ew_prev])

            def f_ws3(h, kt=kt):
                h.dma_start(out=winb[kt], in_=WTMP).then_inc(s_gp, 16)
                return None
            ew_prev = E("gpsimd", f_ws3, deps=[ew_l3])
        e_wprep_done = ew_prev

        def f_wcwev(h):
            last = None
            for nt in range(4):
                last = h.activation(out=WCWS[:, 512 * nt:512 * nt + 512],
                                    in_=PB[nt][0:16, :], func=AF.Copy)
            return last
        e_wcwev = E("scalar", f_wcwev, deps=[e_wcw_mm])

        def f_wcwx(h):
            last = None
            for nt in range(4):
                last = h.matmul(PB7.bitcast(F32)[:, 0:512] if False else PB[4][:, :],
                                RPD[:, :], WCWS[:, 512 * nt:512 * nt + 512],
                                start=True, stop=True)
                last = h.matmul(PB[4][:, :], RPD[:, :], WCWS[:, 512 * nt:512 * nt + 512],
                                start=True, stop=True) if False else last
            return last
        e_wcwx_list = []
        for nt in range(4):
            def f_x1(h, nt=nt):
                return h.matmul(PB[4][:, :], RPD[:, :], WCWS[:, 512 * nt:512 * nt + 512],
                                start=True, stop=True)
            e_x1 = E("tensor", f_x1, deps=[e_wcwev, e_ldc] + ([e_wcwx_list[-1]] if e_wcwx_list else []))

            def f_x2(h, nt=nt):
                return h.activation(out=WCWE[:, 512 * nt:512 * nt + 512], in_=PB[4][:, :],
                                    func=AF.Copy)
            e_x2 = E("scalar", f_x2, deps=[e_x1])
            e_wcwx_list.append(e_x2)
        def f_wcwst(h):
            h.dma_start(out=wcwescr[:], in_=WCWE).then_inc(s_sy, 16)
            return None
        e_wcwst = E("sync", f_wcwst, deps=[e_wcwx_list[-1]])
        e_wcwe_done = e_wcwst

        bank_free = {i: e_wcwe_done for i in range(8)}
        tco_free = {i: None for i in range(2)}

        part_evt = {}
        DBG_TI = [0]
        SHCUM = [0]
        pool_guard = []
        e_shift_done = {}
        e_zbrv_done = {}
        e_cmm_last = {}
        e_zs_written = {}
        e_dst_last = {}
        e_fstore = {}
        last_scan_evt = None
        prev_xn_evt = None
        prev_st = None
        prev_xx = None
        omm_last = None
        cst_last = None
        h1_reader_last = None

        for b in range(b_run):
            e_zs_written[b] = []
            for g in range(NH):
                t0g = g * LH
                for tt in range(LH // TT):
                    t0 = t0g + tt * TT
                    e_xn_list = []
                    prev_repev = None
                    for st in range(4):
                        t0s = t0 + st * 128
                        slot = 0

                        def f_xl(h, b=b, t0s=t0s, slot=slot):
                            h.dma_start(out=XFT[:, 2048 * slot:2048 * slot + 2048]
                                        .rearrange("p (k t) -> p k t", t=128),
                                        in_=xin[b, :, :, t0s:t0s + 128].transpose([1, 0, 2])).then_inc(s_sy, 16)
                            return None
                        e_xl = E("sync", f_xl, deps=[prev_xn_evt, last_scan_evt, e_powst, prev_xx])

                        def f_xbl(h, b=b, t0s=t0s):
                            h.dma_start(out=XB2[:].rearrange("p (k t) -> p k t", t=128),
                                        in_=xin[b, :, :, t0s:t0s + 128].transpose([1, 0, 2])).then_inc(s_gp, 16)
                            return None
                        e_xbl = E("gpsimd", f_xbl, deps=[prev_st, last_scan_evt] + pool_guard)

                        def f_xx(h, slot=slot):
                            return h.activation(out=XXB[:], in_=XFT[:, 2048 * slot:2048 * slot + 2048],
                                                func=AF.Square)
                        e_xx = E("scalar", f_xx, deps=[e_xl, prev_st])
                        prev_xx = e_xx

                        def f_st(h, st=st):
                            last = None
                            for kt in range(KT):
                                last = h.matmul(PB[6][0:1, 0:128], ONEC[:, :],
                                                XB2[:, 128 * kt:128 * kt + 128],
                                                start=(kt == 0), stop=(kt == KT - 1))
                            for kt in range(KT):
                                last = h.matmul(PB[6][32:33, 0:128], ONEC[:, :],
                                                XXB[:, 128 * kt:128 * kt + 128],
                                                start=(kt == 0), stop=(kt == KT - 1))
                            return last
                        e_st = E("tensor", f_st, deps=[e_xbl, e_xx, e_const, bank_free[6]])
                        prev_st = e_st

                        def f_ms(h):
                            h.tensor_scalar(out=MROW[0:1, 0:128], in0=PB[6][0:1, 0:128],
                                            scalar1=1.0 / D, scalar2=None, op0=AluOpType.mult)
                            h.tensor_scalar(out=MROW[0:1, 128:256], in0=PB[6][32:33, 0:128],
                                            scalar1=1.0 / D, scalar2=None, op0=AluOpType.mult)
                            h.tensor_tensor(out=MROW[0:1, 256:384], in0=MROW[0:1, 0:128],
                                            in1=MROW[0:1, 0:128], op=AluOpType.mult)
                            return h.tensor_tensor(out=MROW[0:1, 128:256], in0=MROW[0:1, 128:256],
                                                   in1=MROW[0:1, 256:384], op=AluOpType.subtract)
                        e_ms = E("vector", f_ms, deps=[e_st])
                        bank_free[6] = e_ms

                        def f_is(h):
                            h.activation(out=MROW[0:1, 384:512], in_=MROW[0:1, 128:256],
                                         func=AF.Ln, bias=CONSTS[0:1, 0:1])
                            h.drain()
                            return h.activation(out=SROW[0:1, 0:128], in_=MROW[0:1, 384:512],
                                                func=AF.Exp, scale=-0.5)
                        e_is = E("scalar", f_is, deps=[e_ms, e_const])

                        def f_mis(h):
                            return h.tensor_tensor(out=SROW[0:1, 128:256], in0=MROW[0:1, 0:128],
                                                   in1=SROW[0:1, 0:128], op=AluOpType.mult)
                        e_mis = E("vector", f_mis, deps=[e_is])

                        def f_rep(h):
                            h.matmul(PB[5][:, 0:128], ONER[:, :], SROW[0:1, 0:128],
                                     start=True, stop=True)
                            return h.matmul(PB[5][:, 128:256], ONER[:, :], SROW[0:1, 128:256],
                                            start=True, stop=True)
                        e_rep = E("tensor", f_rep, deps=[e_mis, bank_free[5]])

                        def f_repev(h):
                            h.activation(out=ISR[:, 0:128], in_=PB[5][:, 0:128], func=AF.Copy)
                            return h.activation(out=MISR[:, 0:128], in_=PB[5][:, 128:256],
                                                func=AF.Copy)
                        e_repev = E("scalar", f_repev, deps=[e_rep, prev_repev])
                        bank_free[5] = e_repev
                        prev_repev = e_repev

                        def f_xn(h, tt=tt, st=st, slot=slot):
                            xf = XFT[:, 2048 * slot:2048 * slot + 2048]
                            xf3 = xf.rearrange("p (k t) -> p k t", t=128)
                            isv = ISR[:, 0:128].unsqueeze(1).broadcast_to([128, KT, 128])
                            misv = MISR[:, 0:128].unsqueeze(1).broadcast_to([128, KT, 128])
                            h.tensor_tensor(out=xf3, in0=xf3, in1=isv, op=AluOpType.mult)
                            ov = xtv()[:, :, tt * TT + st * 128: tt * TT + st * 128 + 128]
                            return h.tensor_tensor(out=ov, in0=xf3, in1=misv,
                                                   op=AluOpType.subtract)
                        e_xn = E("vector", f_xn, deps=[e_repev])
                        prev_xn_evt = e_xn
                        e_xn_list.append(e_xn)
                    e_xn_done = e_xn_list[-1]

                    e_xp_done = None
                    wq_dep = omm_last
                    for eq in range(4):
                        def f_wql(h, eq=eq):
                            h.dma_start(out=WQ1.rearrange("p (k e) -> p k e", e=512),
                                        in_=winb[:, :, 512 * eq:512 * eq + 512].transpose([1, 0, 2])).then_inc(s_sy, 16)
                            return None
                        e_wql = E("sync", f_wql, deps=[wq_dep, e_wprep_done, e_wcwe_done] + pool_guard)
                        for et4 in range(4):
                            et = eq * 4 + et4
                            pbk = et4 % 2

                            def f_pmm(h, et4=et4, pbk=pbk, tt=tt):
                                last = None
                                for kt in range(KT):
                                    wsl = WQ1.rearrange("p (k e) -> p k e", e=512)[:, kt, 128 * et4:128 * et4 + 128]
                                    xsl = xtv()[:, kt, tt * TT: tt * TT + TT]
                                    last = h.matmul(PB[pbk][:, :], wsl, xsl,
                                                    start=(kt == 0), stop=(kt == KT - 1))
                                return last
                            e_pmm = E("tensor", f_pmm, deps=[e_wql, e_xn_done, bank_free[pbk]])

                            def f_pev(h, et=et, pbk=pbk, tt=tt):
                                ov = xpv()[:, et, tt * TT:tt * TT + TT]
                                return h.activation(out=ov, in_=PB[pbk][:, :], func=AF.Identity,
                                                    bias=BINS[:, et:et + 1])
                            e_pev = E("scalar", f_pev, deps=[e_pmm])
                            bank_free[pbk] = e_pev
                            e_xp_done = e_pev
                            wq_dep = e_pmm

                    def f_ymm(h, tt=tt):
                        last = None
                        for kt in range(KT):
                            xsl = xpv()[:, kt, tt * TT:tt * TT + TT]
                            for pt in range(4):
                                wv = PWY[:, 512 * kt + 128 * pt:512 * kt + 128 * pt + 128]
                                last = h.matmul(PB[pt][:, :], wv, xsl,
                                                start=(kt == 0), stop=(kt == KT - 1))
                            bsl = BWB[:].rearrange("p (k s) -> p k s", s=16)[:, kt]
                            last = h.matmul(PB[4][0:16, :], bsl, xsl,
                                            start=(kt == 0), stop=(kt == KT - 1))
                        return last
                    e_ymm = E("tensor", f_ymm, deps=[e_xp_done, e_pow,
                                                     bank_free[0], bank_free[1], bank_free[2],
                                                     bank_free[3], bank_free[4]])

                    def f_zev(h, b=b, t0=t0):
                        last = None
                        for pt in range(4):
                            for hi in range(8):
                                j4, j3, j2 = hi >> 2, (hi >> 1) & 1, hi & 1
                                inv = PB[pt][:, :].rearrange("p (c jh j1 j0) -> p c jh j1 j0",
                                                             jh=8, j1=2, j0=2)[:, :, hi]
                                mv = MSK[:, 32 * pt:32 * pt + 32].rearrange(
                                    "p (jh j1 j0) -> p jh j1 j0", jh=8, j1=2, j0=2)[:, hi]
                                mv = mv.unsqueeze(1).broadcast_to([128, 16, 2, 2])
                                zt = ZS[:, L * pt + t0: L * pt + t0 + TT]
                                ov = zt.rearrange("p (c j0 j1 jh) -> p c j0 j1 jh",
                                                  j0=2, j1=2, jh=8)
                                ov = ov[:, :, :, :, 4 * j2 + 2 * j3 + j4].transpose([0, 1, 3, 2])
                                last = h.tensor_tensor(out=ov, in0=inv, in1=mv, op=AluOpType.mult)
                        return last
                    e_zev = E("vector", f_zev, deps=[e_ymm, e_lds, e_shift_done.get(b - 1)])
                    for pt in range(4):
                        bank_free[pt] = e_zev
                    e_zs_written[b].append(e_zev)

                    def f_dev(h, t0=t0):
                        h.activation(out=DTMP[:, :], in_=PB[4][0:16, :], func=AF.Exp)
                        return h.activation(out=DSB[:, t0:t0 + TT], in_=DTMP[:, :],
                                            func=AF.Ln, bias=1.0)
                    e_dev = E("scalar", f_dev, deps=[e_ymm, e_dst_last.get(b - 1), e_const])

                    def f_devS(h, t0=t0):
                        last = None
                        for hi in range(8):
                            j4, j3, j2 = hi >> 2, (hi >> 1) & 1, hi & 1
                            inv = DSB[:, t0:t0 + TT].rearrange("p (c jh j1 j0) -> p c jh j1 j0",
                                                               jh=8, j1=2, j0=2)[:, :, hi]
                            ov = DSS[:, t0:t0 + TT].rearrange("p (c j0 j1 jh) -> p c j0 j1 jh",
                                                              j0=2, j1=2, jh=8)
                            ov = ov[:, :, :, :, 4 * j2 + 2 * j3 + j4].transpose([0, 1, 3, 2])
                            last = h.activation(out=ov, in_=inv, func=AF.Copy)
                        return last
                    e_devS = E("scalar", f_devS, deps=[e_dev])
                    bank_free[4] = e_devS

                    def f_dst(h, b=b, t0=t0):
                        h.dma_start(out=dscrB[b, :, t0:t0 + TT], in_=DSB[:, t0:t0 + TT]).then_inc(s_sy, 16)
                        return None
                    e_dst = E("sync", f_dst, deps=[e_devS])
                    e_dst_last[b] = e_dst

                    def f_h1(h, tt=tt):
                        last = None
                        for kt in range(KT):
                            wsl = RW1B[:].rearrange("p (k e) -> p k e", e=64)[:, kt]
                            xsl = xtv()[:, kt, tt * TT:tt * TT + TT]
                            last = h.matmul(PB[4][0:64, :], wsl, xsl,
                                            start=(kt == 0), stop=(kt == KT - 1))
                        return last
                    e_h1 = E("tensor", f_h1, deps=[e_xn_done, bank_free[4], e_ldc])

                    def f_h1b(h):
                        return h.activation(out=H1F[:, 0:TT], in_=PB[4][0:64, :],
                                            func=AF.Identity, bias=RB1S[:, 0:1])
                    e_h1b = E("scalar", f_h1b, deps=[e_h1])
                    bank_free[4] = e_h1b

                    def f_h1e(h):
                        return h.activation(out=H1F[:, TT:2 * TT], in_=H1F[:, 0:TT],
                                            func=AF.Exp, scale=-1.0)
                    e_h1e = E("scalar", f_h1e, deps=[e_h1b])

                    def f_h1v(h, t0=t0):
                        ex = H1F[:, TT:2 * TT]
                        h.tensor_scalar(out=ex, in0=ex, scalar1=1.0, scalar2=None,
                                        op0=AluOpType.add)
                        h.drain()
                        h.reciprocal(out=ex, in_=ex)
                        h.drain()
                        return h.tensor_tensor(out=H1B[:, t0:t0 + TT], in0=H1F[:, 0:TT],
                                               in1=ex, op=AluOpType.mult)
                    e_h1v = E("vector", f_h1v, deps=[e_h1e, h1_reader_last])

                    LGS = TOP5[:, 0:64]
                    SCR = TOP5[:, 64:128]
                    LSB = TOP5[:, 128:192]
                    ETP = TOP5[:, 192:256]
                    MC = TOP5[:, 256:264]

                    def f_lgmm(h, t0=t0):
                        last = None
                        for st in range(4):
                            last = h.matmul(PB[4][:, 128 * st:128 * st + 32],
                                            H1B[:, t0 + 128 * st:t0 + 128 * st + 128],
                                            RW2B, start=True, stop=True)
                        return last
                    e_lgmm = E("tensor", f_lgmm, deps=[e_h1v, bank_free[4], e_ldc])
                    h1_reader_last = e_lgmm

                    def f_top1(h):
                        for st in range(4):
                            lg = LGS[:, 16 * st:16 * st + 16]
                            h.tensor_tensor(out=lg, in0=PB[4][:, 128 * st:128 * st + 16],
                                            in1=B2G[:, 0:16], op=AluOpType.add)
                            h.drain()
                            m1 = MC[:, 2 * st:2 * st + 1]
                            m2 = MC[:, 2 * st + 1:2 * st + 2]
                            h.reduce_max(out=m1, in_=lg, axis=AX)
                            h.drain()
                            eq = SCR[:, 16 * st:16 * st + 16]
                            h.tensor_scalar(out=eq, in0=lg, scalar1=m1, scalar2=None,
                                            op0=AluOpType.is_ge)
                            h.drain()
                            h.scalar_tensor_tensor(out=eq, in0=eq, scalar=CONSTS[:, 2:3],
                                                   in1=lg, op0=AluOpType.mult,
                                                   op1=AluOpType.add)
                            h.drain()
                            h.reduce_max(out=m2, in_=eq, axis=AX)
                            h.drain()
                            h.tensor_scalar(out=LSB[:, 16 * st:16 * st + 16], in0=lg,
                                            scalar1=m1, scalar2=None, op0=AluOpType.subtract)
                            h.tensor_scalar(out=GEMB[:, 16 * st:16 * st + 16], in0=lg,
                                            scalar1=m2, scalar2=None, op0=AluOpType.is_ge)
                        return None
                    e_top1 = E("vector", f_top1, deps=[e_lgmm, e_const])
                    bank_free[4] = e_top1

                    def f_texp(h):
                        return h.activation(out=ETP, in_=LSB, func=AF.Exp)
                    e_texp = E("scalar", f_texp, deps=[e_top1])

                    def f_top2(h):
                        h.tensor_tensor(out=ETP, in0=ETP, in1=GEMB[:], op=AluOpType.mult)
                        h.drain()
                        for st in range(4):
                            den = MC[:, 2 * st + 1:2 * st + 2]
                            h.reduce_sum(out=den, in_=ETP[:, 16 * st:16 * st + 16], axis=AX)
                            h.drain()
                            h.reciprocal(out=den, in_=den)
                            h.drain()
                            h.tensor_scalar(out=CTT[:, 16 * st:16 * st + 16],
                                            in0=ETP[:, 16 * st:16 * st + 16],
                                            scalar1=den, scalar2=None, op0=AluOpType.mult)
                        return None
                    e_top2 = E("vector", f_top2, deps=[e_texp])

                    e_cev_prev = cst_last
                    for st in range(4):
                        t0s = t0 + st * 128

                        def f_ctr(h, st=st):
                            return h.transpose(PB7[0:16, 0:128], CTT[:, 16 * st:16 * st + 16],
                                               IDB[:])
                        e_ctr = E("tensor", f_ctr, deps=[e_top2, e_cev_prev, e_lds])

                        def f_cev(h, t0s=t0s):
                            return h.activation(out=CFF[:, t0s:t0s + 128],
                                                in_=PB7[0:16, 0:128], func=AF.Copy)
                        e_cev = E("scalar", f_cev, deps=[e_ctr])
                        e_cev_prev = e_cev
                    e_cf_last = e_cev_prev

                    def f_crx(h, t0=t0):
                        h.matmul(PB[5][:, :], RPM[:, 0:128], CFF[:, t0:t0 + TT],
                                 start=True, stop=True)
                        return h.matmul(PB[6][:, :], RPM[:, 128:256], CFF[:, t0:t0 + TT],
                                        start=True, stop=True)
                    e_crx = E("tensor", f_crx, deps=[e_cf_last, bank_free[5], bank_free[6],
                                                     e_ldc])

                    def f_crev(h):
                        h.activation(out=CREP[:, 0:TT], in_=PB[5][:, :], func=AF.Copy)
                        return h.activation(out=CREP[:, TT:2 * TT], in_=PB[6][:, :],
                                            func=AF.Copy)
                    e_crev = E("scalar", f_crev, deps=[e_crx, omm_last])
                    bank_free[5] = e_crev
                    bank_free[6] = e_crev
                    cst_last = e_crev

                    def f_xam(h, tt=tt):
                        last = None
                        for kt in range(KT):
                            xsl = xtv()[:, kt, tt * TT:tt * TT + TT]
                            for q in range(2):
                                wsl = AMB[:].rearrange("p (k e) -> p k e", e=256)[:, kt, 128 * q:128 * q + 128]
                                last = h.matmul(PB[q][:, :], wsl, xsl,
                                                start=(kt == 0), stop=(kt == KT - 1))
                        return last
                    e_xam = E("tensor", f_xam, deps=[e_xn_done, bank_free[0], bank_free[1],
                                                     e_ldc])

                    def f_xaev(h):
                        h.activation(out=XAB[:, 0:TT], in_=PB[0][:, :], func=AF.Copy)
                        return h.activation(out=XAB[:, TT:2 * TT], in_=PB[1][:, :], func=AF.Copy)
                    e_xaev = E("scalar", f_xaev, deps=[e_xam, omm_last])
                    bank_free[0] = e_xaev
                    bank_free[1] = e_xaev

                    def f_wxa(h):
                        return h.tensor_tensor(out=XAB[:, :], in0=XAB[:, :], in1=CREP[:, :],
                                               op=AluOpType.mult)
                    e_wxa = E("vector", f_wxa, deps=[e_xaev, e_crev])

                    oq_dep = omm_last
                    for oq in range(8):
                        def f_oql(h, oq=oq):
                            h.dma_start(out=WQ2.rearrange("p (k e) -> p k e", e=256),
                                        in_=wffnb[:, :, 256 * oq:256 * oq + 256].transpose([1, 0, 2])).then_inc(s_sy, 16)
                            h.dma_start(out=WQ1[:, 0:KT * 256].rearrange("p (k e) -> p k e", e=256),
                                        in_=woutdb[:, :, 256 * oq:256 * oq + 256].transpose([1, 0, 2])).then_inc(s_sy, 16)
                            return None
                        e_oql = E("sync", f_oql, deps=[oq_dep, e_xp_done if oq == 0 else None] + pool_guard, ndma=2)
                        for ot2 in range(2):
                            ot = oq * 2 + ot2
                            pbk = 2 + ot2

                            def f_omm(h, ot=ot, ot2=ot2, pbk=pbk, tt=tt):
                                last = None
                                for kt in range(KT):
                                    wsl = WQ2.rearrange("p (k e) -> p k e", e=256)[:, kt, 128 * ot2:128 * ot2 + 128]
                                    xsl = xtv()[:, kt, tt * TT:tt * TT + TT]
                                    last = h.matmul(PB[pbk][:, :], wsl, xsl,
                                                    start=(kt == 0), stop=False)
                                for kt in range(KT):
                                    wsl = WQ1[:, 0:KT * 256].rearrange("p (k e) -> p k e", e=256)[:, kt, 128 * ot2:128 * ot2 + 128]
                                    xsl = xpv()[:, kt, tt * TT:tt * TT + TT]
                                    last = h.matmul(PB[pbk][:, :], wsl, xsl,
                                                    start=False, stop=False)
                                for q in range(2):
                                    wsl = BMB[:].rearrange("p (q o) -> p q o", o=D)[:, q, 128 * ot:128 * ot + 128]
                                    last = h.matmul(PB[pbk][:, :], wsl, XAB[:, TT * q:TT * q + TT],
                                                    start=False, stop=(q == 1))
                                return last
                            e_omm = E("tensor", f_omm, deps=[e_oql, e_wxa, e_xp_done,
                                                             bank_free[pbk]])
                            omm_last = e_omm
                            oq_dep = e_omm
                            tc = ot % 2

                            def f_oev(h, ot=ot, pbk=pbk, tc=tc):
                                return h.activation(out=TCO[:, TT * tc:TT * tc + TT],
                                                    in_=PB[pbk][:, :], func=AF.Identity,
                                                    bias=BCS[:, ot:ot + 1])
                            e_oev = E("scalar", f_oev, deps=[e_omm, tco_free[tc]])
                            bank_free[pbk] = e_oev

                            def f_oacc(h, b=b, ot=ot, t0=t0, tc=tc):
                                h.dma_start(out=TCO[:, TT * tc:TT * tc + TT],
                                            in_=xin[b, ot, :, t0:t0 + TT],
                                            accum_op=AluOpType.add).then_inc(s_gp, 16)
                                return None
                            e_oacc = E("gpsimd", f_oacc, deps=[e_oev])

                            def f_ost(h, b=b, ot=ot, t0=t0, tc=tc):
                                h.dma_start(out=part[b, ot, :, t0:t0 + TT],
                                            in_=TCO[:, TT * tc:TT * tc + TT]).then_inc(s_sy, 16)
                                return None
                            e_ost = E("sync", f_ost, deps=[e_oacc])
                            tco_free[tc] = e_ost
                            part_evt[(b, ot, t0)] = e_ost
                a_done = omm_last

                def f_pwl(h):
                    h.dma_start(out=PWF, in_=pwfscr[:]).then_inc(s_sy, 16)
                    h.dma_start(out=PWI, in_=pwiscr[:]).then_inc(s_sy, 16)
                    return None
                e_pwl = E("sync", f_pwl, deps=[a_done, e_powst], ndma=2)
                drep_dep = [None, None]
                e_sc_prev = None
                e_fmm_prev = None
                for s in range(S):
                    sl = 0

                    def f_dr(h, b=b, g=g, s=s, sl=sl):
                        src = dscrB[b, s:s + 1, g * LH:(g + 1) * LH].broadcast_to([128, LH])
                        h.dma_start(out=DREP[:, 0:LH], in_=src).then_inc(s_gp, 16)
                        return None
                    e_dr = E("gpsimd", f_dr, deps=[e_dst_last[b], drep_dep[sl]])

                    def f_sc(h, s=s, sl=sl):
                        drv = DREP[:, 0:LH].unsqueeze(1).broadcast_to([128, KT, LH])
                        h.tensor_tensor(out=xtv(), in0=xpv(), in1=drv, op=AluOpType.mult)
                        piv = PWI.rearrange("p (k s r) -> p k s r", s=16, r=32)[:, :, s]
                        piv = piv.unsqueeze(2).broadcast_to([128, KT, NKH, 32])
                        xcv = XT[:].rearrange("p (k c r) -> p k c r", c=NKH, r=32)
                        h.tensor_tensor(out=xcv, in0=xcv, in1=piv, op=AluOpType.mult)
                        w = 16
                        for lev in range(5):
                            iv0 = xcv[:, :, :, 0:w]
                            iv1 = xcv[:, :, :, w:2 * w]
                            h.tensor_tensor(out=iv0, in0=iv0, in1=iv1, op=AluOpType.add)
                            w //= 2
                        sv = xcv[:, :, :, 0]
                        return h.tensor_copy(out=MSEQ.rearrange("p (k c) -> p k c", c=NKH),
                                             in_=sv)
                    e_sc = E("vector", f_sc, deps=[e_dr, a_done, e_sc_prev, e_pwl, e_fmm_prev])
                    drep_dep[sl] = e_sc

                    def f_sc2(h, g=g, s=s):
                        a32 = PWF.rearrange("p (k s r) -> p k s r", s=16, r=32)[:, :, s, 0]
                        av = a32.unsqueeze(2).broadcast_to([128, KT, NKH])
                        h.tensor_copy(out=ACZ.rearrange("p (k c) -> p k c", c=NKH), in_=av)
                        h.memset(ACZ.rearrange("p (k c) -> p k c", c=NKH)[:, :, 0:1], 0.0)
                        if g == 1:
                            cv = CARRY[:].rearrange("p (k s) -> p k s", s=16)[:, :, s]
                            h.tensor_tensor(out=TOP5[:, 0:KT], in0=a32, in1=cv,
                                            op=AluOpType.mult)
                            mv0 = MSEQ.rearrange("p (k c) -> p k c", c=NKH)[:, :, 0]
                            h.tensor_tensor(out=mv0, in0=mv0, in1=TOP5[:, 0:KT],
                                            op=AluOpType.add)
                        h.tensor_tensor_scan(out=MS2, data0=ACZ, data1=MSEQ,
                                             initial=0.0, op0=AluOpType.mult, op1=AluOpType.add)
                        if g == 0:
                            return h.tensor_copy(
                                out=CARRY[:].rearrange("p (k s) -> p k s", s=16)[:, :, s],
                                in_=MS2.rearrange("p (k c) -> p k c", c=NKH)[:, :, NKH - 1])
                        return None
                    e_sc2 = E("vector", f_sc2, deps=[e_sc, e_pwl])
                    e_sc_prev = e_sc2

                    def f_fmm(h, s=s):
                        last = None
                        for kt in range(KT):
                            msl = MS2.rearrange("p (k c) -> p k c", c=NKH)[:, kt]
                            pfv = PWF.rearrange("p (k s r) -> p k s r", s=16, r=32)[:, kt, s]
                            last = h.matmul(PB[4][0:32, 0:32], msl, pfv,
                                            start=(kt == 0), stop=(kt == KT - 1))
                        return last
                    e_fmm = E("tensor", f_fmm, deps=[e_sc2, bank_free[4], e_pwl])
                    e_fmm_prev = e_fmm

                    def f_fev(h, s=s, g=g):
                        return h.tensor_copy(out=FST[32 * g:32 * g + 32, 32 * s:32 * s + 32],
                                             in_=PB[4][0:32, 0:32])
                    e_fev = E("vector", f_fev, deps=[e_fmm, e_fstore.get(b - 1)])
                    bank_free[4] = e_fev
                last_scan_evt = e_sc_prev
                pool_guard = [e_sc_prev, e_fmm_prev]
                e_fev_done = e_fev

            e_zs0 = None
            if dbg and b == 0:
                dzs0_ap = nc.dram_tensor("dzs0", [128, 2 * L], BF16, kind="ExternalOutput").ap()

                def f_zs0(h):
                    h.dma_start(out=dzs0_ap[:, 0:L], in_=ZS[:, 0:L]).then_inc(s_sy, 16)
                    h.dma_start(out=dzs0_ap[:, L:2 * L], in_=ZS[:, L:2 * L]).then_inc(s_sy, 16)
                    return None
                e_zs0 = E("sync", f_zs0, deps=list(e_zs_written[b]), ndma=2)

            def f_fstore(h, b=b):
                h.dma_start(out=fscr[b], in_=FST[:, :]).then_inc(s_sy, 16)
                return None
            e_fst = E("sync", f_fstore, deps=[e_fev_done])
            e_fstore[b] = e_fst

            def f_fgath(h, b=b):
                src = fscr[b].rearrange("k (s j) -> s k j", s=16)[:, 0:63, :]
                h.dma_start(out=GFS.rearrange("s (k j) -> s k j", j=32), in_=src).then_inc(s_gp, 16)
                return None
            e_fgath = E("gpsimd", f_fgath, deps=[e_fst] + pool_guard)

            e_dx_list = []
            for nt in range(4):
                def f_dx(h, nt=nt):
                    return h.matmul(PB[5][:, :], RPD[:, :], DSS[:, 512 * nt:512 * nt + 512],
                                    start=True, stop=True)
                e_dx = E("tensor", f_dx, deps=[bank_free[5], e_zs_written[b][-1],
                                               (e_dx_list[-1] if e_dx_list else None)])

                def f_dxe(h, nt=nt):
                    return h.activation(out=DEXP[:, 512 * nt:512 * nt + 512], in_=PB[5][:, :],
                                        func=AF.Copy)
                e_dxe = E("scalar", f_dxe, deps=[e_dx])
                bank_free[5] = e_dxe
                e_dx_list.append(e_dxe)
            e_dexp_done = e_dx_list[-1]

            def f_zmul(h):
                last = None
                for pt in range(4):
                    last = h.tensor_tensor(out=ZS[:, L * pt:L * pt + L],
                                           in0=ZS[:, L * pt:L * pt + L],
                                           in1=DEXP[:, :], op=AluOpType.mult)
                return last
            e_zmul = E("vector", f_zmul, deps=[e_dexp_done, e_zs0] + e_zs_written[b])

            def f_zacc0(h):
                return h.memset(ZACC[:, :], 0.0)
            e_zacc0 = E("vector", f_zacc0, deps=[e_zmul, e_zbrv_done.get(b - 1)])

            shift_base = sch.cum["gpsimd"]

            def f_shift(h, base=shift_base):
                cum = base
                for pt in range(4):
                    for ph in range(8):
                        p = 8 * pt + ph
                        rows = slice(16 * ph, 16 * ph + 16)
                        if p == 0:
                            src = ZS[rows, 0:L]
                            dst = ZACC[rows, 0:L]
                        else:
                            src = ZS[rows, L * pt:L * pt + L - p]
                            dst = ZACC[rows, p:L]
                        h.dma_start(out=dst, in_=src,
                                    accum_op=AluOpType.add).then_inc(s_gp, 16)
                        cum += 16
                    h.wait_ge(s_gp, cum)
                return None
            e_shift = E("gpsimd", f_shift, deps=[e_zacc0], ndma=32)
            e_shift_done[b] = e_shift

            def f_gfadd(h):
                return h.tensor_tensor(out=ZACC[0:16, 32:32 + 63 * 32],
                                       in0=ZACC[0:16, 32:32 + 63 * 32],
                                       in1=GFS[:, :], op=AluOpType.add)
            e_gfadd = E("vector", f_gfadd, deps=[e_shift, e_fgath])
            pool_guard = pool_guard + [e_gfadd]

            def f_zbrv(h):
                last = None
                for hi in range(8):
                    j4, j3, j2 = hi >> 2, (hi >> 1) & 1, hi & 1
                    inv = ZACC[:].rearrange("p (c jh j1 j0) -> p c jh j1 j0",
                                            jh=8, j1=2, j0=2)[:, :, hi]
                    ov = ZACCB[:].rearrange("p (c j0 j1 jh) -> p c j0 j1 jh",
                                            j0=2, j1=2, jh=8)
                    ov = ov[:, :, :, :, 4 * j2 + 2 * j3 + j4].transpose([0, 1, 3, 2])
                    last = h.activation(out=ov, in_=inv, func=AF.Copy)
                return last
            e_zbrv = E("scalar", f_zbrv, deps=[e_gfadd, e_cmm_last.get(b - 1)])
            e_zbrv_done[b] = e_zbrv

            def f_wcwl(h):
                h.dma_start(out=WCWE, in_=wcwescr[:]).then_inc(s_sy, 16)
                return None
            e_wcwl = E("sync", f_wcwl, deps=[e_zbrv] + pool_guard)
            pool_guard = pool_guard + [e_wcwl]
            for tt in range(4):
                t0 = tt * TT
                for ot in range(KT):
                    pbk = 2 + (ot % 2)
                    tc = ot % 2

                    def f_cmm(h, ot=ot, t0=t0, pbk=pbk):
                        return h.matmul(PB[pbk][:, :], WCWE[:, 128 * ot:128 * ot + 128],
                                        ZACCB[:, t0:t0 + TT], start=True, stop=True)
                    e_cmm = E("tensor", f_cmm, deps=[e_zbrv, e_wcwl, bank_free[pbk]])
                    e_cmm_last[b] = e_cmm

                    def f_cev(h, pbk=pbk, tc=tc):
                        return h.activation(out=TCO[:, TT * tc:TT * tc + TT],
                                            in_=PB[pbk][:, :], func=AF.Copy)
                    e_cev = E("scalar", f_cev, deps=[e_cmm, tco_free[tc]])
                    bank_free[pbk] = e_cev

                    def f_cacc(h, b=b, ot=ot, t0=t0, tc=tc):
                        h.dma_start(out=TCO[:, TT * tc:TT * tc + TT],
                                    in_=part[b, ot, :, t0:t0 + TT],
                                    accum_op=AluOpType.add).then_inc(s_gp, 16)
                        return None
                    e_cacc = E("gpsimd", f_cacc, deps=[e_cev, part_evt[(b, ot, t0)]])

                    def f_cst(h, b=b, ot=ot, t0=t0, tc=tc):
                        h.dma_start(out=yout[b, ot, :, t0:t0 + TT],
                                    in_=TCO[:, TT * tc:TT * tc + TT]).then_inc(s_sy, 16)
                        return None
                    e_cst = E("sync", f_cst, deps=[e_cacc])
                    tco_free[tc] = e_cst
                    final_store = e_cst
            pool_guard = pool_guard + [e_cmm_last[b]]

        if dbg:
            ddump = {}
            for nm, buf in [("dxpb", XPB), ("dxt", XT), ("dzs", ZS), ("dza", ZACC),
                            ("dzb", ZACCB), ("ddexp", DEXP), ("dcar", CARRY),
                            ("dms2", MS2), ("dfst", FST), ("dgfs", GFS),
                            ("drow1", ROW1), ("dpwy", PWY), ("ddbgt", DBGT)]:
                shp = list(buf.shape) if hasattr(buf, "shape") else None
                dt_ = F32 if nm == "ddbgt" else BF16
                ddump[nm] = nc.dram_tensor(nm, shp, dt_, kind="ExternalOutput").ap()

            def f_dump(h):
                for nm, buf in [("dxpb", XPB), ("dxt", XT), ("dzs", ZS), ("dza", ZACC),
                                ("dzb", ZACCB), ("ddexp", DEXP), ("dcar", CARRY),
                                ("dms2", MS2), ("dfst", FST), ("dgfs", GFS),
                                ("drow1", ROW1), ("dpwy", PWY), ("ddbgt", DBGT)]:
                    h.dma_start(out=ddump[nm][:], in_=(buf[:] if hasattr(buf, 'ap_type') or str(type(buf)).find('Handle') >= 0 else buf)).then_inc(s_sy, 16)
                return None
            E("sync", f_dump, deps=[final_store, e_cmm_last.get(b_run - 1)], ndma=13)

        def f_fin(h):
            return None
        E("gpsimd", f_fin, deps=[final_store], ndma=0)

        sch.emit(block, sems)

    return nc


def kernel(**inputs):
    L0 = {k[3:]: np.asarray(v) for k, v in inputs.items() if k.startswith("L0_")}
    x = np.ascontiguousarray(L0["attn_x"])
    perm = bitrev_perm(L)
    import ml_dtypes

    def lhsT(w):
        return np.ascontiguousarray(w.T.reshape(KT, 128, -1))

    brv5 = bitrev_perm(32)[:32]
    mask_h = np.zeros((4, 128, 32), dtype=np.float32)
    for t in range(4):
        for ph in range(8):
            p = 8 * t + ph
            mask_h[t, 16 * ph:16 * ph + 16, :] = ((brv5 + p) < 32)[None, :]
    repmoe_h = np.zeros((16, 256), dtype=np.float32)
    for e in range(16):
        repmoe_h[e, 16 * e:16 * e + 16] = 1.0
    repd_h = np.zeros((16, 128), dtype=np.float32)
    for ph in range(8):
        repd_h[:, 16 * ph:16 * ph + 16] = np.eye(16)

    shared = dict(
        win=lhsT(L0["attn_proj_in_w"]),
        wffn=lhsT(L0["ffn_W"]),
        wout=lhsT(L0["attn_proj_out_w"]),
        amoe=np.ascontiguousarray(L0["ffn_A"].reshape(256, D).T.reshape(KT, 128, 256)),
        bmoe=np.ascontiguousarray(L0["ffn_B"].transpose(0, 2, 1).reshape(256, D)
                                  .reshape(2, 128, D)),
        rw1=lhsT(L0["ffn_rw1"]),
        rw2=np.ascontiguousarray(L0["ffn_rw2"].T),
        bw=lhsT(L0["attn_B_w"]),
        cw=np.ascontiguousarray(L0["attn_C_w"].reshape(KT, 128, 16)),
        aA=np.ascontiguousarray(L0["attn_A"].reshape(KT, 128, 16)),
        dpar=np.ascontiguousarray(L0["attn_D_param"].reshape(KT, 128).T),
        binp=np.ascontiguousarray(L0["attn_proj_in_b"].reshape(KT, 128).T),
        bcmb=np.ascontiguousarray(np.stack([L0["ffn_bias"].reshape(KT, 128).T,
                                            L0["attn_proj_out_b"].reshape(KT, 128).T], 0)),
        rb1v=np.ascontiguousarray(L0["ffn_rb1"].reshape(64, 1)),
        rb2rep=np.ascontiguousarray(np.broadcast_to(L0["ffn_rb2"][None, :], (128, 32))),
        gaterep=np.ascontiguousarray(np.broadcast_to(L0["ffn_gates"][None, :], (128, 16))),
        identb=np.eye(128, dtype=ml_dtypes.bfloat16),
        maskt=mask_h.astype(ml_dtypes.bfloat16),
        repmoe=repmoe_h.astype(ml_dtypes.bfloat16),
        repd=repd_h.astype(ml_dtypes.bfloat16),
    )

    in_maps = []
    for c in range(8):
        xs = x[2 * c:2 * c + 2][:, perm, :]
        xs = np.ascontiguousarray(xs.transpose(0, 2, 1).reshape(B, KT, 128, L))
        m = dict(shared)
        m["xin"] = xs
        in_maps.append(m)

    nc = build_nc()
    res = run_bass_kernel_spmd(nc, in_maps, core_ids=list(range(8)))
    outs = []
    for c in range(8):
        y = np.asarray(res.results[c]["yout"]).reshape(B, D, L).transpose(0, 2, 1)
        outs.append(y[:, perm, :])
    return np.concatenate(outs, 0).astype(np.float32)


# version 11
# speedup vs baseline: 1.1536x; 1.1431x over previous
import numpy as np
import concourse.bass as bass
import concourse.mybir as mybir
from concourse.bass_utils import run_bass_kernel_spmd
from concourse.alu_op_type import AluOpType

F32 = mybir.dt.float32
BF16 = mybir.dt.bfloat16
AF = mybir.ActivationFunctionType
AX = mybir.AxisListType.X

B, L, D, S, C = 2, 2048, 2048, 16, 32
KT, NH = 16, 2
LH = L // NH
NKH = LH // C
TT = 512
NEG = -1.0e30
EPS = 1e-5


def bitrev_perm(n):
    idx = np.arange(n)
    c, pos = idx // 32, idx % 32
    r = np.zeros_like(pos)
    for b in range(5):
        r |= ((pos >> b) & 1) << (4 - b)
    return c * 32 + r


class Sched:
    ENGS = ("tensor", "vector", "scalar", "sync", "gpsimd")

    def __init__(self):
        self.events = {e: [] for e in self.ENGS}
        self.cum = {e: 0 for e in self.ENGS}

    def ev(self, eng, fn, deps=(), ndma=1):
        n = len(self.events[eng])
        dd = [d for d in deps
              if d is not None and (d[0] != eng or eng in ("sync", "gpsimd"))]
        if eng in ("sync", "gpsimd"):
            self.cum[eng] += 16 * ndma
        else:
            self.cum[eng] += 1
        self.events[eng].append((fn, dd, self.cum[eng], ndma))
        return (eng, n)

    def emit(self, block, sems):
        after = {e: [ev[2] for ev in self.events[e]] for e in self.ENGS}
        for eng in self.ENGS:
            evs = self.events[eng]
            is_dma = eng in ("sync", "gpsimd")

            def make(eng=eng, evs=evs, is_dma=is_dma):
                def body(h):
                    sem = sems[eng]
                    cnt = [0]
                    if is_dma:
                        orig = h.dma_start

                        def counted(*a, **k):
                            cnt[0] += 1
                            return orig(*a, **k)
                        h.dma_start = counted
                    for i, (fn, dd, _v, ndma) in enumerate(evs):
                        best = {}
                        for (deng, didx) in dd:
                            v = after[deng][didx]
                            if best.get(deng, -1) < v:
                                best[deng] = v
                        for deng, v in best.items():
                            h.wait_ge(sems[deng], v)
                        cnt[0] = 0
                        r = fn(h)
                        if is_dma:
                            assert cnt[0] == ndma, (
                                f"{eng} event {i}: declared ndma={ndma} actual={cnt[0]}")
                        else:
                            h.drain().then_inc(sem, 1)
                return body
            getattr(block, eng)(make())


def build_nc(dbg=False, b_run=B):
    nc = bass.Bass("TRN2", target_bir_lowering=False)

    def din(name, shp, dt=F32):
        return nc.dram_tensor(name, shp, dt, kind="ExternalInput").ap()

    def dint(name, shp, dt=F32):
        return nc.dram_tensor(name, shp, dt, kind="Internal").ap()

    dkind = "ExternalOutput" if dbg else "Internal"
    xin = din("xin", [B, KT, 128, L])
    win = din("win", [KT, 128, D])
    wffn = din("wffn", [KT, 128, D])
    wout = din("wout", [KT, 128, D])
    amoe = din("amoe", [KT, 128, 256])
    bmoe = din("bmoe", [2, 128, D])
    rw1 = din("rw1", [KT, 128, 64])
    rw2 = din("rw2", [64, 32])
    bwd = din("bw", [KT, 128, 16])
    cwd = din("cw", [KT, 128, 16])
    aA = din("aA", [KT, 128, 16])
    dpar = din("dpar", [128, KT])
    binp = din("binp", [128, KT])
    bcmb = din("bcmb", [2, 128, KT])
    rb1v = din("rb1v", [64, 1])
    rb2rep = din("rb2rep", [128, 32])
    gaterep = din("gaterep", [128, 16])
    identb = din("identb", [128, 128], BF16)
    maskt = din("maskt", [4, 128, 32], BF16)
    repmoe = din("repmoe", [16, 256], BF16)
    repd = din("repd", [16, 128], BF16)

    winb = nc.dram_tensor("winb", [KT, 128, D], BF16, kind=dkind).ap()
    wffnb = nc.dram_tensor("wffnb", [KT, 128, D], BF16, kind=dkind).ap()
    woutdb = nc.dram_tensor("woutdb", [KT, 128, D], BF16, kind=dkind).ap()
    wcwescr = dint("wcwescr", [128, D], BF16)
    dscrB = nc.dram_tensor("dscrB", [B, 16, L], BF16, kind=dkind).ap()
    fscr = nc.dram_tensor("fscr", [B, 64, 512], BF16, kind=dkind).ap()
    part = nc.dram_tensor("part", [B, KT, 128, L], F32, kind=dkind).ap()
    yout = nc.dram_tensor("yout", [B, KT, 128, L], F32, kind="ExternalOutput").ap()

    sch = Sched()
    E = sch.ev

    import contextlib
    st_ = contextlib.ExitStack()
    with st_:
        XT = st_.enter_context(nc.sbuf_tensor([128, KT * LH], BF16))
        XPB = st_.enter_context(nc.sbuf_tensor([128, KT * LH], BF16))
        PWY = st_.enter_context(nc.sbuf_tensor([128, KT * S * 32], BF16))
        PWR = st_.enter_context(nc.sbuf_tensor([128, KT * S * 32], BF16))
        POOL = st_.enter_context(nc.sbuf_tensor([128, 8192], BF16))
        WQ1 = POOL[:, 0:4096]
        WQ2 = POOL[:, 4096:8192]
        WTMP = POOL[:, 0:D]
        WCWS = POOL[0:16, 2048:2048 + D]
        WCWE = POOL[:, 4096:4096 + D]
        SCR = st_.enter_context(nc.sbuf_tensor([128, 8192], BF16))
        GFS = SCR[0:16, 4096:4096 + 63 * 32]
        DSB = SCR[0:16, 6144:8192]
        AMB = st_.enter_context(nc.sbuf_tensor([128, KT * 256], BF16))
        BMB = st_.enter_context(nc.sbuf_tensor([128, 2 * D], BF16))
        RW1B = st_.enter_context(nc.sbuf_tensor([128, KT * 64], BF16))
        RW2T = st_.enter_context(nc.sbuf_tensor([128, 32], BF16))
        RW2B = RW2T[64:128, 0:32]
        BWB = st_.enter_context(nc.sbuf_tensor([128, KT * 16], BF16))
        CWB = st_.enter_context(nc.sbuf_tensor([128, KT * 16], BF16))
        XXB = st_.enter_context(nc.sbuf_tensor([128, 2048], BF16))
        ISR = st_.enter_context(nc.sbuf_tensor([128, 128], F32))
        MISR = st_.enter_context(nc.sbuf_tensor([128, 128], F32))
        ZS = st_.enter_context(nc.sbuf_tensor([128, 4 * TT], BF16))
        ZACC = st_.enter_context(nc.sbuf_tensor([128, L], BF16))
        ZACCB = st_.enter_context(nc.sbuf_tensor([128, L], BF16))
        DEXPT = st_.enter_context(nc.sbuf_tensor([128, TT], BF16))
        DREP = st_.enter_context(nc.sbuf_tensor([128, LH], BF16))
        ROW1 = st_.enter_context(nc.sbuf_tensor([128, TT], BF16))
        DSS = ROW1[0:16, :]
        CFF = ROW1[32:48, :]
        H1B = ROW1[64:128, :]
        ROW2 = st_.enter_context(nc.sbuf_tensor([128, 1536], F32))
        DTMP = ROW2[0:16, 0:512]
        MROW = ROW2[0:1, 512:1024]
        SROW = ROW2[0:1, 1024:1280]
        SIGA = ROW2[:, 1280:1536]
        H1F = ROW2[64:128, 0:1024]
        XAB = st_.enter_context(nc.sbuf_tensor([128, 2 * TT], BF16))
        CTT = st_.enter_context(nc.sbuf_tensor([128, 64], BF16))
        GEMB = st_.enter_context(nc.sbuf_tensor([128, 64], F32))
        CREP = st_.enter_context(nc.sbuf_tensor([128, 2 * TT], BF16))
        TOP5 = st_.enter_context(nc.sbuf_tensor([128, 264], F32))
        CONSTS = st_.enter_context(nc.sbuf_tensor([128, 4], F32))
        DBGT = st_.enter_context(nc.sbuf_tensor([128, 48], F32))
        XB2 = st_.enter_context(nc.sbuf_tensor([128, 1024], BF16))
        MSEQ = XB2[:, 0:512]
        ACZ = XB2[:, 512:1024]
        MS2A = st_.enter_context(nc.sbuf_tensor([128, S * KT * NKH], BF16))
        CARRY = st_.enter_context(nc.sbuf_tensor([128, KT * 16], BF16))
        FST = st_.enter_context(nc.sbuf_tensor([64, 512], BF16))
        A32 = st_.enter_context(nc.sbuf_tensor([128, 256], BF16))
        SIGB = st_.enter_context(nc.sbuf_tensor([128, 256], BF16))
        IDB = st_.enter_context(nc.sbuf_tensor([128, 128], BF16))
        MSK = st_.enter_context(nc.sbuf_tensor([128, 4 * 32], BF16))
        SMALL = st_.enter_context(nc.sbuf_tensor([128, 256], BF16))
        RPM = SMALL[32:48, 0:256]
        RPD = SMALL[0:16, 0:128]
        BINS = st_.enter_context(nc.sbuf_tensor([128, KT], F32))
        BCS = st_.enter_context(nc.sbuf_tensor([128, KT], F32))
        BC2 = st_.enter_context(nc.sbuf_tensor([128, KT], F32))
        DPS = st_.enter_context(nc.sbuf_tensor([128, KT], F32))
        RB1S = st_.enter_context(nc.sbuf_tensor([64, 1], F32))
        B2G = st_.enter_context(nc.sbuf_tensor([128, 32], F32))
        GATS = st_.enter_context(nc.sbuf_tensor([128, 16], F32))
        NEGI = st_.enter_context(nc.sbuf_tensor([128, 16], F32))
        ZERI = st_.enter_context(nc.sbuf_tensor([128, 16], F32))
        ONER = st_.enter_context(nc.sbuf_tensor([1, 128], F32))
        ONEC = st_.enter_context(nc.sbuf_tensor([128, 1], BF16))
        TCO = st_.enter_context(nc.sbuf_tensor([128, 2 * TT], F32))
        PST = TCO[:, 0:256]
        s_te = st_.enter_context(nc.semaphore("s_te"))
        s_ve = st_.enter_context(nc.semaphore("s_ve"))
        s_se = st_.enter_context(nc.semaphore("s_se"))
        s_sy = st_.enter_context(nc.semaphore("s_sy"))
        s_gp = st_.enter_context(nc.semaphore("s_gp"))
        block = st_.enter_context(nc.Block())
        PB0 = st_.enter_context(nc.psum_tensor([128, 512], F32))
        PB1 = st_.enter_context(nc.psum_tensor([128, 512], F32))
        PB2 = st_.enter_context(nc.psum_tensor([128, 512], F32))
        PB3 = st_.enter_context(nc.psum_tensor([128, 512], F32))
        PB4 = st_.enter_context(nc.psum_tensor([128, 512], F32))
        PB5 = st_.enter_context(nc.psum_tensor([128, 512], F32))
        PB6 = st_.enter_context(nc.psum_tensor([128, 512], F32))
        PB7 = st_.enter_context(nc.psum_tensor([128, 1024], BF16))
        PB = [PB0, PB1, PB2, PB3, PB4, PB5, PB6, PB7]
        sems = dict(tensor=s_te, vector=s_ve, scalar=s_se, sync=s_sy, gpsimd=s_gp)

        def xtv(t=LH):
            return XT[:].rearrange("p (k t) -> p k t", t=t)

        def xpv(t=LH):
            return XPB[:].rearrange("p (k t) -> p k t", t=t)

        def f_lds(h):
            h.dma_start(out=SIGA.rearrange("p (k s) -> p k s", s=16),
                        in_=aA.transpose([1, 0, 2])).then_inc(s_sy, 16)
            h.dma_start(out=IDB[:], in_=identb[:]).then_inc(s_sy, 16)
            h.dma_start(out=MSK[:].rearrange("p (t j) -> p t j", j=32),
                        in_=maskt.transpose([1, 0, 2])).then_inc(s_sy, 16)
            h.dma_start(out=BINS[:], in_=binp[:]).then_inc(s_sy, 16)
            h.dma_start(out=BCS[:], in_=bcmb[0]).then_inc(s_sy, 16)
            h.dma_start(out=BC2[:], in_=bcmb[1]).then_inc(s_sy, 16)
            h.dma_start(out=DPS[:], in_=dpar[:]).then_inc(s_sy, 16)
            h.dma_start(out=RB1S[:], in_=rb1v[:]).then_inc(s_sy, 16)
            h.dma_start(out=B2G[:], in_=rb2rep[:]).then_inc(s_sy, 16)
            h.dma_start(out=GATS[:], in_=gaterep[:]).then_inc(s_sy, 16)
            return None
        e_lds = E("sync", f_lds, ndma=10)

        def f_ldc(h):
            h.dma_start(out=AMB[:].rearrange("p (k e) -> p k e", e=256), in_=amoe.transpose([1, 0, 2])).then_inc(s_gp, 16)
            h.dma_start(out=BMB[:].rearrange("p (q o) -> p q o", o=D), in_=bmoe.transpose([1, 0, 2])).then_inc(s_gp, 16)
            h.dma_start(out=RW1B[:].rearrange("p (k e) -> p k e", e=64), in_=rw1.transpose([1, 0, 2])).then_inc(s_gp, 16)
            h.dma_start(out=RW2B, in_=rw2[:]).then_inc(s_gp, 16)
            h.dma_start(out=BWB[:].rearrange("p (k s) -> p k s", s=16), in_=bwd.transpose([1, 0, 2])).then_inc(s_gp, 16)
            h.dma_start(out=CWB[:].rearrange("p (k s) -> p k s", s=16), in_=cwd.transpose([1, 0, 2])).then_inc(s_gp, 16)
            h.dma_start(out=RPM, in_=repmoe[:]).then_inc(s_gp, 16)
            h.dma_start(out=RPD, in_=repd[:]).then_inc(s_gp, 16)
            return None
        e_ldc = E("gpsimd", f_ldc, ndma=8)

        def f_const(h):
            h.memset(NEGI[:], NEG)
            h.memset(CONSTS[:, 0:1], EPS)
            h.memset(CONSTS[:, 1:2], 1.0)
            h.memset(CONSTS[:, 2:3], NEG)
            h.memset(ZERI[:], 0.0)
            h.memset(ONER[:], 1.0)
            h.memset(ONEC[:], 1.0)
            h.tensor_tensor(out=BCS[:], in0=BCS[:], in1=BC2[:], op=AluOpType.add)
            return h.tensor_tensor(out=B2G[:, 0:16], in0=B2G[:, 0:16], in1=GATS[:],
                                   op=AluOpType.add)
        e_const = E("vector", f_const, deps=[e_lds, e_ldc])

        def f_siga(h):
            return h.activation(out=SIGA, in_=SIGA, func=AF.Sigmoid)
        e_siga = E("scalar", f_siga, deps=[e_lds])

        brv5 = bitrev_perm(32)[:32]

        def f_pow(h):
            pst3 = PST.rearrange("q (k s) -> q k s", s=16)
            h.memset(PST, 1.0)
            h.drain()
            last = None
            for p in range(32):
                ov = PWY[:].rearrange("q (k c s) -> q k c s", c=32, s=16)[:, :, p]
                last = h.tensor_copy(out=ov, in_=pst3)
                last = h.tensor_tensor(out=PST, in0=PST, in1=SIGA, op=AluOpType.mult)
            h.tensor_copy(out=A32.rearrange("q (k s) -> q k s", s=16), in_=pst3)
            h.tensor_copy(out=SIGB.rearrange("q (k s) -> q k s", s=16),
                          in_=SIGA.rearrange("q (k s) -> q k s", s=16))
            h.memset(PST, 1.0)
            h.drain()
            for t in range(32):
                i = 31 - t
                slot = int(np.where(brv5 == i)[0][0])
                ov = PWR[:].rearrange("q (k s r) -> q k s r", s=16, r=32)[:, :, :, slot]
                last = h.tensor_copy(out=ov, in_=pst3)
                if t < 31:
                    last = h.tensor_tensor(out=PST, in0=PST, in1=SIGA, op=AluOpType.mult)
            return last
        e_pow = E("vector", f_pow, deps=[e_siga, e_const])

        ew_prev = None
        e_wcw_mm = None
        for kt in range(KT):
            def f_wl(h, kt=kt):
                h.dma_start(out=WTMP, in_=wout[kt]).then_inc(s_gp, 16)
                return None
            ew_l = E("gpsimd", f_wl, deps=[ew_prev, e_ldc])

            def f_wcw(h, kt=kt):
                last = None
                cw_sl = CWB[:].rearrange("p (k s) -> p k s", s=16)[:, kt]
                for nt in range(4):
                    last = h.matmul(PB[nt][0:16, :], cw_sl, WTMP[:, 512 * nt:512 * nt + 512],
                                    start=(kt == 0), stop=(kt == KT - 1))
                return last
            e_wcw_mm = E("tensor", f_wcw, deps=[ew_l, e_ldc])

            def f_wd(h, kt=kt):
                return h.activation(out=WTMP, in_=WTMP, func=AF.Identity,
                                    scale=DPS[:, kt:kt + 1])
            e_wd = E("scalar", f_wd, deps=[e_wcw_mm, e_lds])

            def f_wds(h, kt=kt):
                h.dma_start(out=woutdb[kt], in_=WTMP).then_inc(s_gp, 16)
                return None
            ew_prev = E("gpsimd", f_wds, deps=[e_wd])

        for kt in range(KT):
            def f_wl2(h, kt=kt):
                h.dma_start(out=WTMP, in_=wffn[kt]).then_inc(s_gp, 16)
                return None
            ew_l2 = E("gpsimd", f_wl2, deps=[ew_prev])

            def f_wadd(h, kt=kt):
                return h.tensor_tensor(out=WTMP[:, 128 * kt:128 * kt + 128],
                                       in0=WTMP[:, 128 * kt:128 * kt + 128],
                                       in1=IDB[:], op=AluOpType.add)
            e_wa = E("vector", f_wadd, deps=[ew_l2, e_lds])

            def f_ws2(h, kt=kt):
                h.dma_start(out=wffnb[kt], in_=WTMP).then_inc(s_gp, 16)
                return None
            ew_prev = E("gpsimd", f_ws2, deps=[e_wa])

            def f_wl3(h, kt=kt):
                h.dma_start(out=WTMP, in_=win[kt]).then_inc(s_gp, 16)
                return None
            ew_l3 = E("gpsimd", f_wl3, deps=[ew_prev])

            def f_ws3(h, kt=kt):
                h.dma_start(out=winb[kt], in_=WTMP).then_inc(s_gp, 16)
                return None
            ew_prev = E("gpsimd", f_ws3, deps=[ew_l3])
        e_wprep_done = ew_prev

        def f_wcwev(h):
            last = None
            for nt in range(4):
                last = h.activation(out=WCWS[:, 512 * nt:512 * nt + 512],
                                    in_=PB[nt][0:16, :], func=AF.Copy)
            return last
        e_wcwev = E("scalar", f_wcwev, deps=[e_wcw_mm])

        e_wcwx_list = []
        for nt in range(4):
            def f_x1(h, nt=nt):
                return h.matmul(PB[4][:, :], RPD[:, :], WCWS[:, 512 * nt:512 * nt + 512],
                                start=True, stop=True)
            e_x1 = E("tensor", f_x1, deps=[e_wcwev, e_ldc] + ([e_wcwx_list[-1]] if e_wcwx_list else []))

            def f_x2(h, nt=nt):
                return h.activation(out=WCWE[:, 512 * nt:512 * nt + 512], in_=PB[4][:, :],
                                    func=AF.Copy)
            e_x2 = E("scalar", f_x2, deps=[e_x1])
            e_wcwx_list.append(e_x2)

        def f_wcwst(h):
            h.dma_start(out=wcwescr[:], in_=WCWE).then_inc(s_sy, 16)
            return None
        e_wcwst = E("sync", f_wcwst, deps=[e_wcwx_list[-1]])
        e_wcwe_done = e_wcwst

        bank_free = {i: e_wcwe_done for i in range(8)}
        tco_free = {0: e_pow, 1: e_pow}

        part_evt = {}
        pool_guard = []
        e_shift_prev = None
        e_zbrv_done = {}
        e_cmm_last = {}
        e_dst_last = {}
        e_fstore = {}
        e_gfadd_done = {}
        last_scan_evt = None
        prev_st = None
        prev_xx = None
        omm_last = None
        cst_last = None
        h1_reader_last = None
        final_store = None
        e_fev_last = None

        for b in range(b_run):
            def f_zacc0(h):
                return h.memset(ZACC[:, :], 0.0)
            e_zacc0 = E("vector", f_zacc0, deps=[e_zbrv_done.get(b - 1)])

            for g in range(NH):
                t0g = g * LH
                e_xn_done = None
                prev_repev = None
                for tt in range(LH // TT):
                    for st in range(4):
                        sl0 = tt * TT + st * 128
                        t0s = t0g + sl0

                        def f_xl(h, b=b, t0s=t0s, sl0=sl0):
                            h.dma_start(out=xtv()[:, :, sl0:sl0 + 128],
                                        in_=xin[b, :, :, t0s:t0s + 128].transpose([1, 0, 2])).then_inc(s_gp, 16)
                            return None
                        e_xl = E("gpsimd", f_xl, deps=[last_scan_evt, omm_last, prev_st])

                        def f_xx(h, sl0=sl0):
                            return h.activation(out=XXB[:].rearrange("p (k t) -> p k t", t=128),
                                                in_=xtv()[:, :, sl0:sl0 + 128], func=AF.Square)
                        e_xx = E("scalar", f_xx, deps=[e_xl, prev_st])
                        prev_xx = e_xx

                        def f_st(h, sl0=sl0):
                            last = None
                            for kt in range(KT):
                                last = h.matmul(PB[6][0:1, 0:128], ONEC[:, :],
                                                xtv()[:, kt, sl0:sl0 + 128],
                                                start=(kt == 0), stop=(kt == KT - 1))
                            for kt in range(KT):
                                last = h.matmul(PB[6][32:33, 0:128], ONEC[:, :],
                                                XXB[:, 128 * kt:128 * kt + 128],
                                                start=(kt == 0), stop=(kt == KT - 1))
                            return last
                        e_st = E("tensor", f_st, deps=[e_xl, e_xx, e_const, bank_free[6]])
                        prev_st = e_st

                        def f_ms(h):
                            h.tensor_scalar(out=MROW[0:1, 0:128], in0=PB[6][0:1, 0:128],
                                            scalar1=1.0 / D, scalar2=None, op0=AluOpType.mult)
                            h.tensor_scalar(out=MROW[0:1, 128:256], in0=PB[6][32:33, 0:128],
                                            scalar1=1.0 / D, scalar2=None, op0=AluOpType.mult)
                            h.tensor_tensor(out=MROW[0:1, 256:384], in0=MROW[0:1, 0:128],
                                            in1=MROW[0:1, 0:128], op=AluOpType.mult)
                            return h.tensor_tensor(out=MROW[0:1, 128:256], in0=MROW[0:1, 128:256],
                                                   in1=MROW[0:1, 256:384], op=AluOpType.subtract)
                        e_ms = E("vector", f_ms, deps=[e_st])
                        bank_free[6] = e_ms

                        def f_is(h):
                            h.activation(out=MROW[0:1, 384:512], in_=MROW[0:1, 128:256],
                                         func=AF.Ln, bias=CONSTS[0:1, 0:1])
                            h.drain()
                            return h.activation(out=SROW[0:1, 0:128], in_=MROW[0:1, 384:512],
                                                func=AF.Exp, scale=-0.5)
                        e_is = E("scalar", f_is, deps=[e_ms, e_const])

                        def f_mis(h):
                            return h.tensor_tensor(out=SROW[0:1, 128:256], in0=MROW[0:1, 0:128],
                                                   in1=SROW[0:1, 0:128], op=AluOpType.mult)
                        e_mis = E("vector", f_mis, deps=[e_is])

                        def f_rep(h):
                            h.matmul(PB[5][:, 0:128], ONER[:, :], SROW[0:1, 0:128],
                                     start=True, stop=True)
                            return h.matmul(PB[5][:, 128:256], ONER[:, :], SROW[0:1, 128:256],
                                            start=True, stop=True)
                        e_rep = E("tensor", f_rep, deps=[e_mis, bank_free[5]])

                        def f_repev(h):
                            h.activation(out=ISR[:, 0:128], in_=PB[5][:, 0:128], func=AF.Copy)
                            return h.activation(out=MISR[:, 0:128], in_=PB[5][:, 128:256],
                                                func=AF.Copy)
                        e_repev = E("scalar", f_repev, deps=[e_rep, prev_repev])
                        bank_free[5] = e_repev
                        prev_repev = e_repev

                        def f_xn(h, sl0=sl0):
                            xs = xtv()[:, :, sl0:sl0 + 128]
                            isv = ISR[:, 0:128].unsqueeze(1).broadcast_to([128, KT, 128])
                            misv = MISR[:, 0:128].unsqueeze(1).broadcast_to([128, KT, 128])
                            h.tensor_tensor(out=xs, in0=xs, in1=isv, op=AluOpType.mult)
                            return h.tensor_tensor(out=xs, in0=xs, in1=misv,
                                                   op=AluOpType.subtract)
                        e_xn = E("vector", f_xn, deps=[e_repev, e_st])
                        e_xn_done = e_xn

                e_xp_done = None
                wq_dep = omm_last
                for tt in range(LH // TT):
                    for eq in range(8):
                        def f_wql(h, eq=eq):
                            h.dma_start(out=WQ1.rearrange("p (k e) -> p k e", e=256),
                                        in_=winb[:, :, 256 * eq:256 * eq + 256].transpose([1, 0, 2])).then_inc(s_sy, 16)
                            return None
                        e_wql = E("sync", f_wql, deps=[wq_dep, e_wprep_done, e_wcwe_done] + pool_guard)
                        for et4 in range(2):
                            et = eq * 2 + et4
                            pbk = et4 % 2

                            def f_pmm(h, et4=et4, pbk=pbk, tt=tt):
                                last = None
                                for kt in range(KT):
                                    wsl = WQ1.rearrange("p (k e) -> p k e", e=256)[:, kt, 128 * et4:128 * et4 + 128]
                                    xsl = xtv()[:, kt, tt * TT: tt * TT + TT]
                                    last = h.matmul(PB[pbk][:, :], wsl, xsl,
                                                    start=(kt == 0), stop=(kt == KT - 1))
                                return last
                            e_pmm = E("tensor", f_pmm, deps=[e_wql, e_xn_done, bank_free[pbk],
                                                             last_scan_evt])

                            def f_pev(h, et=et, pbk=pbk, tt=tt):
                                ov = xpv()[:, et, tt * TT:tt * TT + TT]
                                return h.activation(out=ov, in_=PB[pbk][:, :], func=AF.Identity,
                                                    bias=BINS[:, et:et + 1])
                            e_pev = E("scalar", f_pev, deps=[e_pmm])
                            bank_free[pbk] = e_pev
                            e_xp_done = e_pev
                            wq_dep = e_pmm

                e_wxa = {}
                e_top2_t = {}
                e_zev_t = {}
                for tt in range(LH // TT):
                    t0 = t0g + tt * TT

                    def f_ymm(h, tt=tt):
                        last = None
                        for kt in range(KT):
                            xsl = xpv()[:, kt, tt * TT:tt * TT + TT]
                            for pt in range(4):
                                wv = PWY[:, 512 * kt + 128 * pt:512 * kt + 128 * pt + 128]
                                last = h.matmul(PB[pt][:, :], wv, xsl,
                                                start=(kt == 0), stop=(kt == KT - 1))
                            bsl = BWB[:].rearrange("p (k s) -> p k s", s=16)[:, kt]
                            last = h.matmul(PB[4][0:16, :], bsl, xsl,
                                            start=(kt == 0), stop=(kt == KT - 1))
                        return last
                    e_ymm = E("tensor", f_ymm, deps=[e_xp_done, e_pow,
                                                     bank_free[0], bank_free[1], bank_free[2],
                                                     bank_free[3], bank_free[4]])

                    def f_zev(h, tt=tt):
                        last = None
                        for pt in range(4):
                            for hi in range(8):
                                j4, j3, j2 = hi >> 2, (hi >> 1) & 1, hi & 1
                                inv = PB[pt][:, :].rearrange("p (c jh j1 j0) -> p c jh j1 j0",
                                                             jh=8, j1=2, j0=2)[:, :, hi]
                                mv = MSK[:, 32 * pt:32 * pt + 32].rearrange(
                                    "p (jh j1 j0) -> p jh j1 j0", jh=8, j1=2, j0=2)[:, hi]
                                mv = mv.unsqueeze(1).broadcast_to([128, 16, 2, 2])
                                zt = ZS[:, TT * pt: TT * pt + TT]
                                ov = zt.rearrange("p (c j0 j1 jh) -> p c j0 j1 jh",
                                                  j0=2, j1=2, jh=8)
                                ov = ov[:, :, :, :, 4 * j2 + 2 * j3 + j4].transpose([0, 1, 3, 2])
                                last = h.tensor_tensor(out=ov, in0=inv, in1=mv, op=AluOpType.mult)
                        return last
                    e_zev = E("vector", f_zev, deps=[e_ymm, e_lds, e_shift_prev])
                    e_zev_t[tt] = e_zev
                    for pt in range(4):
                        bank_free[pt] = e_zev

                    def f_dev(h, t0=t0):
                        h.activation(out=DTMP[:, :], in_=PB[4][0:16, :], func=AF.Exp)
                        h.drain()
                        return h.activation(out=DSB[:, t0:t0 + TT], in_=DTMP[:, :],
                                            func=AF.Ln, bias=1.0)
                    e_dev = E("scalar", f_dev, deps=[e_ymm, e_const, last_scan_evt])

                    def f_devS(h, t0=t0):
                        last = None
                        for hi in range(8):
                            j4, j3, j2 = hi >> 2, (hi >> 1) & 1, hi & 1
                            inv = DSB[:, t0:t0 + TT].rearrange("p (c jh j1 j0) -> p c jh j1 j0",
                                                               jh=8, j1=2, j0=2)[:, :, hi]
                            ov = DSS[:, 0:TT].rearrange("p (c j0 j1 jh) -> p c j0 j1 jh",
                                                        j0=2, j1=2, jh=8)
                            ov = ov[:, :, :, :, 4 * j2 + 2 * j3 + j4].transpose([0, 1, 3, 2])
                            last = h.activation(out=ov, in_=inv, func=AF.Copy)
                        return last
                    e_devS = E("scalar", f_devS, deps=[e_dev])
                    bank_free[4] = e_devS

                    def f_dst(h, b=b, t0=t0):
                        h.dma_start(out=dscrB[b, :, t0:t0 + TT], in_=DSB[:, t0:t0 + TT]).then_inc(s_sy, 16)
                        return None
                    e_dst = E("sync", f_dst, deps=[e_devS])
                    e_dst_last[b] = e_dst

                    def f_dx(h):
                        return h.matmul(PB[5][:, :], RPD[:, :], DSS[:, 0:TT],
                                        start=True, stop=True)
                    e_dx = E("tensor", f_dx, deps=[bank_free[5], e_devS])

                    def f_dxe(h):
                        return h.activation(out=DEXPT[:, :], in_=PB[5][:, :], func=AF.Copy)
                    e_dxe = E("scalar", f_dxe, deps=[e_dx])
                    bank_free[5] = e_dxe

                    def f_zmul(h):
                        last = None
                        for pt in range(4):
                            last = h.tensor_tensor(out=ZS[:, TT * pt:TT * pt + TT],
                                                   in0=ZS[:, TT * pt:TT * pt + TT],
                                                   in1=DEXPT[:, :], op=AluOpType.mult)
                        return last
                    e_zmul = E("vector", f_zmul, deps=[e_zev, e_dxe])

                    shift_base = sch.cum["gpsimd"]

                    def f_shift(h, base=shift_base, t0=t0):
                        cum = base
                        for pt in range(4):
                            for ph in range(8):
                                p = 8 * pt + ph
                                rows = slice(16 * ph, 16 * ph + 16)
                                n = min(TT, L - t0 - p)
                                src = ZS[rows, TT * pt:TT * pt + n]
                                dst = ZACC[rows, t0 + p:t0 + p + n]
                                h.dma_start(out=dst, in_=src,
                                            accum_op=AluOpType.add).then_inc(s_gp, 16)
                                cum += 16
                            h.wait_ge(s_gp, cum)
                        return None
                    e_shift = E("gpsimd", f_shift, deps=[e_zmul, e_zacc0, e_shift_prev], ndma=32)
                    e_shift_prev = e_shift

                    def f_h1(h, tt=tt):
                        last = None
                        for kt in range(KT):
                            wsl = RW1B[:].rearrange("p (k e) -> p k e", e=64)[:, kt]
                            xsl = xtv()[:, kt, tt * TT:tt * TT + TT]
                            last = h.matmul(PB[4][0:64, :], wsl, xsl,
                                            start=(kt == 0), stop=(kt == KT - 1))
                        return last
                    e_h1 = E("tensor", f_h1, deps=[e_xn_done, bank_free[4], e_ldc])

                    def f_h1b(h):
                        return h.activation(out=H1F[:, 0:TT], in_=PB[4][0:64, :],
                                            func=AF.Identity, bias=RB1S[:, 0:1])
                    e_h1b = E("scalar", f_h1b, deps=[e_h1])
                    bank_free[4] = e_h1b

                    def f_h1e(h):
                        return h.activation(out=H1F[:, TT:2 * TT], in_=H1F[:, 0:TT],
                                            func=AF.Exp, scale=-1.0)
                    e_h1e = E("scalar", f_h1e, deps=[e_h1b])

                    def f_h1v(h, t0=t0):
                        ex = H1F[:, TT:2 * TT]
                        h.tensor_scalar(out=ex, in0=ex, scalar1=1.0, scalar2=None,
                                        op0=AluOpType.add)
                        h.drain()
                        h.reciprocal(out=ex, in_=ex)
                        h.drain()
                        return h.tensor_tensor(out=H1B[:, 0:TT], in0=H1F[:, 0:TT],
                                               in1=ex, op=AluOpType.mult)
                    e_h1v = E("vector", f_h1v, deps=[e_h1e, h1_reader_last])

                    LGS = TOP5[:, 0:64]
                    SCT = TOP5[:, 64:128]
                    LSB = TOP5[:, 128:192]
                    ETP = TOP5[:, 192:256]
                    MC = TOP5[:, 256:264]

                    def f_lgmm(h):
                        last = None
                        for st in range(4):
                            last = h.matmul(PB[4][:, 128 * st:128 * st + 32],
                                            H1B[:, 128 * st:128 * st + 128],
                                            RW2B, start=True, stop=True)
                        return last
                    e_lgmm = E("tensor", f_lgmm, deps=[e_h1v, bank_free[4], e_ldc])
                    h1_reader_last = e_lgmm

                    def f_top1(h):
                        for st in range(4):
                            lg = LGS[:, 16 * st:16 * st + 16]
                            h.tensor_tensor(out=lg, in0=PB[4][:, 128 * st:128 * st + 16],
                                            in1=B2G[:, 0:16], op=AluOpType.add)
                            h.drain()
                            m1 = MC[:, 2 * st:2 * st + 1]
                            m2 = MC[:, 2 * st + 1:2 * st + 2]
                            h.reduce_max(out=m1, in_=lg, axis=AX)
                            h.drain()
                            eq = SCT[:, 16 * st:16 * st + 16]
                            h.tensor_scalar(out=eq, in0=lg, scalar1=m1, scalar2=None,
                                            op0=AluOpType.is_ge)
                            h.drain()
                            h.scalar_tensor_tensor(out=eq, in0=eq, scalar=CONSTS[:, 2:3],
                                                   in1=lg, op0=AluOpType.mult,
                                                   op1=AluOpType.add)
                            h.drain()
                            h.reduce_max(out=m2, in_=eq, axis=AX)
                            h.drain()
                            h.tensor_scalar(out=LSB[:, 16 * st:16 * st + 16], in0=lg,
                                            scalar1=m1, scalar2=None, op0=AluOpType.subtract)
                            h.tensor_scalar(out=GEMB[:, 16 * st:16 * st + 16], in0=lg,
                                            scalar1=m2, scalar2=None, op0=AluOpType.is_ge)
                        return None
                    e_top1 = E("vector", f_top1, deps=[e_lgmm, e_const])
                    bank_free[4] = e_top1

                    def f_texp(h):
                        return h.activation(out=ETP, in_=LSB, func=AF.Exp)
                    e_texp = E("scalar", f_texp, deps=[e_top1])

                    def f_top2(h):
                        h.tensor_tensor(out=ETP, in0=ETP, in1=GEMB[:], op=AluOpType.mult)
                        h.drain()
                        for st in range(4):
                            den = MC[:, 2 * st + 1:2 * st + 2]
                            h.reduce_sum(out=den, in_=ETP[:, 16 * st:16 * st + 16], axis=AX)
                            h.drain()
                            h.reciprocal(out=den, in_=den)
                            h.drain()
                            h.tensor_scalar(out=CTT[:, 16 * st:16 * st + 16],
                                            in0=ETP[:, 16 * st:16 * st + 16],
                                            scalar1=den, scalar2=None, op0=AluOpType.mult)
                        return None
                    e_top2_t[tt] = E("vector", f_top2, deps=[e_texp])

                for tt in range(LH // TT):
                    t0 = t0g + tt * TT

                    e_cev_prev = cst_last
                    for st in range(4):
                        def f_ctr(h, st=st, tt=tt):
                            return h.transpose(PB7[0:16, 0:128], CTT[:, 16 * st:16 * st + 16],
                                               IDB[:])
                        e_ctr = E("tensor", f_ctr, deps=[e_top2_t[tt], e_cev_prev, e_lds])

                        def f_cev(h, st=st):
                            return h.activation(out=CFF[:, 128 * st:128 * st + 128],
                                                in_=PB7[0:16, 0:128], func=AF.Copy)
                        e_cev = E("scalar", f_cev, deps=[e_ctr, omm_last])
                        e_cev_prev = e_cev
                    e_cf_last = e_cev_prev

                    def f_crx(h):
                        h.matmul(PB[5][:, :], RPM[:, 0:128], CFF[:, 0:TT],
                                 start=True, stop=True)
                        return h.matmul(PB[6][:, :], RPM[:, 128:256], CFF[:, 0:TT],
                                        start=True, stop=True)
                    e_crx = E("tensor", f_crx, deps=[e_cf_last, bank_free[5], bank_free[6],
                                                     e_ldc])

                    def f_crev(h):
                        h.activation(out=CREP[:, 0:TT], in_=PB[5][:, :], func=AF.Copy)
                        return h.activation(out=CREP[:, TT:2 * TT], in_=PB[6][:, :],
                                            func=AF.Copy)
                    e_crev = E("scalar", f_crev, deps=[e_crx, omm_last])
                    bank_free[5] = e_crev
                    bank_free[6] = e_crev
                    cst_last = e_crev

                    def f_xam(h, tt=tt):
                        last = None
                        for kt in range(KT):
                            xsl = xtv()[:, kt, tt * TT:tt * TT + TT]
                            for q in range(2):
                                wsl = AMB[:].rearrange("p (k e) -> p k e", e=256)[:, kt, 128 * q:128 * q + 128]
                                last = h.matmul(PB[q][:, :], wsl, xsl,
                                                start=(kt == 0), stop=(kt == KT - 1))
                        return last
                    e_xam = E("tensor", f_xam, deps=[e_xn_done, bank_free[0], bank_free[1],
                                                     e_ldc])

                    def f_xaev(h):
                        h.activation(out=XAB[:, 0:TT], in_=PB[0][:, :], func=AF.Copy)
                        return h.activation(out=XAB[:, TT:2 * TT], in_=PB[1][:, :], func=AF.Copy)
                    e_xaev = E("scalar", f_xaev, deps=[e_xam, omm_last])
                    bank_free[0] = e_xaev
                    bank_free[1] = e_xaev

                    def f_wxa(h):
                        return h.tensor_tensor(out=XAB[:, :], in0=XAB[:, :], in1=CREP[:, :],
                                               op=AluOpType.mult)
                    e_wxa[tt] = E("vector", f_wxa, deps=[e_xaev, e_crev])

                    oq_dep = omm_last
                    for oq in range(8):
                        def f_oql(h, oq=oq):
                            h.dma_start(out=WQ2.rearrange("p (k e) -> p k e", e=256),
                                        in_=wffnb[:, :, 256 * oq:256 * oq + 256].transpose([1, 0, 2])).then_inc(s_sy, 16)
                            h.dma_start(out=WQ1[:, 0:KT * 256].rearrange("p (k e) -> p k e", e=256),
                                        in_=woutdb[:, :, 256 * oq:256 * oq + 256].transpose([1, 0, 2])).then_inc(s_sy, 16)
                            return None
                        e_oql = E("sync", f_oql, deps=[oq_dep, e_xp_done if oq == 0 else None] + pool_guard, ndma=2)
                        for ot2 in range(2):
                            ot = oq * 2 + ot2
                            pbk = 2 + ot2

                            def f_omm(h, ot=ot, ot2=ot2, pbk=pbk, tt=tt):
                                last = None
                                for kt in range(KT):
                                    wsl = WQ2.rearrange("p (k e) -> p k e", e=256)[:, kt, 128 * ot2:128 * ot2 + 128]
                                    xsl = xtv()[:, kt, tt * TT:tt * TT + TT]
                                    last = h.matmul(PB[pbk][:, :], wsl, xsl,
                                                    start=(kt == 0), stop=False)
                                for kt in range(KT):
                                    wsl = WQ1[:, 0:KT * 256].rearrange("p (k e) -> p k e", e=256)[:, kt, 128 * ot2:128 * ot2 + 128]
                                    xsl = xpv()[:, kt, tt * TT:tt * TT + TT]
                                    last = h.matmul(PB[pbk][:, :], wsl, xsl,
                                                    start=False, stop=False)
                                for q in range(2):
                                    wsl = BMB[:].rearrange("p (q o) -> p q o", o=D)[:, q, 128 * ot:128 * ot + 128]
                                    last = h.matmul(PB[pbk][:, :], wsl, XAB[:, TT * q:TT * q + TT],
                                                    start=False, stop=(q == 1))
                                return last
                            e_omm = E("tensor", f_omm, deps=[e_oql, e_wxa[tt], e_xp_done,
                                                             bank_free[pbk]])
                            omm_last = e_omm
                            oq_dep = e_omm
                            tc = ot % 2

                            def f_oev(h, ot=ot, pbk=pbk, tc=tc):
                                return h.activation(out=TCO[:, TT * tc:TT * tc + TT],
                                                    in_=PB[pbk][:, :], func=AF.Identity,
                                                    bias=BCS[:, ot:ot + 1])
                            e_oev = E("scalar", f_oev, deps=[e_omm, tco_free[tc]])
                            bank_free[pbk] = e_oev

                            def f_ost(h, b=b, ot=ot, t0=t0, tc=tc):
                                h.dma_start(out=part[b, ot, :, t0:t0 + TT],
                                            in_=TCO[:, TT * tc:TT * tc + TT]).then_inc(s_sy, 16)
                                return None
                            e_ost = E("sync", f_ost, deps=[e_oev])
                            tco_free[tc] = e_ost
                            part_evt[(b, ot, t0)] = e_ost
                a_done = omm_last

                e_sc_prev = None
                e_fmm = {}
                e_fevd = {}
                fmm_guard = [bank_free[4], bank_free[5]]
                for s in range(S):
                    def f_dr(h, b=b, g=g, s=s):
                        src = dscrB[b, s:s + 1, g * LH:(g + 1) * LH].broadcast_to([128, LH])
                        h.dma_start(out=DREP[:, 0:LH], in_=src).then_inc(s_gp, 16)
                        return None
                    e_dr = E("gpsimd", f_dr, deps=[e_dst_last[b], e_sc_prev])

                    def f_sc(h, s=s):
                        last = None
                        for kh in range(2):
                            k0 = kh * 8
                            drv = DREP[:, 0:LH].unsqueeze(1).broadcast_to([128, 8, LH])
                            xph = xpv()[:, k0:k0 + 8, :]
                            sc3 = SCR[:].rearrange("p (k t) -> p k t", t=LH)
                            h.tensor_tensor(out=sc3, in0=xph, in1=drv, op=AluOpType.mult)
                            prv = PWR[:].rearrange("p (k s r) -> p k s r", s=16, r=32)[:, k0:k0 + 8, s]
                            prv = prv.unsqueeze(2).broadcast_to([128, 8, NKH, 32])
                            xcv = SCR[:].rearrange("p (k c r) -> p k c r", c=NKH, r=32)
                            h.tensor_tensor(out=xcv, in0=xcv, in1=prv, op=AluOpType.mult)
                            w = 16
                            for lev in range(5):
                                iv0 = xcv[:, :, :, 0:w]
                                iv1 = xcv[:, :, :, w:2 * w]
                                h.tensor_tensor(out=iv0, in0=iv0, in1=iv1, op=AluOpType.add)
                                w //= 2
                            sv = xcv[:, :, :, 0]
                            avb = SIGB[:].rearrange("p (k s) -> p k s", s=16)[:, k0:k0 + 8, s]
                            avb = avb.unsqueeze(2).broadcast_to([128, 8, NKH])
                            mh = MSEQ.rearrange("p (k c) -> p k c", c=NKH)[:, k0:k0 + 8]
                            last = h.tensor_tensor(out=mh, in0=sv, in1=avb, op=AluOpType.mult)
                        return last
                    e_sc = E("vector", f_sc, deps=[e_dr, e_xp_done, e_sc_prev,
                                                   e_gfadd_done.get(b - 1) if (g == 0 and s == 0) else None,
                                                   e_pow])

                    def f_sc2(h, g=g, s=s):
                        a32 = A32[:].rearrange("p (k s) -> p k s", s=16)[:, :, s]
                        av = a32.unsqueeze(2).broadcast_to([128, KT, NKH])
                        h.tensor_copy(out=ACZ.rearrange("p (k c) -> p k c", c=NKH), in_=av)
                        h.memset(ACZ.rearrange("p (k c) -> p k c", c=NKH)[:, :, 0:1], 0.0)
                        h.drain()
                        if g == 1:
                            cv = CARRY[:].rearrange("p (k s) -> p k s", s=16)[:, :, s]
                            h.tensor_tensor(out=TOP5[:, 0:KT], in0=a32, in1=cv,
                                            op=AluOpType.mult)
                            h.drain()
                            mv0 = MSEQ.rearrange("p (k c) -> p k c", c=NKH)[:, :, 0]
                            h.tensor_tensor(out=mv0, in0=mv0, in1=TOP5[:, 0:KT],
                                            op=AluOpType.add)
                            h.drain()
                        ms2 = MS2A[:, 512 * s:512 * s + 512]
                        h.tensor_tensor_scan(out=ms2, data0=ACZ, data1=MSEQ,
                                             initial=0.0, op0=AluOpType.mult, op1=AluOpType.add)
                        h.drain()
                        if g == 0:
                            return h.tensor_copy(
                                out=CARRY[:].rearrange("p (k s) -> p k s", s=16)[:, :, s],
                                in_=ms2.rearrange("p (k c) -> p k c", c=NKH)[:, :, NKH - 1])
                        return None
                    e_sc2 = E("vector", f_sc2, deps=[e_sc])
                    e_sc_prev = e_sc2

                    q = s % 8
                    fbank = 4 if q < 4 else 5
                    fcol = 128 * (q % 4)

                    def f_fmm(h, s=s, fbank=fbank, fcol=fcol):
                        last = None
                        ms2 = MS2A[:, 512 * s:512 * s + 512]
                        for kt in range(KT):
                            msl = ms2.rearrange("p (k c) -> p k c", c=NKH)[:, kt]
                            pfv = PWY[:].rearrange("p (k c s) -> p k c s", c=32, s=16)[:, kt, :, s]
                            last = h.matmul(PB[fbank][0:32, fcol:fcol + 32], msl, pfv,
                                            start=(kt == 0), stop=(kt == KT - 1))
                        return last
                    fdeps = [e_sc2, e_pow] + (fmm_guard if s < 8 else [e_fevd[s - 8]])
                    e_fmm[s] = E("tensor", f_fmm, deps=fdeps)

                    def f_fev(h, s=s, g=g, fbank=fbank, fcol=fcol):
                        return h.activation(out=FST[32 * g:32 * g + 32, 32 * s:32 * s + 32],
                                            in_=PB[fbank][0:32, fcol:fcol + 32], func=AF.Copy)
                    e_fevd[s] = E("scalar", f_fev, deps=[e_fmm[s], e_fstore.get(b - 1)])
                last_scan_evt = e_sc_prev

                e_fev = e_fevd[S - 1]
                bank_free[4] = e_fev
                bank_free[5] = e_fev
                e_fev_last = e_fev

            def f_fstore(h, b=b):
                h.dma_start(out=fscr[b], in_=FST[:, :]).then_inc(s_sy, 16)
                return None
            e_fst = E("sync", f_fstore, deps=[e_fev_last])
            e_fstore[b] = e_fst

            def f_fgath(h, b=b):
                src = fscr[b].rearrange("k (s j) -> s k j", s=16)[:, 0:63, :]
                h.dma_start(out=GFS.rearrange("s (k j) -> s k j", j=32), in_=src).then_inc(s_gp, 16)
                return None
            e_fgath = E("gpsimd", f_fgath, deps=[e_fst, last_scan_evt])

            def f_gfadd(h):
                return h.tensor_tensor(out=ZACC[0:16, 32:32 + 63 * 32],
                                       in0=ZACC[0:16, 32:32 + 63 * 32],
                                       in1=GFS[:, :], op=AluOpType.add)
            e_gfadd = E("vector", f_gfadd, deps=[e_shift_prev, e_fgath])
            e_gfadd_done[b] = e_gfadd

            def f_zbrv(h):
                last = None
                for hi in range(8):
                    j4, j3, j2 = hi >> 2, (hi >> 1) & 1, hi & 1
                    inv = ZACC[:].rearrange("p (c jh j1 j0) -> p c jh j1 j0",
                                            jh=8, j1=2, j0=2)[:, :, hi]
                    ov = ZACCB[:].rearrange("p (c j0 j1 jh) -> p c j0 j1 jh",
                                            j0=2, j1=2, jh=8)
                    ov = ov[:, :, :, :, 4 * j2 + 2 * j3 + j4].transpose([0, 1, 3, 2])
                    last = h.activation(out=ov, in_=inv, func=AF.Copy)
                return last
            e_zbrv = E("scalar", f_zbrv, deps=[e_gfadd, e_cmm_last.get(b - 1)])
            e_zbrv_done[b] = e_zbrv

            def f_wcwl(h):
                h.dma_start(out=WCWE, in_=wcwescr[:]).then_inc(s_sy, 16)
                return None
            e_wcwl = E("sync", f_wcwl, deps=[e_zbrv, omm_last] + pool_guard)
            pool_guard = [e_wcwl]
            for tt4 in range(4):
                t0 = tt4 * TT
                for ot in range(KT):
                    pbk = 2 + (ot % 2)
                    tc = ot % 2

                    def f_cmm(h, ot=ot, t0=t0, pbk=pbk):
                        return h.matmul(PB[pbk][:, :], WCWE[:, 128 * ot:128 * ot + 128],
                                        ZACCB[:, t0:t0 + TT], start=True, stop=True)
                    e_cmm = E("tensor", f_cmm, deps=[e_zbrv, e_wcwl, bank_free[pbk]])
                    e_cmm_last[b] = e_cmm

                    def f_cev(h, pbk=pbk, tc=tc):
                        return h.activation(out=TCO[:, TT * tc:TT * tc + TT],
                                            in_=PB[pbk][:, :], func=AF.Copy)
                    e_cev = E("scalar", f_cev, deps=[e_cmm, tco_free[tc]])
                    bank_free[pbk] = e_cev

                    cacc_base = sch.cum["gpsimd"]

                    def f_cacc(h, b=b, ot=ot, t0=t0, tc=tc, base=cacc_base):
                        h.dma_start(out=TCO[:, TT * tc:TT * tc + TT],
                                    in_=part[b, ot, :, t0:t0 + TT],
                                    accum_op=AluOpType.add).then_inc(s_gp, 16)
                        h.wait_ge(s_gp, base + 16)
                        h.dma_start(out=TCO[:, TT * tc:TT * tc + TT],
                                    in_=xin[b, ot, :, t0:t0 + TT],
                                    accum_op=AluOpType.add).then_inc(s_gp, 16)
                        return None
                    e_cacc = E("gpsimd", f_cacc, deps=[e_cev, part_evt[(b, ot, t0)]], ndma=2)

                    def f_cst(h, b=b, ot=ot, t0=t0, tc=tc):
                        h.dma_start(out=yout[b, ot, :, t0:t0 + TT],
                                    in_=TCO[:, TT * tc:TT * tc + TT]).then_inc(s_sy, 16)
                        return None
                    e_cst = E("sync", f_cst, deps=[e_cacc])
                    tco_free[tc] = e_cst
                    final_store = e_cst
            pool_guard = pool_guard + [e_cmm_last[b]]

        if dbg:
            ddump = {}
            dlist = [("dxpb", XPB), ("dxt", XT), ("dza", ZACC),
                     ("dzb", ZACCB), ("dcar", CARRY), ("dms2", MS2A),
                     ("dfst", FST), ("dgfs", GFS),
                     ("drow1", ROW1), ("dpwy", PWY), ("dpwr", PWR),
                     ("da32", A32), ("ddbgt", DBGT)]
            for nm, buf in dlist:
                shp = list(buf.shape)
                dt_ = F32 if nm == "ddbgt" else BF16
                ddump[nm] = nc.dram_tensor(nm, shp, dt_, kind="ExternalOutput").ap()

            def f_dump(h):
                for nm, buf in dlist:
                    h.dma_start(out=ddump[nm][:], in_=(buf[:] if hasattr(buf, "shape") and not hasattr(buf, "offset") else buf)).then_inc(s_sy, 16)
                return None
            E("sync", f_dump, deps=[final_store, e_cmm_last.get(b_run - 1)], ndma=len(dlist))

        def f_fin(h):
            return None
        E("gpsimd", f_fin, deps=[final_store], ndma=0)

        sch.emit(block, sems)

    return nc


def kernel(**inputs):
    L0 = {k[3:]: np.asarray(v) for k, v in inputs.items() if k.startswith("L0_")}
    x = np.ascontiguousarray(L0["attn_x"])
    perm = bitrev_perm(L)
    import ml_dtypes

    def lhsT(w):
        return np.ascontiguousarray(w.T.reshape(KT, 128, -1))

    brv5 = bitrev_perm(32)[:32]
    mask_h = np.zeros((4, 128, 32), dtype=np.float32)
    for t in range(4):
        for ph in range(8):
            p = 8 * t + ph
            mask_h[t, 16 * ph:16 * ph + 16, :] = ((brv5 + p) < 32)[None, :]
    repmoe_h = np.zeros((16, 256), dtype=np.float32)
    for e in range(16):
        repmoe_h[e, 16 * e:16 * e + 16] = 1.0
    repd_h = np.zeros((16, 128), dtype=np.float32)
    for ph in range(8):
        repd_h[:, 16 * ph:16 * ph + 16] = np.eye(16)

    shared = dict(
        win=lhsT(L0["attn_proj_in_w"]), wffn=lhsT(L0["ffn_W"]),
        wout=lhsT(L0["attn_proj_out_w"]),
        amoe=np.ascontiguousarray(L0["ffn_A"].reshape(256, D).T.reshape(KT, 128, 256)),
        bmoe=np.ascontiguousarray(L0["ffn_B"].transpose(0, 2, 1).reshape(256, D).reshape(2, 128, D)),
        rw1=lhsT(L0["ffn_rw1"]), rw2=np.ascontiguousarray(L0["ffn_rw2"].T),
        bw=lhsT(L0["attn_B_w"]), cw=np.ascontiguousarray(L0["attn_C_w"].reshape(KT, 128, 16)),
        aA=np.ascontiguousarray(L0["attn_A"].reshape(KT, 128, 16)),
        dpar=np.ascontiguousarray(L0["attn_D_param"].reshape(KT, 128).T),
        binp=np.ascontiguousarray(L0["attn_proj_in_b"].reshape(KT, 128).T),
        bcmb=np.ascontiguousarray(np.stack([L0["ffn_bias"].reshape(KT, 128).T,
                                            L0["attn_proj_out_b"].reshape(KT, 128).T], 0)),
        rb1v=np.ascontiguousarray(L0["ffn_rb1"].reshape(64, 1)),
        rb2rep=np.ascontiguousarray(np.broadcast_to(L0["ffn_rb2"][None, :], (128, 32))),
        gaterep=np.ascontiguousarray(np.broadcast_to(L0["ffn_gates"][None, :], (128, 16))),
        identb=np.eye(128, dtype=ml_dtypes.bfloat16),
        maskt=mask_h.astype(ml_dtypes.bfloat16),
        repmoe=repmoe_h.astype(ml_dtypes.bfloat16),
        repd=repd_h.astype(ml_dtypes.bfloat16),
    )

    nc = build_nc(dbg=False, b_run=B)
    in_maps = []
    for core in range(8):
        xs = x[2 * core:2 * core + 2][:, perm, :]
        xs = np.ascontiguousarray(xs.transpose(0, 2, 1).reshape(B, KT, 128, L))
        m = dict(shared)
        m["xin"] = xs
        in_maps.append(m)

    res = run_bass_kernel_spmd(nc, in_maps, core_ids=list(range(8)))
    out = np.empty((16, L, D), dtype=np.float32)
    inv = np.argsort(perm)
    for core in range(8):
        y = np.asarray(res.results[core]["yout"], np.float32)
        y = y.reshape(B, D, L).transpose(0, 2, 1)
        out[2 * core:2 * core + 2] = y[:, inv, :]
    return out


# version 12
# speedup vs baseline: 1.3057x; 1.1318x over previous
import numpy as np
import concourse.bass as bass
import concourse.mybir as mybir
from concourse.bass_utils import run_bass_kernel_spmd
from concourse.alu_op_type import AluOpType

F32 = mybir.dt.float32
BF16 = mybir.dt.bfloat16
AF = mybir.ActivationFunctionType
AX = mybir.AxisListType.X

B, L, D, S, C = 2, 2048, 2048, 16, 32
KT, NH = 16, 2
LH = L // NH
NKH = LH // C
TT = 512
NEG = -1.0e30
EPS = 1e-5


def bitrev_perm(n):
    idx = np.arange(n)
    c, pos = idx // 32, idx % 32
    r = np.zeros_like(pos)
    for b in range(5):
        r |= ((pos >> b) & 1) << (4 - b)
    return c * 32 + r


class Sched:
    ENGS = ("tensor", "vector", "scalar", "sync", "gpsimd")

    def __init__(self):
        self.events = {e: [] for e in self.ENGS}
        self.cum = {e: 0 for e in self.ENGS}

    def ev(self, eng, fn, deps=(), ndma=1):
        n = len(self.events[eng])
        dd = [d for d in deps
              if d is not None and (d[0] != eng or eng in ("sync", "gpsimd"))]
        if eng in ("sync", "gpsimd"):
            self.cum[eng] += 16 * ndma
        else:
            self.cum[eng] += 1
        self.events[eng].append((fn, dd, self.cum[eng], ndma))
        return (eng, n)

    def emit(self, block, sems):
        after = {e: [ev[2] for ev in self.events[e]] for e in self.ENGS}
        for eng in self.ENGS:
            evs = self.events[eng]
            is_dma = eng in ("sync", "gpsimd")

            def make(eng=eng, evs=evs, is_dma=is_dma):
                def body(h):
                    sem = sems[eng]
                    cnt = [0]
                    if is_dma:
                        orig = h.dma_start

                        def counted(*a, **k):
                            cnt[0] += 1
                            return orig(*a, **k)
                        h.dma_start = counted
                    for i, (fn, dd, _v, ndma) in enumerate(evs):
                        best = {}
                        for (deng, didx) in dd:
                            v = after[deng][didx]
                            if best.get(deng, -1) < v:
                                best[deng] = v
                        for deng, v in best.items():
                            h.wait_ge(sems[deng], v)
                        cnt[0] = 0
                        r = fn(h)
                        if is_dma:
                            assert cnt[0] == ndma, (
                                f"{eng} event {i}: declared ndma={ndma} actual={cnt[0]}")
                        else:
                            h.drain().then_inc(sem, 1)
                return body
            getattr(block, eng)(make())


def build_nc(dbg=False, b_run=B):
    nc = bass.Bass("TRN2", target_bir_lowering=False)

    def din(name, shp, dt=F32):
        return nc.dram_tensor(name, shp, dt, kind="ExternalInput").ap()

    def dint(name, shp, dt=F32):
        return nc.dram_tensor(name, shp, dt, kind="Internal").ap()

    dkind = "ExternalOutput" if dbg else "Internal"
    xin = din("xin", [B, KT, 128, L])
    win = din("win", [KT, 128, D])
    wffn = din("wffn", [KT, 128, D])
    wout = din("wout", [KT, 128, D])
    amoe = din("amoe", [KT, 128, 256])
    bmoe = din("bmoe", [2, 128, D])
    rw1 = din("rw1", [KT, 128, 64])
    rw2 = din("rw2", [64, 32])
    bwd = din("bw", [KT, 128, 16])
    cwd = din("cw", [KT, 128, 16])
    aA = din("aA", [KT, 128, 16])
    dpar = din("dpar", [128, KT])
    binp = din("binp", [128, KT])
    bcmb = din("bcmb", [2, 128, KT])
    rb1v = din("rb1v", [64, 1])
    rb2rep = din("rb2rep", [128, 32])
    gaterep = din("gaterep", [128, 16])
    identb = din("identb", [128, 128], BF16)
    maskt = din("maskt", [4, 128, 32], BF16)
    repmoe = din("repmoe", [16, 256], BF16)
    repd = din("repd", [16, 128], BF16)

    winb = nc.dram_tensor("winb", [KT, 128, D], BF16, kind=dkind).ap()
    wffnb = nc.dram_tensor("wffnb", [KT, 128, D], BF16, kind=dkind).ap()
    woutdb = nc.dram_tensor("woutdb", [KT, 128, D], BF16, kind=dkind).ap()
    wcwescr = dint("wcwescr", [128, D], BF16)
    dscrB = nc.dram_tensor("dscrB", [B, 16, L], BF16, kind=dkind).ap()
    fscr = nc.dram_tensor("fscr", [B, 64, 512], BF16, kind=dkind).ap()
    part = nc.dram_tensor("part", [B, KT, 128, L], F32, kind=dkind).ap()
    yout = nc.dram_tensor("yout", [B, KT, 128, L], F32, kind="ExternalOutput").ap()

    sch = Sched()
    E = sch.ev

    import contextlib
    st_ = contextlib.ExitStack()
    with st_:
        XT = st_.enter_context(nc.sbuf_tensor([128, KT * LH], BF16))
        XPB = st_.enter_context(nc.sbuf_tensor([128, KT * LH], BF16))
        PWY = st_.enter_context(nc.sbuf_tensor([128, KT * S * 32], BF16))
        PWR = st_.enter_context(nc.sbuf_tensor([128, KT * S * 32], BF16))
        POOL = st_.enter_context(nc.sbuf_tensor([128, 8192], BF16))
        WQ1 = POOL[:, 0:4096]
        WQ2 = POOL[:, 4096:8192]
        WTMP = POOL[:, 0:D]
        WCWS = POOL[0:16, 2048:2048 + D]
        WCWE = POOL[:, 4096:4096 + D]
        SCR = st_.enter_context(nc.sbuf_tensor([128, 8192], BF16))
        GFS = SCR[0:16, 4096:4096 + 63 * 32]
        DSB = SCR[0:16, 6144:8192]
        AMB = st_.enter_context(nc.sbuf_tensor([128, KT * 256], BF16))
        BMB = st_.enter_context(nc.sbuf_tensor([128, 2 * D], BF16))
        RW1B = st_.enter_context(nc.sbuf_tensor([128, KT * 64], BF16))
        RW2T = st_.enter_context(nc.sbuf_tensor([128, 32], BF16))
        RW2B = RW2T[64:128, 0:32]
        BWB = st_.enter_context(nc.sbuf_tensor([128, KT * 16], BF16))
        CWB = st_.enter_context(nc.sbuf_tensor([128, KT * 16], BF16))
        XXB = st_.enter_context(nc.sbuf_tensor([128, 2048], BF16))
        ISR = st_.enter_context(nc.sbuf_tensor([128, 128], F32))
        MISR = st_.enter_context(nc.sbuf_tensor([128, 128], F32))
        ZS = st_.enter_context(nc.sbuf_tensor([128, 4 * TT], BF16))
        ZACC = st_.enter_context(nc.sbuf_tensor([128, L], BF16))
        ZACCB = st_.enter_context(nc.sbuf_tensor([128, L], BF16))
        DEXPT = st_.enter_context(nc.sbuf_tensor([128, TT], BF16))
        DREP = st_.enter_context(nc.sbuf_tensor([128, LH], BF16))
        ROW1 = st_.enter_context(nc.sbuf_tensor([128, TT], BF16))
        DSS = ROW1[0:16, :]
        CFF = ROW1[32:48, :]
        H1B = ROW1[64:128, :]
        ROW2 = st_.enter_context(nc.sbuf_tensor([128, 1536], F32))
        DTMP = ROW2[0:16, 0:512]
        MROW = ROW2[0:1, 512:1024]
        SROW = ROW2[0:1, 1024:1280]
        SIGA = ROW2[:, 1280:1536]
        H1F = ROW2[64:128, 0:1024]
        XAB = st_.enter_context(nc.sbuf_tensor([128, 2 * TT], BF16))
        CTT = st_.enter_context(nc.sbuf_tensor([128, 64], BF16))
        GEMB = st_.enter_context(nc.sbuf_tensor([128, 64], F32))
        CREP = st_.enter_context(nc.sbuf_tensor([128, 2 * TT], BF16))
        TOP5 = st_.enter_context(nc.sbuf_tensor([128, 264], F32))
        CONSTS = st_.enter_context(nc.sbuf_tensor([128, 4], F32))
        DBGT = st_.enter_context(nc.sbuf_tensor([128, 48], F32))
        XB2 = st_.enter_context(nc.sbuf_tensor([128, 1024], BF16))
        MSEQ = XB2[:, 0:512]
        ACZ = XB2[:, 512:1024]
        MS2A = st_.enter_context(nc.sbuf_tensor([128, S * KT * NKH], BF16))
        CARRY = st_.enter_context(nc.sbuf_tensor([128, KT * 16], BF16))
        FST = st_.enter_context(nc.sbuf_tensor([64, 512], BF16))
        A32 = st_.enter_context(nc.sbuf_tensor([128, 256], BF16))
        SIGB = st_.enter_context(nc.sbuf_tensor([128, 256], BF16))
        IDB = st_.enter_context(nc.sbuf_tensor([128, 128], BF16))
        MSK = st_.enter_context(nc.sbuf_tensor([128, 4 * 32], BF16))
        SMALL = st_.enter_context(nc.sbuf_tensor([128, 256], BF16))
        RPM = SMALL[32:48, 0:256]
        RPD = SMALL[0:16, 0:128]
        BINS = st_.enter_context(nc.sbuf_tensor([128, KT], F32))
        BCS = st_.enter_context(nc.sbuf_tensor([128, KT], F32))
        BC2 = st_.enter_context(nc.sbuf_tensor([128, KT], F32))
        DPS = st_.enter_context(nc.sbuf_tensor([128, KT], F32))
        RB1S = st_.enter_context(nc.sbuf_tensor([64, 1], F32))
        B2G = st_.enter_context(nc.sbuf_tensor([128, 32], F32))
        GATS = st_.enter_context(nc.sbuf_tensor([128, 16], F32))
        NEGI = st_.enter_context(nc.sbuf_tensor([128, 16], F32))
        ZERI = st_.enter_context(nc.sbuf_tensor([128, 16], F32))
        ONER = st_.enter_context(nc.sbuf_tensor([1, 128], F32))
        ONEC = st_.enter_context(nc.sbuf_tensor([128, 1], BF16))
        TCO = st_.enter_context(nc.sbuf_tensor([128, 2 * TT], F32))
        PST = TCO[:, 0:256]
        s_te = st_.enter_context(nc.semaphore("s_te"))
        s_ve = st_.enter_context(nc.semaphore("s_ve"))
        s_se = st_.enter_context(nc.semaphore("s_se"))
        s_sy = st_.enter_context(nc.semaphore("s_sy"))
        s_gp = st_.enter_context(nc.semaphore("s_gp"))
        block = st_.enter_context(nc.Block())
        PB0 = st_.enter_context(nc.psum_tensor([128, 512], F32))
        PB1 = st_.enter_context(nc.psum_tensor([128, 512], F32))
        PB2 = st_.enter_context(nc.psum_tensor([128, 512], F32))
        PB3 = st_.enter_context(nc.psum_tensor([128, 512], F32))
        PB4 = st_.enter_context(nc.psum_tensor([128, 512], F32))
        PB5 = st_.enter_context(nc.psum_tensor([128, 512], F32))
        PB6 = st_.enter_context(nc.psum_tensor([128, 512], F32))
        PB7 = st_.enter_context(nc.psum_tensor([128, 1024], BF16))
        PB = [PB0, PB1, PB2, PB3, PB4, PB5, PB6, PB7]
        sems = dict(tensor=s_te, vector=s_ve, scalar=s_se, sync=s_sy, gpsimd=s_gp)

        def xtv(t=LH):
            return XT[:].rearrange("p (k t) -> p k t", t=t)

        def xpv(t=LH):
            return XPB[:].rearrange("p (k t) -> p k t", t=t)

        def f_lds(h):
            h.dma_start(out=SIGA.rearrange("p (k s) -> p k s", s=16),
                        in_=aA.transpose([1, 0, 2])).then_inc(s_sy, 16)
            h.dma_start(out=IDB[:], in_=identb[:]).then_inc(s_sy, 16)
            h.dma_start(out=MSK[:].rearrange("p (t j) -> p t j", j=32),
                        in_=maskt.transpose([1, 0, 2])).then_inc(s_sy, 16)
            h.dma_start(out=BINS[:], in_=binp[:]).then_inc(s_sy, 16)
            h.dma_start(out=BCS[:], in_=bcmb[0]).then_inc(s_sy, 16)
            h.dma_start(out=BC2[:], in_=bcmb[1]).then_inc(s_sy, 16)
            h.dma_start(out=DPS[:], in_=dpar[:]).then_inc(s_sy, 16)
            h.dma_start(out=RB1S[:], in_=rb1v[:]).then_inc(s_sy, 16)
            h.dma_start(out=B2G[:], in_=rb2rep[:]).then_inc(s_sy, 16)
            h.dma_start(out=GATS[:], in_=gaterep[:]).then_inc(s_sy, 16)
            return None
        e_lds = E("sync", f_lds, ndma=10)

        def f_ldc(h):
            h.dma_start(out=AMB[:].rearrange("p (k e) -> p k e", e=256), in_=amoe.transpose([1, 0, 2])).then_inc(s_gp, 16)
            h.dma_start(out=BMB[:].rearrange("p (q o) -> p q o", o=D), in_=bmoe.transpose([1, 0, 2])).then_inc(s_gp, 16)
            h.dma_start(out=RW1B[:].rearrange("p (k e) -> p k e", e=64), in_=rw1.transpose([1, 0, 2])).then_inc(s_gp, 16)
            h.dma_start(out=RW2B, in_=rw2[:]).then_inc(s_gp, 16)
            h.dma_start(out=BWB[:].rearrange("p (k s) -> p k s", s=16), in_=bwd.transpose([1, 0, 2])).then_inc(s_gp, 16)
            h.dma_start(out=CWB[:].rearrange("p (k s) -> p k s", s=16), in_=cwd.transpose([1, 0, 2])).then_inc(s_gp, 16)
            h.dma_start(out=RPM, in_=repmoe[:]).then_inc(s_gp, 16)
            h.dma_start(out=RPD, in_=repd[:]).then_inc(s_gp, 16)
            return None
        e_ldc = E("gpsimd", f_ldc, ndma=8)

        def f_const(h):
            h.memset(NEGI[:], NEG)
            h.memset(CONSTS[:, 0:1], EPS)
            h.memset(CONSTS[:, 1:2], 1.0)
            h.memset(CONSTS[:, 2:3], NEG)
            h.memset(ZERI[:], 0.0)
            h.memset(ONER[:], 1.0)
            h.memset(ONEC[:], 1.0)
            h.tensor_tensor(out=BCS[:], in0=BCS[:], in1=BC2[:], op=AluOpType.add)
            return h.tensor_tensor(out=B2G[:, 0:16], in0=B2G[:, 0:16], in1=GATS[:],
                                   op=AluOpType.add)
        e_const = E("vector", f_const, deps=[e_lds, e_ldc])

        def f_siga(h):
            return h.activation(out=SIGA, in_=SIGA, func=AF.Sigmoid)
        e_siga = E("scalar", f_siga, deps=[e_lds])

        brv5 = bitrev_perm(32)[:32]

        def f_pow(h):
            pst3 = PST.rearrange("q (k s) -> q k s", s=16)
            h.memset(PST, 1.0)
            h.drain()
            last = None
            for p in range(32):
                ov = PWY[:].rearrange("q (k c s) -> q k c s", c=32, s=16)[:, :, p]
                last = h.tensor_copy(out=ov, in_=pst3)
                last = h.tensor_tensor(out=PST, in0=PST, in1=SIGA, op=AluOpType.mult)
            h.tensor_copy(out=A32.rearrange("q (k s) -> q k s", s=16), in_=pst3)
            h.tensor_copy(out=SIGB.rearrange("q (k s) -> q k s", s=16),
                          in_=SIGA.rearrange("q (k s) -> q k s", s=16))
            h.memset(PST, 1.0)
            h.drain()
            for t in range(32):
                i = 31 - t
                slot = int(np.where(brv5 == i)[0][0])
                ov = PWR[:].rearrange("q (k s r) -> q k s r", s=16, r=32)[:, :, :, slot]
                last = h.tensor_copy(out=ov, in_=pst3)
                if t < 31:
                    last = h.tensor_tensor(out=PST, in0=PST, in1=SIGA, op=AluOpType.mult)
            return last
        e_pow = E("vector", f_pow, deps=[e_siga, e_const])

        ew_prev = None
        e_wcw_mm = None
        for kt in range(KT):
            def f_wl(h, kt=kt):
                h.dma_start(out=WTMP, in_=wout[kt]).then_inc(s_gp, 16)
                return None
            ew_l = E("gpsimd", f_wl, deps=[ew_prev, e_ldc])

            def f_wcw(h, kt=kt):
                last = None
                cw_sl = CWB[:].rearrange("p (k s) -> p k s", s=16)[:, kt]
                for nt in range(4):
                    last = h.matmul(PB[nt][0:16, :], cw_sl, WTMP[:, 512 * nt:512 * nt + 512],
                                    start=(kt == 0), stop=(kt == KT - 1))
                return last
            e_wcw_mm = E("tensor", f_wcw, deps=[ew_l, e_ldc])

            def f_wd(h, kt=kt):
                return h.activation(out=WTMP, in_=WTMP, func=AF.Identity,
                                    scale=DPS[:, kt:kt + 1])
            e_wd = E("scalar", f_wd, deps=[e_wcw_mm, e_lds])

            def f_wds(h, kt=kt):
                h.dma_start(out=woutdb[kt], in_=WTMP).then_inc(s_gp, 16)
                return None
            ew_prev = E("gpsimd", f_wds, deps=[e_wd])

        for kt in range(KT):
            def f_wl2(h, kt=kt):
                h.dma_start(out=WTMP, in_=wffn[kt]).then_inc(s_gp, 16)
                return None
            ew_l2 = E("gpsimd", f_wl2, deps=[ew_prev])

            def f_wadd(h, kt=kt):
                return h.tensor_tensor(out=WTMP[:, 128 * kt:128 * kt + 128],
                                       in0=WTMP[:, 128 * kt:128 * kt + 128],
                                       in1=IDB[:], op=AluOpType.add)
            e_wa = E("vector", f_wadd, deps=[ew_l2, e_lds])

            def f_ws2(h, kt=kt):
                h.dma_start(out=wffnb[kt], in_=WTMP).then_inc(s_gp, 16)
                return None
            ew_prev = E("gpsimd", f_ws2, deps=[e_wa])

            def f_wl3(h, kt=kt):
                h.dma_start(out=WTMP, in_=win[kt]).then_inc(s_gp, 16)
                return None
            ew_l3 = E("gpsimd", f_wl3, deps=[ew_prev])

            def f_ws3(h, kt=kt):
                h.dma_start(out=winb[kt], in_=WTMP).then_inc(s_gp, 16)
                return None
            ew_prev = E("gpsimd", f_ws3, deps=[ew_l3])
        e_wprep_done = ew_prev

        def f_wcwev(h):
            last = None
            for nt in range(4):
                last = h.activation(out=WCWS[:, 512 * nt:512 * nt + 512],
                                    in_=PB[nt][0:16, :], func=AF.Copy)
            return last
        e_wcwev = E("scalar", f_wcwev, deps=[e_wcw_mm])

        e_wcwx_list = []
        for nt in range(4):
            def f_x1(h, nt=nt):
                return h.matmul(PB[4][:, :], RPD[:, :], WCWS[:, 512 * nt:512 * nt + 512],
                                start=True, stop=True)
            e_x1 = E("tensor", f_x1, deps=[e_wcwev, e_ldc] + ([e_wcwx_list[-1]] if e_wcwx_list else []))

            def f_x2(h, nt=nt):
                return h.activation(out=WCWE[:, 512 * nt:512 * nt + 512], in_=PB[4][:, :],
                                    func=AF.Copy)
            e_x2 = E("scalar", f_x2, deps=[e_x1])
            e_wcwx_list.append(e_x2)

        def f_wcwst(h):
            h.dma_start(out=wcwescr[:], in_=WCWE).then_inc(s_sy, 16)
            return None
        e_wcwst = E("sync", f_wcwst, deps=[e_wcwx_list[-1]])
        e_wcwe_done = e_wcwst

        bank_free = {i: e_wcwe_done for i in range(8)}
        tco_free = {0: e_pow, 1: e_pow}

        part_evt = {}
        pool_guard = []
        e_shift_prev = None
        e_zbrv_done = {}
        e_cmm_last = {}
        e_dst_last = {}
        e_fstore = {}
        e_gfadd_done = {}
        last_scan_evt = None
        prev_st = None
        prev_xx = None
        omm_last = None
        cst_last = None
        h1_reader_last = None
        final_store = None
        e_fev_last = None

        for b in range(b_run):
            def f_zacc0(h):
                return h.memset(ZACC[:, :], 0.0)
            e_zacc0 = E("vector", f_zacc0, deps=[e_zbrv_done.get(b - 1)])

            for g in range(NH):
                t0g = g * LH
                e_xn_done = None
                prev_repev = None
                for tt in range(LH // TT):
                    for st in range(4):
                        sl0 = tt * TT + st * 128
                        t0s = t0g + sl0

                        def f_xl(h, b=b, t0s=t0s, sl0=sl0):
                            h.dma_start(out=xtv()[:, :, sl0:sl0 + 128],
                                        in_=xin[b, :, :, t0s:t0s + 128].transpose([1, 0, 2])).then_inc(s_gp, 16)
                            return None
                        e_xl = E("gpsimd", f_xl, deps=[last_scan_evt, omm_last, prev_st])

                        def f_xx(h, sl0=sl0):
                            return h.activation(out=XXB[:].rearrange("p (k t) -> p k t", t=128),
                                                in_=xtv()[:, :, sl0:sl0 + 128], func=AF.Square)
                        e_xx = E("scalar", f_xx, deps=[e_xl, prev_st])
                        prev_xx = e_xx

                        def f_st(h, sl0=sl0):
                            last = None
                            for kt in range(KT):
                                last = h.matmul(PB[6][0:1, 0:128], ONEC[:, :],
                                                xtv()[:, kt, sl0:sl0 + 128],
                                                start=(kt == 0), stop=(kt == KT - 1))
                            for kt in range(KT):
                                last = h.matmul(PB[6][32:33, 0:128], ONEC[:, :],
                                                XXB[:, 128 * kt:128 * kt + 128],
                                                start=(kt == 0), stop=(kt == KT - 1))
                            return last
                        e_st = E("tensor", f_st, deps=[e_xl, e_xx, e_const, bank_free[6]])
                        prev_st = e_st

                        def f_ms(h):
                            h.tensor_scalar(out=MROW[0:1, 0:128], in0=PB[6][0:1, 0:128],
                                            scalar1=1.0 / D, scalar2=None, op0=AluOpType.mult)
                            h.tensor_scalar(out=MROW[0:1, 128:256], in0=PB[6][32:33, 0:128],
                                            scalar1=1.0 / D, scalar2=None, op0=AluOpType.mult)
                            h.tensor_tensor(out=MROW[0:1, 256:384], in0=MROW[0:1, 0:128],
                                            in1=MROW[0:1, 0:128], op=AluOpType.mult)
                            return h.tensor_tensor(out=MROW[0:1, 128:256], in0=MROW[0:1, 128:256],
                                                   in1=MROW[0:1, 256:384], op=AluOpType.subtract)
                        e_ms = E("vector", f_ms, deps=[e_st])
                        bank_free[6] = e_ms

                        def f_is(h):
                            h.activation(out=MROW[0:1, 384:512], in_=MROW[0:1, 128:256],
                                         func=AF.Ln, bias=CONSTS[0:1, 0:1])
                            h.drain()
                            return h.activation(out=SROW[0:1, 0:128], in_=MROW[0:1, 384:512],
                                                func=AF.Exp, scale=-0.5)
                        e_is = E("scalar", f_is, deps=[e_ms, e_const])

                        def f_mis(h):
                            return h.tensor_tensor(out=SROW[0:1, 128:256], in0=MROW[0:1, 0:128],
                                                   in1=SROW[0:1, 0:128], op=AluOpType.mult)
                        e_mis = E("vector", f_mis, deps=[e_is])

                        def f_rep(h):
                            h.matmul(PB[5][:, 0:128], ONER[:, :], SROW[0:1, 0:128],
                                     start=True, stop=True)
                            return h.matmul(PB[5][:, 128:256], ONER[:, :], SROW[0:1, 128:256],
                                            start=True, stop=True)
                        e_rep = E("tensor", f_rep, deps=[e_mis, bank_free[5]])

                        def f_repev(h):
                            h.activation(out=ISR[:, 0:128], in_=PB[5][:, 0:128], func=AF.Copy)
                            return h.activation(out=MISR[:, 0:128], in_=PB[5][:, 128:256],
                                                func=AF.Copy)
                        e_repev = E("scalar", f_repev, deps=[e_rep, prev_repev])
                        bank_free[5] = e_repev
                        prev_repev = e_repev

                        def f_xn(h, sl0=sl0):
                            xs = xtv()[:, :, sl0:sl0 + 128]
                            isv = ISR[:, 0:128].unsqueeze(1).broadcast_to([128, KT, 128])
                            misv = MISR[:, 0:128].unsqueeze(1).broadcast_to([128, KT, 128])
                            h.tensor_tensor(out=xs, in0=xs, in1=isv, op=AluOpType.mult)
                            return h.tensor_tensor(out=xs, in0=xs, in1=misv,
                                                   op=AluOpType.subtract)
                        e_xn = E("vector", f_xn, deps=[e_repev, e_st])
                        e_xn_done = e_xn

                e_xp_done = None
                wq_dep = omm_last
                for tt in range(LH // TT):
                    for eq in range(8):
                        def f_wql(h, eq=eq):
                            h.dma_start(out=WQ1.rearrange("p (k e) -> p k e", e=256),
                                        in_=winb[:, :, 256 * eq:256 * eq + 256].transpose([1, 0, 2])).then_inc(s_sy, 16)
                            return None
                        e_wql = E("sync", f_wql, deps=[wq_dep, e_wprep_done, e_wcwe_done] + pool_guard)
                        for et4 in range(2):
                            et = eq * 2 + et4
                            pbk = et4 % 2

                            def f_pmm(h, et4=et4, pbk=pbk, tt=tt):
                                last = None
                                for kt in range(KT):
                                    wsl = WQ1.rearrange("p (k e) -> p k e", e=256)[:, kt, 128 * et4:128 * et4 + 128]
                                    xsl = xtv()[:, kt, tt * TT: tt * TT + TT]
                                    last = h.matmul(PB[pbk][:, :], wsl, xsl,
                                                    start=(kt == 0), stop=(kt == KT - 1))
                                return last
                            e_pmm = E("tensor", f_pmm, deps=[e_wql, e_xn_done, bank_free[pbk],
                                                             last_scan_evt])

                            def f_pev(h, et=et, pbk=pbk, tt=tt):
                                ov = xpv()[:, et, tt * TT:tt * TT + TT]
                                return h.activation(out=ov, in_=PB[pbk][:, :], func=AF.Identity,
                                                    bias=BINS[:, et:et + 1])
                            e_pev = E("scalar", f_pev, deps=[e_pmm])
                            bank_free[pbk] = e_pev
                            e_xp_done = e_pev
                            wq_dep = e_pmm

                e_wxa = {}
                e_top2_t = {}
                e_zev_t = {}
                for tt in range(LH // TT):
                    t0 = t0g + tt * TT

                    def f_ymm(h, tt=tt):
                        last = None
                        for kt in range(KT):
                            xsl = xpv()[:, kt, tt * TT:tt * TT + TT]
                            for pt in range(4):
                                wv = PWY[:, 512 * kt + 128 * pt:512 * kt + 128 * pt + 128]
                                last = h.matmul(PB[pt][:, :], wv, xsl,
                                                start=(kt == 0), stop=(kt == KT - 1))
                            bsl = BWB[:].rearrange("p (k s) -> p k s", s=16)[:, kt]
                            last = h.matmul(PB[4][0:16, :], bsl, xsl,
                                            start=(kt == 0), stop=(kt == KT - 1))
                        return last
                    e_ymm = E("tensor", f_ymm, deps=[e_xp_done, e_pow,
                                                     bank_free[0], bank_free[1], bank_free[2],
                                                     bank_free[3], bank_free[4]])

                    def f_zev(h, tt=tt):
                        last = None
                        for pt in range(4):
                            for hi in range(8):
                                j4, j3, j2 = hi >> 2, (hi >> 1) & 1, hi & 1
                                inv = PB[pt][:, :].rearrange("p (c jh j1 j0) -> p c jh j1 j0",
                                                             jh=8, j1=2, j0=2)[:, :, hi]
                                mv = MSK[:, 32 * pt:32 * pt + 32].rearrange(
                                    "p (jh j1 j0) -> p jh j1 j0", jh=8, j1=2, j0=2)[:, hi]
                                mv = mv.unsqueeze(1).broadcast_to([128, 16, 2, 2])
                                zt = ZS[:, TT * pt: TT * pt + TT]
                                ov = zt.rearrange("p (c j0 j1 jh) -> p c j0 j1 jh",
                                                  j0=2, j1=2, jh=8)
                                ov = ov[:, :, :, :, 4 * j2 + 2 * j3 + j4].transpose([0, 1, 3, 2])
                                last = h.tensor_tensor(out=ov, in0=inv, in1=mv, op=AluOpType.mult)
                        return last
                    e_zev = E("vector", f_zev, deps=[e_ymm, e_lds, e_shift_prev])
                    e_zev_t[tt] = e_zev
                    for pt in range(4):
                        bank_free[pt] = e_zev

                    def f_dev(h, t0=t0):
                        h.activation(out=DTMP[:, :], in_=PB[4][0:16, :], func=AF.Exp)
                        h.drain()
                        return h.activation(out=DSB[:, t0:t0 + TT], in_=DTMP[:, :],
                                            func=AF.Ln, bias=1.0)
                    e_dev = E("scalar", f_dev, deps=[e_ymm, e_const, last_scan_evt])

                    def f_devS(h, t0=t0):
                        last = None
                        for hi in range(8):
                            j4, j3, j2 = hi >> 2, (hi >> 1) & 1, hi & 1
                            inv = DSB[:, t0:t0 + TT].rearrange("p (c jh j1 j0) -> p c jh j1 j0",
                                                               jh=8, j1=2, j0=2)[:, :, hi]
                            ov = DSS[:, 0:TT].rearrange("p (c j0 j1 jh) -> p c j0 j1 jh",
                                                        j0=2, j1=2, jh=8)
                            ov = ov[:, :, :, :, 4 * j2 + 2 * j3 + j4].transpose([0, 1, 3, 2])
                            last = h.activation(out=ov, in_=inv, func=AF.Copy)
                        return last
                    e_devS = E("scalar", f_devS, deps=[e_dev])
                    bank_free[4] = e_devS

                    def f_dst(h, b=b, t0=t0):
                        h.dma_start(out=dscrB[b, :, t0:t0 + TT], in_=DSB[:, t0:t0 + TT]).then_inc(s_sy, 16)
                        return None
                    e_dst = E("sync", f_dst, deps=[e_devS])
                    e_dst_last[b] = e_dst

                    def f_dx(h):
                        return h.matmul(PB[5][:, :], RPD[:, :], DSS[:, 0:TT],
                                        start=True, stop=True)
                    e_dx = E("tensor", f_dx, deps=[bank_free[5], e_devS])

                    def f_dxe(h):
                        return h.activation(out=DEXPT[:, :], in_=PB[5][:, :], func=AF.Copy)
                    e_dxe = E("scalar", f_dxe, deps=[e_dx])
                    bank_free[5] = e_dxe

                    def f_zmul(h):
                        last = None
                        for pt in range(4):
                            last = h.tensor_tensor(out=ZS[:, TT * pt:TT * pt + TT],
                                                   in0=ZS[:, TT * pt:TT * pt + TT],
                                                   in1=DEXPT[:, :], op=AluOpType.mult)
                        return last
                    e_zmul = E("vector", f_zmul, deps=[e_zev, e_dxe])

                    shift_base = sch.cum["gpsimd"]

                    def f_shift(h, base=shift_base, t0=t0):
                        cum = base
                        for pt in range(4):
                            for ph in range(8):
                                p = 8 * pt + ph
                                rows = slice(16 * ph, 16 * ph + 16)
                                n = min(TT, L - t0 - p)
                                src = ZS[rows, TT * pt:TT * pt + n]
                                dst = ZACC[rows, t0 + p:t0 + p + n]
                                h.dma_start(out=dst, in_=src,
                                            accum_op=AluOpType.add).then_inc(s_gp, 16)
                                cum += 16
                            h.wait_ge(s_gp, cum)
                        return None
                    e_shift = E("gpsimd", f_shift, deps=[e_zmul, e_zacc0, e_shift_prev], ndma=32)
                    e_shift_prev = e_shift

                    def f_h1(h, tt=tt):
                        last = None
                        for kt in range(KT):
                            wsl = RW1B[:].rearrange("p (k e) -> p k e", e=64)[:, kt]
                            xsl = xtv()[:, kt, tt * TT:tt * TT + TT]
                            last = h.matmul(PB[4][0:64, :], wsl, xsl,
                                            start=(kt == 0), stop=(kt == KT - 1))
                        return last
                    e_h1 = E("tensor", f_h1, deps=[e_xn_done, bank_free[4], e_ldc])

                    def f_h1b(h):
                        return h.activation(out=H1F[:, 0:TT], in_=PB[4][0:64, :],
                                            func=AF.Identity, bias=RB1S[:, 0:1])
                    e_h1b = E("scalar", f_h1b, deps=[e_h1])
                    bank_free[4] = e_h1b

                    def f_h1e(h):
                        return h.activation(out=H1F[:, TT:2 * TT], in_=H1F[:, 0:TT],
                                            func=AF.Exp, scale=-1.0)
                    e_h1e = E("scalar", f_h1e, deps=[e_h1b])

                    def f_h1v(h, t0=t0):
                        ex = H1F[:, TT:2 * TT]
                        h.tensor_scalar(out=ex, in0=ex, scalar1=1.0, scalar2=None,
                                        op0=AluOpType.add)
                        h.drain()
                        h.reciprocal(out=ex, in_=ex)
                        h.drain()
                        return h.tensor_tensor(out=H1B[:, 0:TT], in0=H1F[:, 0:TT],
                                               in1=ex, op=AluOpType.mult)
                    e_h1v = E("vector", f_h1v, deps=[e_h1e, h1_reader_last])

                    LGS = TOP5[:, 0:64]
                    SCT = TOP5[:, 64:128]
                    LSB = TOP5[:, 128:192]
                    ETP = TOP5[:, 192:256]
                    MC = TOP5[:, 256:264]

                    def f_lgmm(h):
                        last = None
                        for st in range(4):
                            last = h.matmul(PB[4][:, 128 * st:128 * st + 32],
                                            H1B[:, 128 * st:128 * st + 128],
                                            RW2B, start=True, stop=True)
                        return last
                    e_lgmm = E("tensor", f_lgmm, deps=[e_h1v, bank_free[4], e_ldc])
                    h1_reader_last = e_lgmm

                    def f_top1(h):
                        for st in range(4):
                            lg = LGS[:, 16 * st:16 * st + 16]
                            h.tensor_tensor(out=lg, in0=PB[4][:, 128 * st:128 * st + 16],
                                            in1=B2G[:, 0:16], op=AluOpType.add)
                            h.drain()
                            m1 = MC[:, 2 * st:2 * st + 1]
                            m2 = MC[:, 2 * st + 1:2 * st + 2]
                            h.reduce_max(out=m1, in_=lg, axis=AX)
                            h.drain()
                            eq = SCT[:, 16 * st:16 * st + 16]
                            h.tensor_scalar(out=eq, in0=lg, scalar1=m1, scalar2=None,
                                            op0=AluOpType.is_ge)
                            h.drain()
                            h.scalar_tensor_tensor(out=eq, in0=eq, scalar=CONSTS[:, 2:3],
                                                   in1=lg, op0=AluOpType.mult,
                                                   op1=AluOpType.add)
                            h.drain()
                            h.reduce_max(out=m2, in_=eq, axis=AX)
                            h.drain()
                            h.tensor_scalar(out=LSB[:, 16 * st:16 * st + 16], in0=lg,
                                            scalar1=m1, scalar2=None, op0=AluOpType.subtract)
                            h.tensor_scalar(out=GEMB[:, 16 * st:16 * st + 16], in0=lg,
                                            scalar1=m2, scalar2=None, op0=AluOpType.is_ge)
                        return None
                    e_top1 = E("vector", f_top1, deps=[e_lgmm, e_const])
                    bank_free[4] = e_top1

                    def f_texp(h):
                        return h.activation(out=ETP, in_=LSB, func=AF.Exp)
                    e_texp = E("scalar", f_texp, deps=[e_top1])

                    def f_top2(h):
                        h.tensor_tensor(out=ETP, in0=ETP, in1=GEMB[:], op=AluOpType.mult)
                        h.drain()
                        for st in range(4):
                            den = MC[:, 2 * st + 1:2 * st + 2]
                            h.reduce_sum(out=den, in_=ETP[:, 16 * st:16 * st + 16], axis=AX)
                            h.drain()
                            h.reciprocal(out=den, in_=den)
                            h.drain()
                            h.tensor_scalar(out=CTT[:, 16 * st:16 * st + 16],
                                            in0=ETP[:, 16 * st:16 * st + 16],
                                            scalar1=den, scalar2=None, op0=AluOpType.mult)
                        return None
                    e_top2_t[tt] = E("vector", f_top2, deps=[e_texp])

                scst = dict(prev=None, fmm={}, fevd={})

                def scan_iters(s_lo, s_hi, fmm_guard, b=b, g=g):
                    for s in range(s_lo, s_hi):
                        def f_dr(h, b=b, g=g, s=s):
                            src = dscrB[b, s:s + 1, g * LH:(g + 1) * LH].broadcast_to([128, LH])
                            h.dma_start(out=DREP[:, 0:LH], in_=src).then_inc(s_gp, 16)
                            return None
                        e_dr = E("gpsimd", f_dr, deps=[e_dst_last[b], scst["prev"]])

                        def f_sc(h, s=s):
                            last = None
                            for kh in range(2):
                                k0 = kh * 8
                                drv = DREP[:, 0:LH].unsqueeze(1).broadcast_to([128, 8, LH])
                                xph = xpv()[:, k0:k0 + 8, :]
                                sc3 = SCR[:].rearrange("p (k t) -> p k t", t=LH)
                                h.tensor_tensor(out=sc3, in0=xph, in1=drv, op=AluOpType.mult)
                                prv = PWR[:].rearrange("p (k s r) -> p k s r", s=16, r=32)[:, k0:k0 + 8, s]
                                prv = prv.unsqueeze(2).broadcast_to([128, 8, NKH, 32])
                                xcv = SCR[:].rearrange("p (k c r) -> p k c r", c=NKH, r=32)
                                h.tensor_tensor(out=xcv, in0=xcv, in1=prv, op=AluOpType.mult)
                                w = 16
                                for lev in range(5):
                                    iv0 = xcv[:, :, :, 0:w]
                                    iv1 = xcv[:, :, :, w:2 * w]
                                    h.tensor_tensor(out=iv0, in0=iv0, in1=iv1, op=AluOpType.add)
                                    w //= 2
                                sv = xcv[:, :, :, 0]
                                avb = SIGB[:].rearrange("p (k s) -> p k s", s=16)[:, k0:k0 + 8, s]
                                avb = avb.unsqueeze(2).broadcast_to([128, 8, NKH])
                                mh = MSEQ.rearrange("p (k c) -> p k c", c=NKH)[:, k0:k0 + 8]
                                last = h.tensor_tensor(out=mh, in0=sv, in1=avb, op=AluOpType.mult)
                            return last
                        e_sc = E("vector", f_sc, deps=[e_dr, e_xp_done, scst["prev"],
                                                       e_gfadd_done.get(b - 1) if (g == 0 and s == 0) else None,
                                                       e_pow])

                        def f_sc2(h, g=g, s=s):
                            a32 = A32[:].rearrange("p (k s) -> p k s", s=16)[:, :, s]
                            av = a32.unsqueeze(2).broadcast_to([128, KT, NKH])
                            h.tensor_copy(out=ACZ.rearrange("p (k c) -> p k c", c=NKH), in_=av)
                            h.memset(ACZ.rearrange("p (k c) -> p k c", c=NKH)[:, :, 0:1], 0.0)
                            h.drain()
                            if g == 1:
                                cv = CARRY[:].rearrange("p (k s) -> p k s", s=16)[:, :, s]
                                h.tensor_tensor(out=TOP5[:, 0:KT], in0=a32, in1=cv,
                                                op=AluOpType.mult)
                                h.drain()
                                mv0 = MSEQ.rearrange("p (k c) -> p k c", c=NKH)[:, :, 0]
                                h.tensor_tensor(out=mv0, in0=mv0, in1=TOP5[:, 0:KT],
                                                op=AluOpType.add)
                                h.drain()
                            ms2 = MS2A[:, 512 * s:512 * s + 512]
                            h.tensor_tensor_scan(out=ms2, data0=ACZ, data1=MSEQ,
                                                 initial=0.0, op0=AluOpType.mult, op1=AluOpType.add)
                            h.drain()
                            if g == 0:
                                return h.tensor_copy(
                                    out=CARRY[:].rearrange("p (k s) -> p k s", s=16)[:, :, s],
                                    in_=ms2.rearrange("p (k c) -> p k c", c=NKH)[:, :, NKH - 1])
                            return None
                        e_sc2 = E("vector", f_sc2, deps=[e_sc])
                        scst["prev"] = e_sc2

                        q = s % 8
                        fbank = 4 if q < 4 else 5
                        fcol = 128 * (q % 4)

                        def f_fmm(h, s=s, fbank=fbank, fcol=fcol):
                            last = None
                            ms2 = MS2A[:, 512 * s:512 * s + 512]
                            for kt in range(KT):
                                msl = ms2.rearrange("p (k c) -> p k c", c=NKH)[:, kt]
                                pfv = PWY[:].rearrange("p (k c s) -> p k c s", c=32, s=16)[:, kt, :, s]
                                last = h.matmul(PB[fbank][0:32, fcol:fcol + 32], msl, pfv,
                                                start=(kt == 0), stop=(kt == KT - 1))
                            return last
                        fdeps = [e_sc2, e_pow] + (fmm_guard if s < 8 else [scst["fevd"][s - 8]])
                        scst["fmm"][s] = E("tensor", f_fmm, deps=fdeps)

                        def f_fev(h, s=s, g=g, fbank=fbank, fcol=fcol):
                            return h.activation(out=FST[32 * g:32 * g + 32, 32 * s:32 * s + 32],
                                                in_=PB[fbank][0:32, fcol:fcol + 32], func=AF.Copy)
                        scst["fevd"][s] = E("scalar", f_fev, deps=[scst["fmm"][s], e_fstore.get(b - 1)])

                for tt in range(LH // TT):
                    t0 = t0g + tt * TT

                    e_cev_prev = cst_last
                    for st in range(4):
                        def f_ctr(h, st=st, tt=tt):
                            return h.transpose(PB7[0:16, 0:128], CTT[:, 16 * st:16 * st + 16],
                                               IDB[:])
                        e_ctr = E("tensor", f_ctr, deps=[e_top2_t[tt], e_cev_prev, e_lds])

                        def f_cev(h, st=st):
                            return h.activation(out=CFF[:, 128 * st:128 * st + 128],
                                                in_=PB7[0:16, 0:128], func=AF.Copy)
                        e_cev = E("scalar", f_cev, deps=[e_ctr, omm_last])
                        e_cev_prev = e_cev
                    e_cf_last = e_cev_prev

                    def f_crx(h):
                        h.matmul(PB[5][:, :], RPM[:, 0:128], CFF[:, 0:TT],
                                 start=True, stop=True)
                        return h.matmul(PB[6][:, :], RPM[:, 128:256], CFF[:, 0:TT],
                                        start=True, stop=True)
                    e_crx = E("tensor", f_crx, deps=[e_cf_last, bank_free[5], bank_free[6],
                                                     e_ldc])

                    def f_crev(h):
                        h.activation(out=CREP[:, 0:TT], in_=PB[5][:, :], func=AF.Copy)
                        return h.activation(out=CREP[:, TT:2 * TT], in_=PB[6][:, :],
                                            func=AF.Copy)
                    e_crev = E("scalar", f_crev, deps=[e_crx, omm_last])
                    bank_free[5] = e_crev
                    bank_free[6] = e_crev
                    cst_last = e_crev

                    def f_xam(h, tt=tt):
                        last = None
                        for kt in range(KT):
                            xsl = xtv()[:, kt, tt * TT:tt * TT + TT]
                            for q in range(2):
                                wsl = AMB[:].rearrange("p (k e) -> p k e", e=256)[:, kt, 128 * q:128 * q + 128]
                                last = h.matmul(PB[q][:, :], wsl, xsl,
                                                start=(kt == 0), stop=(kt == KT - 1))
                        return last
                    e_xam = E("tensor", f_xam, deps=[e_xn_done, bank_free[0], bank_free[1],
                                                     e_ldc])

                    def f_xaev(h):
                        h.activation(out=XAB[:, 0:TT], in_=PB[0][:, :], func=AF.Copy)
                        return h.activation(out=XAB[:, TT:2 * TT], in_=PB[1][:, :], func=AF.Copy)
                    e_xaev = E("scalar", f_xaev, deps=[e_xam, omm_last])
                    bank_free[0] = e_xaev
                    bank_free[1] = e_xaev

                    def f_wxa(h):
                        return h.tensor_tensor(out=XAB[:, :], in0=XAB[:, :], in1=CREP[:, :],
                                               op=AluOpType.mult)
                    e_wxa[tt] = E("vector", f_wxa, deps=[e_xaev, e_crev])

                    oq_dep = omm_last
                    for oq in range(8):
                        def f_oql(h, oq=oq):
                            h.dma_start(out=WQ2.rearrange("p (k e) -> p k e", e=256),
                                        in_=wffnb[:, :, 256 * oq:256 * oq + 256].transpose([1, 0, 2])).then_inc(s_sy, 16)
                            h.dma_start(out=WQ1[:, 0:KT * 256].rearrange("p (k e) -> p k e", e=256),
                                        in_=woutdb[:, :, 256 * oq:256 * oq + 256].transpose([1, 0, 2])).then_inc(s_sy, 16)
                            return None
                        e_oql = E("sync", f_oql, deps=[oq_dep, e_xp_done if oq == 0 else None] + pool_guard, ndma=2)
                        for ot2 in range(2):
                            ot = oq * 2 + ot2
                            pbk = 2 + ot2

                            def f_omm(h, ot=ot, ot2=ot2, pbk=pbk, tt=tt):
                                last = None
                                for kt in range(KT):
                                    wsl = WQ2.rearrange("p (k e) -> p k e", e=256)[:, kt, 128 * ot2:128 * ot2 + 128]
                                    xsl = xtv()[:, kt, tt * TT:tt * TT + TT]
                                    last = h.matmul(PB[pbk][:, :], wsl, xsl,
                                                    start=(kt == 0), stop=False)
                                for kt in range(KT):
                                    wsl = WQ1[:, 0:KT * 256].rearrange("p (k e) -> p k e", e=256)[:, kt, 128 * ot2:128 * ot2 + 128]
                                    xsl = xpv()[:, kt, tt * TT:tt * TT + TT]
                                    last = h.matmul(PB[pbk][:, :], wsl, xsl,
                                                    start=False, stop=False)
                                for q in range(2):
                                    wsl = BMB[:].rearrange("p (q o) -> p q o", o=D)[:, q, 128 * ot:128 * ot + 128]
                                    last = h.matmul(PB[pbk][:, :], wsl, XAB[:, TT * q:TT * q + TT],
                                                    start=False, stop=(q == 1))
                                return last
                            e_omm = E("tensor", f_omm, deps=[e_oql, e_wxa[tt], e_xp_done,
                                                             bank_free[pbk]])
                            omm_last = e_omm
                            oq_dep = e_omm
                            tc = ot % 2

                            def f_oev(h, ot=ot, pbk=pbk, tc=tc):
                                return h.activation(out=TCO[:, TT * tc:TT * tc + TT],
                                                    in_=PB[pbk][:, :], func=AF.Identity,
                                                    bias=BCS[:, ot:ot + 1])
                            e_oev = E("scalar", f_oev, deps=[e_omm, tco_free[tc]])
                            bank_free[pbk] = e_oev

                            def f_ost(h, b=b, ot=ot, t0=t0, tc=tc):
                                h.dma_start(out=part[b, ot, :, t0:t0 + TT],
                                            in_=TCO[:, TT * tc:TT * tc + TT]).then_inc(s_sy, 16)
                                return None
                            e_ost = E("sync", f_ost, deps=[e_oev])
                            tco_free[tc] = e_ost
                            part_evt[(b, ot, t0)] = e_ost
                    if tt == 0:
                        scan_iters(0, 8, [bank_free[4], bank_free[5]])
                        bank_free[5] = scst["fevd"][7]
                a_done = omm_last
                scan_iters(8, S, None)
                last_scan_evt = scst["prev"]

                e_fev = scst["fevd"][S - 1]
                bank_free[4] = e_fev
                bank_free[5] = e_fev
                e_fev_last = e_fev
            def f_fstore(h, b=b):
                h.dma_start(out=fscr[b], in_=FST[:, :]).then_inc(s_sy, 16)
                return None
            e_fst = E("sync", f_fstore, deps=[e_fev_last])
            e_fstore[b] = e_fst

            def f_fgath(h, b=b):
                src = fscr[b].rearrange("k (s j) -> s k j", s=16)[:, 0:63, :]
                h.dma_start(out=GFS.rearrange("s (k j) -> s k j", j=32), in_=src).then_inc(s_gp, 16)
                return None
            e_fgath = E("gpsimd", f_fgath, deps=[e_fst, last_scan_evt])

            def f_gfadd(h):
                return h.tensor_tensor(out=ZACC[0:16, 32:32 + 63 * 32],
                                       in0=ZACC[0:16, 32:32 + 63 * 32],
                                       in1=GFS[:, :], op=AluOpType.add)
            e_gfadd = E("vector", f_gfadd, deps=[e_shift_prev, e_fgath])
            e_gfadd_done[b] = e_gfadd

            def f_zbrv(h):
                last = None
                for hi in range(8):
                    j4, j3, j2 = hi >> 2, (hi >> 1) & 1, hi & 1
                    inv = ZACC[:].rearrange("p (c jh j1 j0) -> p c jh j1 j0",
                                            jh=8, j1=2, j0=2)[:, :, hi]
                    ov = ZACCB[:].rearrange("p (c j0 j1 jh) -> p c j0 j1 jh",
                                            j0=2, j1=2, jh=8)
                    ov = ov[:, :, :, :, 4 * j2 + 2 * j3 + j4].transpose([0, 1, 3, 2])
                    last = h.activation(out=ov, in_=inv, func=AF.Copy)
                return last
            e_zbrv = E("scalar", f_zbrv, deps=[e_gfadd, e_cmm_last.get(b - 1)])
            e_zbrv_done[b] = e_zbrv

            def f_wcwl(h):
                h.dma_start(out=WCWE, in_=wcwescr[:]).then_inc(s_sy, 16)
                return None
            e_wcwl = E("sync", f_wcwl, deps=[e_zbrv, omm_last] + pool_guard)
            pool_guard = [e_wcwl]
            for tt4 in range(4):
                t0 = tt4 * TT
                for ot in range(KT):
                    pbk = 2 + (ot % 2)
                    tc = ot % 2

                    def f_cmm(h, ot=ot, t0=t0, pbk=pbk):
                        return h.matmul(PB[pbk][:, :], WCWE[:, 128 * ot:128 * ot + 128],
                                        ZACCB[:, t0:t0 + TT], start=True, stop=True)
                    e_cmm = E("tensor", f_cmm, deps=[e_zbrv, e_wcwl, bank_free[pbk]])
                    e_cmm_last[b] = e_cmm

                    def f_cev(h, pbk=pbk, tc=tc):
                        return h.activation(out=TCO[:, TT * tc:TT * tc + TT],
                                            in_=PB[pbk][:, :], func=AF.Copy)
                    e_cev = E("scalar", f_cev, deps=[e_cmm, tco_free[tc]])
                    bank_free[pbk] = e_cev

                    cacc_base = sch.cum["gpsimd"]

                    def f_cacc(h, b=b, ot=ot, t0=t0, tc=tc, base=cacc_base):
                        h.dma_start(out=TCO[:, TT * tc:TT * tc + TT],
                                    in_=part[b, ot, :, t0:t0 + TT],
                                    accum_op=AluOpType.add).then_inc(s_gp, 16)
                        h.wait_ge(s_gp, base + 16)
                        h.dma_start(out=TCO[:, TT * tc:TT * tc + TT],
                                    in_=xin[b, ot, :, t0:t0 + TT],
                                    accum_op=AluOpType.add).then_inc(s_gp, 16)
                        return None
                    e_cacc = E("gpsimd", f_cacc, deps=[e_cev, part_evt[(b, ot, t0)]], ndma=2)

                    def f_cst(h, b=b, ot=ot, t0=t0, tc=tc):
                        h.dma_start(out=yout[b, ot, :, t0:t0 + TT],
                                    in_=TCO[:, TT * tc:TT * tc + TT]).then_inc(s_sy, 16)
                        return None
                    e_cst = E("sync", f_cst, deps=[e_cacc])
                    tco_free[tc] = e_cst
                    final_store = e_cst
            pool_guard = pool_guard + [e_cmm_last[b]]

        if dbg:
            ddump = {}
            dlist = [("dxpb", XPB), ("dxt", XT), ("dza", ZACC),
                     ("dzb", ZACCB), ("dcar", CARRY), ("dms2", MS2A),
                     ("dfst", FST), ("dgfs", GFS),
                     ("drow1", ROW1), ("dpwy", PWY), ("dpwr", PWR),
                     ("da32", A32), ("ddbgt", DBGT)]
            for nm, buf in dlist:
                shp = list(buf.shape)
                dt_ = F32 if nm == "ddbgt" else BF16
                ddump[nm] = nc.dram_tensor(nm, shp, dt_, kind="ExternalOutput").ap()

            def f_dump(h):
                for nm, buf in dlist:
                    h.dma_start(out=ddump[nm][:], in_=(buf[:] if hasattr(buf, "shape") and not hasattr(buf, "offset") else buf)).then_inc(s_sy, 16)
                return None
            E("sync", f_dump, deps=[final_store, e_cmm_last.get(b_run - 1)], ndma=len(dlist))

        def f_fin(h):
            return None
        E("gpsimd", f_fin, deps=[final_store], ndma=0)

        sch.emit(block, sems)

    return nc


def kernel(**inputs):
    L0 = {k[3:]: np.asarray(v) for k, v in inputs.items() if k.startswith("L0_")}
    x = np.ascontiguousarray(L0["attn_x"])
    perm = bitrev_perm(L)
    import ml_dtypes

    def lhsT(w):
        return np.ascontiguousarray(w.T.reshape(KT, 128, -1))

    brv5 = bitrev_perm(32)[:32]
    mask_h = np.zeros((4, 128, 32), dtype=np.float32)
    for t in range(4):
        for ph in range(8):
            p = 8 * t + ph
            mask_h[t, 16 * ph:16 * ph + 16, :] = ((brv5 + p) < 32)[None, :]
    repmoe_h = np.zeros((16, 256), dtype=np.float32)
    for e in range(16):
        repmoe_h[e, 16 * e:16 * e + 16] = 1.0
    repd_h = np.zeros((16, 128), dtype=np.float32)
    for ph in range(8):
        repd_h[:, 16 * ph:16 * ph + 16] = np.eye(16)

    shared = dict(
        win=lhsT(L0["attn_proj_in_w"]), wffn=lhsT(L0["ffn_W"]),
        wout=lhsT(L0["attn_proj_out_w"]),
        amoe=np.ascontiguousarray(L0["ffn_A"].reshape(256, D).T.reshape(KT, 128, 256)),
        bmoe=np.ascontiguousarray(L0["ffn_B"].transpose(0, 2, 1).reshape(256, D).reshape(2, 128, D)),
        rw1=lhsT(L0["ffn_rw1"]), rw2=np.ascontiguousarray(L0["ffn_rw2"].T),
        bw=lhsT(L0["attn_B_w"]), cw=np.ascontiguousarray(L0["attn_C_w"].reshape(KT, 128, 16)),
        aA=np.ascontiguousarray(L0["attn_A"].reshape(KT, 128, 16)),
        dpar=np.ascontiguousarray(L0["attn_D_param"].reshape(KT, 128).T),
        binp=np.ascontiguousarray(L0["attn_proj_in_b"].reshape(KT, 128).T),
        bcmb=np.ascontiguousarray(np.stack([L0["ffn_bias"].reshape(KT, 128).T,
                                            L0["attn_proj_out_b"].reshape(KT, 128).T], 0)),
        rb1v=np.ascontiguousarray(L0["ffn_rb1"].reshape(64, 1)),
        rb2rep=np.ascontiguousarray(np.broadcast_to(L0["ffn_rb2"][None, :], (128, 32))),
        gaterep=np.ascontiguousarray(np.broadcast_to(L0["ffn_gates"][None, :], (128, 16))),
        identb=np.eye(128, dtype=ml_dtypes.bfloat16),
        maskt=mask_h.astype(ml_dtypes.bfloat16),
        repmoe=repmoe_h.astype(ml_dtypes.bfloat16),
        repd=repd_h.astype(ml_dtypes.bfloat16),
    )

    nc = build_nc(dbg=False, b_run=B)
    in_maps = []
    for core in range(8):
        xs = x[2 * core:2 * core + 2][:, perm, :]
        xs = np.ascontiguousarray(xs.transpose(0, 2, 1).reshape(B, KT, 128, L))
        m = dict(shared)
        m["xin"] = xs
        in_maps.append(m)

    res = run_bass_kernel_spmd(nc, in_maps, core_ids=list(range(8)))
    out = np.empty((16, L, D), dtype=np.float32)
    inv = np.argsort(perm)
    for core in range(8):
        y = np.asarray(res.results[core]["yout"], np.float32)
        y = y.reshape(B, D, L).transpose(0, 2, 1)
        out[2 * core:2 * core + 2] = y[:, inv, :]
    return out
